# version 11
# speedup vs baseline: 1.9202x; 1.0096x over previous
import numpy as np
from contextlib import ExitStack

import concourse.bass as bass
import concourse.mybir as mybir
from concourse.alu_op_type import AluOpType as OP
from concourse.bass_utils import run_bass_kernel_spmd

F32 = mybir.dt.float32
BF16 = mybir.dt.bfloat16
FP16 = mybir.dt.float16
I16 = mybir.dt.int16
I32 = mybir.dt.int32
AF = mybir.ActivationFunctionType
AX = mybir.AxisListType

D = 1024
DT = D // 128
FF0 = 4096
FT0 = FF0 // 128
DI = 2048
IT = DI // 128
DS = 16
DTRANK = 64
FF1 = 3072
FT1 = FF1 // 128
QMAX = 32767.0
NCORES = 8
LN_EPS = 1e-5

ENG_NAMES = {"t": "tensor", "v": "vector", "s": "scalar", "g": "gpsimd", "sy": "sync"}


class Sem:
    def __init__(self, st, nc, name):
        self.h = st.enter_context(nc.semaphore(name))
        self.n = 0
        self.thr = 0

    def inc(self, instr, k=1):
        instr.then_inc(self.h, k)
        self.n += k
        return self.n

    def throttle(self, raw, limit=192):
        if self.n - self.thr >= limit:
            raw.wait_ge(self.h, self.n)
            self.thr = self.n


class SemPool:

    def __init__(self, st, nc, name, n):
        self.sems = [Sem(st, nc, "%s%d" % (name, i)) for i in range(n)]
        self.i = 0
        self.open = False
        self.maps = []

    def cur(self, raw):
        sm = self.sems[self.i]
        if not self.open and sm.n > sm.thr:
            raw.wait_ge(sm.h, sm.n)
            sm.thr = sm.n
        self.open = True
        return sm

    def mark(self):
        sm = self.sems[self.i]
        m = (sm.h, sm.n)
        if self.open:
            for ent in self.maps:
                for j, (n2, l2, h2, sh, sv) in enumerate(ent):
                    if sh is sm.h and sv < sm.n:
                        ent[j] = (n2, l2, h2, sh, sm.n)
            self.i = (self.i + 1) % len(self.sems)
            self.open = False
        return m

    def all_marks(self):
        return [(sm.h, sm.n) for sm in self.sems]


class EngQ:
    def __init__(self, raw, name):
        self.raw = raw
        self.name = name
        self.dw = []
        self.dr = []

    @staticmethod
    def _rng(ap):
        t = getattr(ap, "tensor", None)
        name = getattr(t, "name", None) if t is not None else None
        if name is None:
            return None
        try:
            lo = ap.offset * 4
        except Exception:
            return (name, 0, 1 << 60)
        try:
            sz = 1
            for st_, n_ in ap.ap:
                pass
            spans = [(st_ * (n_ - 1)) for st_, n_ in list(ap.ap)[1:]]
            ext = sum(spans) + 1
        except Exception:
            ext = 1 << 40
        try:
            import concourse.mybir as _mb
            esz = {F32: 4, I32: 4, BF16: 2, FP16: 2, I16: 2}.get(ap.dtype, 4)
        except Exception:
            esz = 4
        lo = ap.offset * esz
        return (name, lo, lo + ext * esz)

    @staticmethod
    def _ov(a, b):
        return a[0] == b[0] and a[1] < b[2] and b[1] < a[2]

    def _pre(self, outs, ins):
        og = [r for r in (self._rng(a) for a in outs) if r is not None]
        ig = [r for r in (self._rng(a) for a in ins) if r is not None]
        conflict = False
        for o in og:
            for d in self.dw:
                if self._ov(o, d):
                    conflict = True
                    break
            if conflict:
                break
            for d in self.dr:
                if self._ov(o, d):
                    conflict = True
                    break
            if conflict:
                break
        if not conflict:
            for i_ in ig:
                for d in self.dw:
                    if self._ov(i_, d):
                        conflict = True
                        break
                if conflict:
                    break
        if conflict:
            self.raw.drain()
            self.dw = []
            self.dr = []
        self.dw += og
        self.dr += ig

    def drain(self):
        self.raw.drain()
        self.dw = []
        self.dr = []

    def clear(self):
        self.dw = []
        self.dr = []

    def tt(self, out, a, bb, op):
        self._pre([out], [a, bb])
        return self.raw.tensor_tensor(out, a, bb, op)

    def ts(self, out, a, s1, s2, op0, op1):
        self._pre([out], [a, s1, s2])
        return self.raw.tensor_scalar(out, a, s1, s2, op0, op1)

    def tss(self, out, a, s, op):
        self._pre([out], [a])
        return self.raw.tensor_single_scalar(out, a, s, op)

    def stt(self, out, a, s, bb, op0, op1):
        self._pre([out], [a, bb, s])
        return self.raw.scalar_tensor_tensor(out, a, s, bb, op0, op1)

    def scan(self, out, d0, d1, init, op0, op1):
        self._pre([out], [d0, d1, init])
        return self.raw.tensor_tensor_scan(out, d0, d1, init, op0, op1)

    def red(self, out, a, op, absval=False):
        self._pre([out], [a])
        return self.raw.tensor_reduce(out, a, AX.X, op, apply_absolute_value=absval)

    def copy(self, out, a):
        self._pre([out], [a])
        return self.raw.tensor_copy(out, a)

    def memset(self, out, v):
        self._pre([out], [])
        return self.raw.memset(out, v)

    def recip(self, out, a):
        self._pre([out], [a])
        return self.raw.reciprocal(out, a)

    def act(self, out, a, func, bias=0.0, scale=1.0):
        self._pre([out], [a, bias, scale])
        return self.raw.activation(out, a, func, bias=bias, scale=scale)

    def scopy(self, out, a):
        self._pre([out], [a])
        return self.raw.copy(out, a)

    def mm(self, out, lhsT, rhs, start, stop):
        return self.raw.matmul(out, lhsT, rhs, start=start, stop=stop)

    def transpose(self, out, a, ident):
        return self.raw.transpose(out, a, ident)

    def iota(self, out, pattern, base=0, cm=0):
        self._pre([out], [])
        return self.raw.iota(out, pattern, base=base, channel_multiplier=cm,
                             allow_small_or_imprecise_dtypes=True)

    def dma(self, out, in_, **kw):
        self._pre([], [in_])
        return self.raw.dma_start(out=out, in_=in_, **kw)

    def collective(self, kind, op, groups, ins, outs):
        return self.raw.collective_compute(kind, op, replica_groups=groups,
                                           ins=ins, outs=outs)


class Prog:
    def __init__(self, nc, st, blk, T):
        self.nc = nc
        self.st = st
        self.blk = blk
        self.T = T
        self.q = {k: EngQ(None, k) for k in ENG_NAMES}
        self.DL = SemPool(st, nc, "DL", 12)
        self.DG = SemPool(st, nc, "DG", 8)
        self.CC = Sem(st, nc, "CCs")
        self.EV = {e: Sem(st, nc, "ev_" + e) for e in ("t", "v", "s", "g")}
        self.bank = [None] * 8
        self.lnscr = None

    def bwait(self, e, b):
        m = self.bank[b]
        if m is not None:
            e.raw.wait_ge(*m)

    def bfree(self, b, key="s"):
        self.bank[b] = (self.EV[key].h, self.EV[key].n)

    def on(self, key):
        dec = getattr(self.blk, ENG_NAMES[key])
        q = self.q[key]

        def wrap(fn):
            def body(raw):
                q.raw = raw
                fn(q)
            dec(body)
        return wrap

    def ev(self, key, instr=None, k=1):
        s = self.EV[key]
        if instr is None:
            self.q[key].raw.maybe_drain_then_inc((s.h, k))
            s.n += k
        else:
            instr.then_inc(s.h, k)
            s.n += k
        s.throttle(self.q[key].raw)
        return s.n

    def wt(self, e, key):
        e.raw.wait_ge(self.EV[key].h, self.EV[key].n)

    def barrier(self):
        for k in ("t", "v", "s", "g"):
            @self.on(k)
            def _(e):
                self.ev(k)
        marks = {k: self.EV[k].n for k in self.EV}
        dlm = self.DL.all_marks()
        dgm = self.DG.all_marks()
        for k in ("t", "v", "s", "g", "sy"):
            @self.on(k)
            def _(e):
                for j in ("t", "v", "s", "g"):
                    e.raw.wait_ge(self.EV[j].h, marks[j])
                for h, v in dlm + dgm:
                    if v:
                        e.raw.wait_ge(h, v)
                e.clear()
        for s_ in self.EV.values():
            s_.thr = s_.n




def build_core_program(T=2048, TCF=512, TCW=256, group=None, stages=6, dbgq=False, rmode="int"):
    if group is None:
        group = [list(range(NCORES))]
    nc = bass.Bass("TRN2", target_bir_lowering=False, debug=False)
    st = ExitStack()
    NCF = T // TCF
    NCW = T // TCW

    xin = nc.dram_tensor("x_in", [T, D], F32, kind="ExternalInput").ap()
    wshapes = {
        "L0_attn_time_decay": (1, D), "L0_attn_time_first": (1, D),
        "L0_attn_time_shift": (1, D),
        "L0_attn_Wk": (D, D), "L0_attn_Wv": (D, D), "L0_attn_Wr": (D, D),
        "L0_attn_Wck": (D, FF0), "L0_attn_Wcv": (FF0, D), "L0_attn_Wcr": (D, D),
        "L0_attn_ln1_g": (1, D), "L0_attn_ln1_b": (1, D),
        "L0_attn_ln2_g": (1, D), "L0_attn_ln2_b": (1, D),
        "L0_ffn_W1": (FF0, D), "L0_ffn_b1": (1, FF0), "L0_ffn_W2": (D, FF0),
        "L0_ffn_b2": (1, D),
        "L1_attn_in_proj_w": (2 * DI, D), "L1_attn_conv_w": (DI, 4),
        "L1_attn_conv_b": (1, DI),
        "L1_attn_x_proj_w": (DTRANK + 2 * DS, DI),
        "L1_attn_dt_proj_w": (DI, DTRANK),
        "L1_attn_dt_proj_b": (1, DI), "L1_attn_D_skip": (1, DI),
        "L1_attn_A_log": (DI, DS),
        "L1_attn_out_proj_w": (D, DI),
        "L1_ffn_w_gate": (FF1, D), "L1_ffn_w_up": (FF1, D),
        "L1_ffn_w_down": (D, FF1),
    }
    W = {k: nc.dram_tensor(k, list(sh), F32, kind="ExternalInput").ap()
         for k, sh in wshapes.items()}
    xout = nc.dram_tensor("x_out", [T, D], F32, kind="ExternalOutput").ap()

    Hn = {}
    for k, sh in [("w1n", (FF0, D)), ("w2n", (D, FF0)), ("wkn", (D, D)),
                  ("wvn", (D, D)), ("wrn", (D, D)), ("wckn", (D, FF0)),
                  ("wcvn", (FF0, D)), ("wcrn", (D, D)), ("inpn", (2 * DI, D)),
                  ("outpn", (D, DI)), ("wgn", (FF1, D)), ("wun", (FF1, D)),
                  ("wdn", (D, FF1)), ("xpn", (DTRANK + 2 * DS, DI)),
                  ("dtpn", (DI, 128))]:
        Hn[k] = nc.dram_tensor(k, list(sh), FP16 if k == "dtpn" else BF16).ap()
    Hn["upraw"] = nc.dram_tensor("upraw", [FF1, T], BF16).ap()
    Hn["gateraw"] = nc.dram_tensor("gateraw", [FF1, T], BF16).ap()
    Hn["g2raw"] = nc.dram_tensor("g2raw", [FF1, T], BF16).ap()
    Hn["qup"] = nc.dram_tensor("qup", [FF1, T], I16).ap()
    Hn["bounce"] = nc.dram_tensor("bounceb", [64, 64], F32).ap()
    Hn["rowbB"] = nc.dram_tensor("rowbB", [DS, T], FP16).ap()
    Hn["rowbC"] = nc.dram_tensor("rowbC", [DS, T], FP16).ap()
    Hn["ccin"] = nc.dram_tensor("ccin", [1, 4], F32).ap()
    DBGQ = [nc.dram_tensor("dbgq", [D, T], BF16, kind="ExternalOutput").ap() if dbgq else None]
    Hn["ccout"] = nc.dram_tensor("ccout", [1, 4], F32).ap()

    def sb(name, shape, dt):
        return st.enter_context(nc.sbuf_tensor(name, shape, dt))

    X = sb("X", [128, DT, T], F32)
    PA = sb("PA", [128, 14336], F32)
    PB = sb("PB", [128, 8192], F32)
    PC = sb("PC", [128, 6144], F32)
    LNB = sb("LNB", [128, DT, TCF + 1], BF16)
    MREP = sb("MREP", [128, TCF], F32)
    VREP = sb("VREP", [128, TCF], F32)
    VEC = sb("VEC", [128, 56], F32)
    VEC2 = sb("VEC2", [128, 96], F32)
    CONVW = sb("CONVW", [128, 4 * IT], F32)
    GS = sb("GS", [128, 16], F32)
    GS1 = sb("GS1", [1, 160], F32)
    ONESD = sb("ONESD", [128, 128], BF16)
    IDF = sb("IDF", [128, 128], F32)
    IDH = sb("IDH", [128, 128], FP16)
    IOTA_P = sb("IOTA_P", [128, 1], F32)
    NSCALE = None
    CONSTS = sb("CONSTS", [128, 4], F32)
    ASB = sb("ASB", [128, IT * DS], F32)
    XPT = sb("XPT", [128, IT, 320], FP16)
    DTPT = sb("DTPT", [128, DI], FP16)
    CARRY = sb("CARRY", [128, IT, DS], F32)
    WCAR = sb("WCAR", [128, 16], F32)
    XHALO = sb("XHALO", [128, IT, 3], F32)
    SM = sb("SM", [128, 128], F32)
    WKS = sb("WKS", [128, 40], F32)

    PS = [st.enter_context(nc.psum_tensor("pp%d" % i, [128, 512], F32))
          for i in range(8)]

    def bf(ap):
        return ap.bitcast(BF16)

    def hf(ap):
        return ap.bitcast(FP16)

    def r3(ap, inner):
        return ap.rearrange("p (a b) -> p a b", b=inner)

    with nc.Block() as blk:
        P = Prog(nc, st, blk, T)
        DL, DG, CC, EV = P.DL, P.DG, P.CC, P.EV

        ld_map = []
        gd_map = []
        DL.maps.append(ld_map)
        DG.maps.append(gd_map)

        def _span(ap):
            t = getattr(ap, "tensor", None)
            name = getattr(t, "name", None)
            try:
                lo = ap.offset
                hi = lo + max(ap.free_nbytes(), 1)
            except Exception:
                lo, hi = 0, 1 << 60
            return name, lo, hi

        def _waw(e, mp, out, sem_of_cur):
            name, lo, hi = _span(out)
            keep = []
            for (n2, l2, h2, sh, sv) in mp:
                if n2 == name and l2 < hi and h2 > lo:
                    e.raw.wait_ge(sh, sv)
                else:
                    keep.append((n2, l2, h2, sh, sv))
            mp[:] = keep
            return name, lo, hi

        def ld(out, in_, **kw):
            @P.on("sy")
            def _(e):
                name, lo, hi = _span(out)
                for (n2, l2, h2, sh, sv) in ld_map:
                    if n2 == name and l2 < hi and h2 > lo and DL.open and \
                            sh is DL.sems[DL.i].h and sv > DL.sems[DL.i].thr:
                        DL.mark()
                        break
                _waw(e, ld_map, out, None)
                sm = DL.cur(e.raw)
                sm.inc(e.dma(out, in_, **kw), 16)
                ld_map.append((name, lo, hi, sm.h, sm.n))

        def ldm():
            return DL.mark()

        def ld_guard(*evs):
            marks = [(EV[k].h, EV[k].n) for k in evs]

            @P.on("sy")
            def _(e):
                for h_, v_ in marks:
                    e.raw.wait_ge(h_, v_)

        def gdma(e, out, in_, **kw):
            name, lo, hi = _span(out)
            for (n2, l2, h2, sh, sv) in gd_map:
                if n2 == name and l2 < hi and h2 > lo and DG.open and \
                        sh is DG.sems[DG.i].h and sv > DG.sems[DG.i].thr:
                    DG.mark()
                    break
            _waw(e, gd_map, out, None)
            sm = DG.cur(e.raw)
            sm.inc(e.dma(out, in_, **kw), 16)
            gd_map.append((name, lo, hi, sm.h, sm.n))

        def gdm():
            return DG.mark()

        def bcast_scalar(src_1x1, gs_col, after_ev="v"):
            @P.on("g")
            def _(e):
                e.raw.wait_ge(EV[after_ev].h, EV[after_ev].n)
                gdma(e, Hn["bounce"][0:1, gs_col:gs_col + 1], src_1x1)
                m = gdm()
                e.raw.wait_ge(*m)
                gdma(e, GS[:, gs_col:gs_col + 1],
                     Hn["bounce"][0:1, gs_col:gs_col + 1].broadcast_to([128, 1]))
            return gdm()

        def part_gather_reduce(src_128x1, dst_col, op):
            @P.on("g")
            def _(e):
                e.raw.wait_ge(EV["v"].h, EV["v"].n)
                gdma(e, GS1[0:1, 0:128], src_128x1,
                     allow_slow_non_contiguous=True)
            dgmk = gdm()

            @P.on("v")
            def _(e):
                e.raw.wait_ge(*dgmk)
                e.red(GS1[0:1, dst_col:dst_col + 1], GS1[0:1, 0:128], op)
                P.ev("v")

        @P.on("g")
        def _(e):
            e.iota(IOTA_P[:], [[0, 1]], base=0, cm=1)
            e.iota(PA[:, 0:128], [[1, 128]], base=0, cm=0)
            P.ev("g")

        @P.on("v")
        def _(e):
            e.raw.wait_ge(EV["g"].h, EV["g"].n)
            e.ts(IDF[:], PA[:, 0:128], IOTA_P[:, 0:1], 0.0, OP.is_equal, OP.add)
            e.copy(IDH[:], IDF[:])
            e.memset(ONESD[:], 1.0 / D)
            e.memset(CONSTS[:, 0:1], LN_EPS)
            e.memset(CONSTS[:, 1:2], 65536.0)
            e.memset(CONSTS[:, 2:3], 0.5)
            e.memset(CONSTS[:, 3:4], 1.0)
            P.ev("v")

        vecmap = [("L0_attn_time_decay", VEC, 0, DT), ("L0_attn_time_first", VEC, 8, DT),
                  ("L0_attn_ln1_g", VEC, 16, DT), ("L0_attn_ln1_b", VEC, 24, DT),
                  ("L0_attn_ln2_g", VEC, 32, DT), ("L0_attn_ln2_b", VEC, 40, DT),
                  ("L0_attn_time_shift", VEC, 48, DT),
                  ("L0_ffn_b1", VEC2, 0, FT0), ("L0_ffn_b2", VEC2, 32, DT),
                  ("L1_attn_conv_b", VEC2, 40, IT), ("L1_attn_dt_proj_b", VEC2, 56, IT),
                  ("L1_attn_D_skip", VEC2, 72, IT)]
        for key, dstt, c0, nt in vecmap:
            ld(dstt[:, c0:c0 + nt], W[key].rearrange("o (j p) -> (o p) j", p=128),
               allow_slow_non_contiguous=True)
        for j in range(4):
            ld(CONVW[:, j * IT:(j + 1) * IT],
               W["L1_attn_conv_w"][:, j:j + 1].rearrange("(j p) o -> p (j o)", p=128),
               allow_slow_non_contiguous=True)
        ldv = ldm()

        @P.on("v")
        def _(e):
            e.raw.wait_ge(*ldv)
            e.ts(VEC2[:, 88:96], VEC2[:, 32:40], 0.5, 0.0, OP.mult, OP.add)
            P.ev("v")

        for ik in range(IT):
            ld_guard("v", "s")
            ld(SM[:, 16:32], W["L1_attn_A_log"][ik * 128:(ik + 1) * 128, :])
            mark = ldm()

            @P.on("s")
            def _(e, mark=mark):
                e.raw.wait_ge(*mark)
                P.wt(e, "v")
                P.ev("s", e.act(SM[:, 32:48], SM[:, 16:32], AF.Exp))
            smk = EV["s"].n

            @P.on("v")
            def _(e, smk=smk, ik=ik):
                e.raw.wait_ge(EV["s"].h, smk)
                e.ts(ASB[:, ik * DS:(ik + 1) * DS], SM[:, 32:48], -1.0, 0.0,
                     OP.mult, OP.add)
                P.ev("v")

        tr_tmk = [None, None]
        tr_smk = [None]
        for tt in range(T // 128):
            buf = PA[:, 1024:2048] if tt % 2 == 0 else PA[:, 2048:3072]
            pm = tr_tmk[tt % 2]
            if pm is not None:
                @P.on("sy")
                def _(e, pm=pm):
                    e.raw.wait_ge(*pm)
            ld(buf, xin[tt * 128:(tt + 1) * 128, :])
            mark = ldm()
            sm_prev = tr_smk[0]

            @P.on("t")
            def _(e, buf=buf, mark=mark, sm_prev=sm_prev, tt=tt):
                e.raw.wait_ge(*mark)
                if tt == 0:
                    e.raw.wait_ge(EV["v"].h, EV["v"].n)
                if sm_prev is not None:
                    e.raw.wait_ge(*sm_prev)
                for dk in range(DT):
                    ins = e.transpose(PS[dk][:, 0:128],
                                      buf[:, dk * 128:(dk + 1) * 128], IDF[:])
                P.ev("t", ins)
            tr_tmk[tt % 2] = (EV["t"].h, EV["t"].n)
            tmk = (EV["t"].h, EV["t"].n)

            @P.on("s")
            def _(e, tmk=tmk, tt=tt):
                e.raw.wait_ge(*tmk)
                for dk in range(DT):
                    ins = e.scopy(X[:, dk, tt * 128:(tt + 1) * 128],
                                  PS[dk][:, 0:128])
                P.ev("s", ins)
            tr_smk[0] = (EV["s"].h, EV["s"].n)

        P.barrier()

        PSETS = [
            (PA[:, 0:2048], PA[:, 2048:4096], PA[:, 4096:6144],
             bf(PA[:, 6144:7168])[:, 0:2048]),
            (PA[:, 7168:9216], PA[:, 9216:11264], PA[:, 11264:13312],
             bf(PA[:, 13312:14336])[:, 0:2048]),
        ]
        SCR = PSETS[0][0]
        SCR2 = PSETS[0][1]
        SCR3 = PSETS[0][2]
        WPB = PSETS[0][3]
        WPH = hf(PA[:, 6144:6208])[:, 0:128]
        pp_idx = [0]
        pp_cmk = [None, None]
        pp_gmk = [None, None]

        def pp_next():
            i = pp_idx[0] % 2
            pp_idx[0] += 1
            pm = pp_cmk[i]
            if pm is not None:
                @P.on("sy")
                def _(e, pm=pm):
                    e.raw.wait_ge(*pm)
            return i

        def cast_slice(key, hkey, r0, c0, cols, rr=128):
            i = pp_next()
            scr, _, _, wpb = PSETS[i]
            ld(scr[0:rr, 0:cols], W[key][r0:r0 + rr, c0:c0 + cols])
            mark = ldm()
            gm = pp_gmk[i]

            @P.on("s")
            def _(e, mark=mark, gm=gm, scr=scr, wpb=wpb, rr=rr, cols=cols):
                e.raw.wait_ge(*mark)
                if gm is not None:
                    e.raw.wait_ge(*gm)
                P.ev("s", e.act(wpb[0:rr, 0:cols], scr[0:rr, 0:cols], AF.Identity))
            smk = (EV["s"].h, EV["s"].n)
            pp_cmk[i] = smk

            @P.on("g")
            def _(e, smk=smk, r0=r0, c0=c0, hkey=hkey, wpb=wpb, rr=rr, cols=cols):
                e.raw.wait_ge(*smk)
                gdma(e, Hn[hkey][r0:r0 + rr, c0:c0 + cols], wpb[0:rr, 0:cols])
            pp_gmk[i] = gdm()

        for key, hkey, rows, cols in [
                ("L0_attn_Wk", "wkn", D, D), ("L0_attn_Wv", "wvn", D, D),
                ("L0_attn_Wr", "wrn", D, D), ("L0_attn_Wcr", "wcrn", D, D),
                ("L1_attn_in_proj_w", "inpn", 2 * DI, D),
                ("L1_attn_out_proj_w", "outpn", D, DI),
                ("L1_attn_x_proj_w", "xpn", DTRANK + 2 * DS, DI),
                ("L0_attn_Wck", "wckn", D, FF0), ("L0_attn_Wcv", "wcvn", FF0, D)]:
            for r0 in range(0, rows, 128):
                rr = min(128, rows - r0)
                for c0 in range(0, cols, 2048):
                    cw = min(2048, cols - c0)
                    cast_slice(key, hkey, r0, c0, cw, rr=rr)
        P.barrier()

        WPHS = [hf(PA[:, 6144:6208])[:, 0:128], hf(PA[:, 13312:13376])[:, 0:128]]
        for r0 in range(0, DI, 128):
            i = pp_next()
            scr = PSETS[i][0]
            wph = WPHS[i]
            ld(scr[:, 0:DTRANK], W["L1_attn_dt_proj_w"][r0:r0 + 128, :])
            mark = ldm()
            gm = pp_gmk[i]

            @P.on("s")
            def _(e, mark=mark, gm=gm, scr=scr, wph=wph):
                e.raw.wait_ge(*mark)
                if gm is not None:
                    e.raw.wait_ge(*gm)
                e.act(wph[:, 0:DTRANK], scr[:, 0:DTRANK], AF.Identity)
                e._pre([wph], [])
                e.raw.memzero(wph[:, DTRANK:128])
                P.ev("s")
            smk = EV["s"].n
            pp_cmk[i] = (EV["s"].h, smk)

            @P.on("g")
            def _(e, smk=smk, r0=r0, wph=wph):
                e.raw.wait_ge(EV["s"].h, smk)
                gdma(e, Hn["dtpn"][r0:r0 + 128, 0:128], wph[:, 0:128])
            pp_gmk[i] = gdm()
        def ternary(key, hkey, rows, cols, gs_col):
            colw = min(cols, 2048)

            @P.on("v")
            def _(e):
                e.memset(SM[:, 8:9], 0.0)
                P.ev("v")
            for r0 in range(0, rows, 128):
                for c0 in range(0, cols, colw):
                    i = pp_next()
                    scr = PSETS[i][0]
                    ld(scr[:, 0:colw], W[key][r0:r0 + 128, c0:c0 + colw])
                    mark = ldm()

                    @P.on("v")
                    def _(e, mark=mark, scr=scr):
                        e.raw.wait_ge(*mark)
                        e.red(SM[:, 9:10], scr[:, 0:colw], OP.add, absval=True)
                        e.tt(SM[:, 8:9], SM[:, 8:9], SM[:, 9:10], OP.add)
                        P.ev("v")
                    pp_cmk[i] = (EV["v"].h, EV["v"].n)
            part_gather_reduce(SM[:, 8:9], 128, OP.add)

            @P.on("v")
            def _(e):
                e.raw.wait_ge(EV["v"].h, EV["v"].n)
                e.ts(GS1[0:1, 129:130], GS1[0:1, 128:129], 1.0 / (rows * cols),
                     1e-8, OP.mult, OP.max)
                e.recip(GS1[0:1, 130:131], GS1[0:1, 129:130])
                P.ev("v")
            dgm2 = bcast_scalar(GS1[0:1, 130:131], gs_col)
            for r0 in range(0, rows, 128):
                for c0 in range(0, cols, colw):
                    i = pp_next()
                    scr, scr2, scr3, wpb = PSETS[i]
                    ld(scr[:, 0:colw], W[key][r0:r0 + 128, c0:c0 + colw])
                    mark = ldm()
                    gm = pp_gmk[i]
                    cm = pp_cmk[i]

                    @P.on("s")
                    def _(e, mark=mark, cm=cm, scr=scr, scr2=scr2):
                        e.raw.wait_ge(*mark)
                        if cm is not None:
                            e.raw.wait_ge(*cm)
                        e.raw.wait_ge(*dgm2)
                        P.ev("s", e.act(scr2[:, 0:colw], scr[:, 0:colw], AF.Identity,
                                        scale=GS[:, gs_col:gs_col + 1]))
                    smk = EV["s"].n

                    @P.on("v")
                    def _(e, smk=smk, gm=gm, scr2=scr2, scr3=scr3, wpb=wpb):
                        e.raw.wait_ge(EV["s"].h, smk)
                        if gm is not None:
                            e.raw.wait_ge(*gm)
                        if rmode == "int":
                            e.copy(scr3[:, 0:colw].bitcast(I16)[:, 0:colw],
                                   scr2[:, 0:colw])
                            e.drain()
                            e.ts(wpb[:, 0:colw],
                                 scr3[:, 0:colw].bitcast(I16)[:, 0:colw],
                                 1.0, -1.0, OP.min, OP.max)
                        else:
                            e.ts(scr2[:, 0:colw], scr2[:, 0:colw], 0.5, 0.0,
                                 OP.add, OP.add)
                            e.drain()
                            e.tss(scr3[:, 0:colw], scr2[:, 0:colw], 1.0, OP.mod)
                            e.drain()
                            e.tt(scr2[:, 0:colw], scr2[:, 0:colw], scr3[:, 0:colw],
                                 OP.subtract)
                            e.drain()
                            e.ts(wpb[:, 0:colw], scr2[:, 0:colw], 1.0, -1.0,
                                 OP.min, OP.max)
                        P.ev("v")
                    vmk = EV["v"].n
                    pp_cmk[i] = (EV["v"].h, vmk)

                    @P.on("g")
                    def _(e, vmk=vmk, r0=r0, c0=c0, wpb=wpb):
                        e.raw.wait_ge(EV["v"].h, vmk)
                        gdma(e, Hn[hkey][r0:r0 + 128, c0:c0 + colw],
                                     wpb[:, 0:colw])
                    pp_gmk[i] = gdm()

        ternary("L0_ffn_W1", "w1n", FF0, D, 0)
        ternary("L0_ffn_W2", "w2n", D, FF0, 1)
        P.barrier()

        def lpbq(key, hkey, rows, cols):
            colw = min(cols, 2048)
            for r0 in range(0, rows, 128):
                for c0 in range(0, cols, colw):
                    cw = min(colw, cols - c0)
                    nb = cw // 32
                    i = pp_next()
                    scr, scr2, scr3, wpb = PSETS[i]
                    ld(scr[:, 0:cw], W[key][r0:r0 + 128, c0:c0 + cw])
                    mark = ldm()
                    gm = pp_gmk[i]
                    cm = pp_cmk[i]

                    @P.on("s")
                    def _(e, mark=mark, cm=cm, scr=scr, scr2=scr2):
                        e.raw.wait_ge(*mark)
                        if cm is not None:
                            e.raw.wait_ge(*cm)
                        P.ev("s", e.act(scr2[:, 0:cw], scr[:, 0:cw], AF.Abs))
                    smk = EV["s"].n

                    @P.on("v")
                    def _(e, smk=smk, gm=gm, scr=scr, scr2=scr2, scr3=scr3,
                          wpb=wpb):
                        e.raw.wait_ge(EV["s"].h, smk)
                        if gm is not None:
                            e.raw.wait_ge(*gm)
                        e.red(SM[:, 0:nb], r3(scr2[:, 0:cw], 32), OP.max)
                        e.ts(SM[:, 64:64 + nb], SM[:, 0:nb], 1.0 / 7.0, 1e-12,
                             OP.mult, OP.max)
                        e.recip(SM[:, 0:nb], SM[:, 64:64 + nb])
                        rsb = SM[:, 0:nb].rearrange("p (b o) -> p b o", o=1) \
                            .broadcast_to([128, nb, 32])
                        e.tt(r3(scr2[:, 0:cw], 32), r3(scr[:, 0:cw], 32), rsb,
                             OP.mult)
                        e.drain()
                        if rmode == "int":
                            e.copy(scr3[:, 0:cw].bitcast(I16)[:, 0:cw], scr2[:, 0:cw])
                            e.drain()
                            e.ts(scr2[:, 0:cw], scr3[:, 0:cw].bitcast(I16)[:, 0:cw],
                                 7.0, -8.0, OP.min, OP.max)
                        else:
                            e.ts(scr2[:, 0:cw], scr2[:, 0:cw], 0.5, 0.0, OP.add, OP.add)
                            e.drain()
                            e.tss(scr3[:, 0:cw], scr2[:, 0:cw], 1.0, OP.mod)
                            e.drain()
                            e.tt(scr2[:, 0:cw], scr2[:, 0:cw], scr3[:, 0:cw],
                                 OP.subtract)
                            e.drain()
                            e.ts(scr2[:, 0:cw], scr2[:, 0:cw], 7.0, -8.0, OP.min, OP.max)
                        ssb = SM[:, 64:64 + nb].rearrange("p (b o) -> p b o", o=1) \
                            .broadcast_to([128, nb, 32])
                        e.tt(r3(wpb[:, 0:cw], 32), r3(scr2[:, 0:cw], 32), ssb,
                             OP.mult)
                        P.ev("v")
                    vmk = EV["v"].n
                    pp_cmk[i] = (EV["v"].h, vmk)

                    @P.on("g")
                    def _(e, vmk=vmk, r0=r0, c0=c0, wpb=wpb):
                        e.raw.wait_ge(EV["v"].h, vmk)
                        gdma(e, Hn[hkey][r0:r0 + 128, c0:c0 + cw],
                                     wpb[:, 0:cw])
                    pp_gmk[i] = gdm()

        lpbq("L1_ffn_w_gate", "wgn", FF1, D)
        lpbq("L1_ffn_w_up", "wun", FF1, D)
        lpbq("L1_ffn_w_down", "wdn", D, FF1)
        P.barrier()

        xp_mk = [None, None]
        for ik in range(IT):
            i = ik % 2
            wpb = PSETS[i][3]
            pm = xp_mk[i] if xp_mk[i] is not None else pp_gmk[i]
            if pm is not None:
                @P.on("sy")
                def _(e, pm=pm):
                    e.raw.wait_ge(*pm)
            ld(wpb[:, 0:96], Hn["xpn"][0:96, ik * 128:(ik + 1) * 128],
               transpose=True)
            mark = ldm()

            @P.on("v")
            def _(e, mark=mark, ik=ik, wpb=wpb):
                e.raw.wait_ge(*mark)
                tile = wpb[:, 0:96]
                e.copy(XPT[:, ik, 0:64], tile[:, 0:64])
                for r in range(8):
                    e.copy(XPT[:, ik, 64 + 16 * r:80 + 16 * r], tile[:, 64:80])
                for r in range(8):
                    e.copy(XPT[:, ik, 192 + 16 * r:208 + 16 * r], tile[:, 80:96])
                P.ev("v")
            xp_mk[i] = (EV["v"].h, EV["v"].n)
        ld(DTPT[:, :], Hn["dtpn"][:, 0:128], transpose=True)
        P.barrier()

        def emit_ln(ts0, TCc, dst, dst_off, g_col=None, b_col=None, store_mv=None):
            XC_ = r3(PC[:, 0:DT * TCc], TCc)
            XBQ = r3(bf(PC[:, 4096:4096 + DT * TCc // 2])[:, 0:DT * TCc], TCc)
            Xsl = X[:, :, ts0:ts0 + TCc]
            pm, pv = PS[6], PS[7]

            @P.on("s")
            def _(e):
                P.bwait(e, 6)
                P.bwait(e, 7)
                if P.lnscr is not None:
                    e.raw.wait_ge(*P.lnscr)
                P.wt(e, "v")
                P.ev("s", e.act(XBQ, Xsl, AF.Identity))
            smk = EV["s"].n

            @P.on("t")
            def _(e):
                e.raw.wait_ge(EV["s"].h, smk)
                for k in range(DT):
                    ins = e.mm(pm[:, 0:TCc], ONESD[:], XBQ[:, k, :],
                               start=(k == 0), stop=(k == DT - 1))
                P.ev("t", ins)
            tmk = EV["t"].n

            @P.on("s")
            def _(e):
                e.raw.wait_ge(EV["t"].h, tmk)
                P.ev("s", e.scopy(MREP[:, 0:TCc], pm[:, 0:TCc]))
            P.bfree(6)
            smk2 = EV["s"].n

            @P.on("v")
            def _(e):
                e.raw.wait_ge(EV["s"].h, smk2)
                mb = MREP[:, 0:TCc].rearrange("p (a b) -> p a b", a=1) \
                    .broadcast_to([128, DT, TCc])
                e.tt(XC_, Xsl, mb, OP.subtract)
                P.ev("v")
            vmk = EV["v"].n

            @P.on("s")
            def _(e):
                e.raw.wait_ge(EV["v"].h, vmk)
                P.ev("s", e.act(XBQ, XC_, AF.Square))
            smk3 = EV["s"].n

            @P.on("t")
            def _(e):
                e.raw.wait_ge(EV["s"].h, smk3)
                for k in range(DT):
                    ins = e.mm(pv[:, 0:TCc], ONESD[:], XBQ[:, k, :],
                               start=(k == 0), stop=(k == DT - 1))
                P.ev("t", ins)
            tmk2 = EV["t"].n
            P.lnscr = (EV["t"].h, EV["t"].n)

            @P.on("s")
            def _(e):
                e.raw.wait_ge(EV["t"].h, tmk2)
                P.ev("s", e.act(MREP[:, 0:TCc], pv[:, 0:TCc], AF.Sqrt, bias=CONSTS[:, 0:1]))
            P.bfree(7)
            smk4 = EV["s"].n

            @P.on("v")
            def _(e):
                e.raw.wait_ge(EV["s"].h, smk4)
                e.recip(VREP[:, 0:TCc], MREP[:, 0:TCc])
                e.drain()
                vb = VREP[:, 0:TCc].rearrange("p (a b) -> p a b", a=1) \
                    .broadcast_to([128, DT, TCc])
                e.tt(dst[:, :, dst_off:dst_off + TCc], XC_, vb, OP.mult)
                if g_col is not None:
                    e.drain()
                    for k in range(DT):
                        e.ts(dst[:, k, dst_off:dst_off + TCc],
                             dst[:, k, dst_off:dst_off + TCc],
                             VEC[:, g_col + k:g_col + k + 1],
                             VEC[:, b_col + k:b_col + k + 1], OP.mult, OP.add)
                if store_mv is not None:
                    mdst, vdst = store_mv
                    e.copy(mdst, MREP[:, 0:TCc])
                    e.copy(vdst, VREP[:, 0:TCc])
                P.ev("v")

        def sub_ffn0():
            W1TH = r3(bf(PA[:, 0:8192]), 2048)
            W2KT = r3(bf(PA[:, 8192:10240]), 1024)
            HB_ = r3(bf(PB[:, 0:8192]), TCF)
            for tseg in range(NCF):
                emit_ln(tseg * TCF, TCF, LNB, 0)
                lnmk = EV["v"].n
                for fh in range(2):
                    tprev = EV["t"].n

                    @P.on("sy")
                    def _(e):
                        e.raw.wait_ge(EV["t"].h, tprev)
                    for k in range(DT):
                        ld(W1TH[:, k, :], Hn["w1n"][fh * 2048:(fh + 1) * 2048,
                                                    k * 128:(k + 1) * 128],
                           transpose=True)
                    ldmk = ldm()
                    for pf in range(16):
                        f = fh * 16 + pf
                        pmx = PS[pf % 4]

                        @P.on("t")
                        def _(e, pmx=pmx, pf=pf, f=f):
                            e.raw.wait_ge(*ldmk)
                            e.raw.wait_ge(EV["v"].h, lnmk)
                            P.wt(e, "s")
                            for k in range(DT):
                                ins = e.mm(pmx[:, 0:TCF],
                                           W1TH[:, k, pf * 128:(pf + 1) * 128],
                                           LNB[:, k, 0:TCF],
                                           start=(k == 0), stop=(k == DT - 1))
                            P.ev("t", ins)
                        tmk = EV["t"].n

                        @P.on("s")
                        def _(e, tmk=tmk, f=f, pmx=pmx):
                            e.raw.wait_ge(EV["t"].h, tmk)
                            P.ev("s", e.act(HB_[:, f, :], pmx[:, 0:TCF], AF.Gelu,
                                            bias=VEC2[:, f:f + 1]))
                smk = EV["s"].n
                w2marks = [None] * 4
                qglob = [0]
                for dk in range(DT):
                    pmo = PS[4 + dk % 2]

                    @P.on("t")
                    def _(e, dk=dk):
                        P.bwait(e, 4 + dk % 2)
                    for kk in range(FT0):
                        if kk % 2 == 0:
                            q = qglob[0]
                            for j in range(2):
                                slj = (q + j) % 4
                                mk = w2marks[slj]

                                @P.on("sy")
                                def _(e, mk=mk):
                                    if mk is not None:
                                        e.raw.wait_ge(*mk)
                                ld(W2KT[:, slj, :],
                                   Hn["w2n"][0:1024, (kk + j) * 128:(kk + j + 1) * 128],
                                   transpose=True)
                            ldmk2 = ldm()

                            @P.on("t")
                            def _(e):
                                e.raw.wait_ge(*ldmk2)
                                e.raw.wait_ge(EV["s"].h, smk)

                        @P.on("t")
                        def _(e, pmo=pmo, dk=dk, kk=kk):
                            sl_ = (qglob[0] + (kk % 2)) % 4
                            ins = e.mm(pmo[:, 0:TCF],
                                       W2KT[:, sl_, dk * 128:(dk + 1) * 128],
                                       HB_[:, kk, :], start=(kk == 0),
                                       stop=(kk == FT0 - 1))
                            if kk % 2 == 1 or kk == FT0 - 1:
                                P.ev("t", ins)
                        if kk % 2 == 1:
                            w2marks[qglob[0] % 4] = (EV["t"].h, EV["t"].n)
                            w2marks[(qglob[0] + 1) % 4] = (EV["t"].h, EV["t"].n)
                            qglob[0] = (qglob[0] + 2) % 4
                    tmk = EV["t"].n

                    @P.on("s")
                    def _(e, tmk=tmk, dk=dk, pmo=pmo):
                        e.raw.wait_ge(EV["t"].h, tmk)
                        P.wt(e, "v")
                        P.ev("s", e.act(MREP[:, 0:TCF], pmo[:, 0:TCF], AF.Identity,
                                        bias=VEC2[:, 88 + dk:88 + dk + 1], scale=0.5))
                    P.bfree(4 + dk % 2)
                    smk2 = EV["s"].n

                    @P.on("v")
                    def _(e, smk2=smk2, dk=dk, tseg=tseg):
                        e.raw.wait_ge(EV["s"].h, smk2)
                        xs = X[:, dk, tseg * TCF:(tseg + 1) * TCF]
                        e.tt(xs, xs, MREP[:, 0:TCF], OP.add)
                        P.ev("v")
            P.barrier()

        def sub_rwkv():
            TCq = TCW
            NCq = NCW
            RSUB = 8
            KB = r3(PA[:, 0:DT * TCq], TCq)
            E2 = r3(PA[:, DT * TCq:2 * DT * TCq], TCq)
            BARR = r3(PA[:, 2 * DT * TCq:2 * DT * TCq + DT * (TCq + 1)], TCq + 1)
            XT = r3(PA[:, 6160:6160 + DT * (TCq + 1)], TCq + 1)
            SGX = r3(hf(PA[:, 8216:8216 + DT * TCq // 2]), TCq)
            RAR = r3(hf(PA[:, 9240:9240 + DT * (TCq + 1) // 2]), TCq + 1)
            VB = r3(hf(PA[:, 10268:10268 + DT * TCq // 2]), TCq)
            YW = r3(PA[:, 11292:11292 + DT * TCq], TCq)
            LNSJ = PA[:, 11292:11292 + (TCq // RSUB) * DT]
            RGB = r3(bf(PC[:, 2048:2048 + DT * TCq // 2]), TCq)
            WKT = r3(bf(PB[:, 0:4096]), 1024)
            WVT = r3(bf(PB[:, 4096:8192]), 1024)
            WRS = r3(bf(PC[:, 5120:6144]), 1024)
            WEXP = SM[:, 120:128]
            NLC = SM[:, 112:120]
            a_, m_, p_, r_, racc = (WKS[:, 0:8], WKS[:, 8:16], WKS[:, 16:24],
                                    WKS[:, 24:32], WKS[:, 32:40])
            for k in range(DT):
                ld(WKT[:, k, :], Hn["wkn"][k * 128:(k + 1) * 128, :])
                ld(WVT[:, k, :], Hn["wvn"][k * 128:(k + 1) * 128, :])
            wldmk = ldm()

            @P.on("s")
            def _(e):
                P.wt(e, "v")
                P.ev("s", e.act(WEXP, VEC[:, 0:8], AF.Exp))

            @P.on("v")
            def _(e):
                P.wt(e, "s")
                for k in range(DT):
                    e.copy(LNB[:, k, 0:1], VEC[:, 48 + k:48 + k + 1])
                e.memset(WCAR[:, 0:8], -1.0e38)
                e.memset(WCAR[:, 8:16], 0.0)
                P.ev("v")

            def dense_mm(wt, dcast, ldmark, lnmk, freemk):
                for dk in range(DT):
                    pmx = PS[dk % 4]

                    @P.on("t")
                    def _(e, pmx=pmx, dk=dk):
                        e.raw.wait_ge(*ldmark)
                        e.raw.wait_ge(*lnmk)
                        if freemk is not None:
                            e.raw.wait_ge(*freemk)
                        P.bwait(e, dk % 4)
                        for k in range(DT):
                            ins = e.mm(pmx[:, 0:TCq], wt[:, k, dk * 128:(dk + 1) * 128],
                                       LNB[:, k, 0:TCq], start=(k == 0),
                                       stop=(k == DT - 1))
                        P.ev("t", ins)
                    tmk = EV["t"].n

                    @P.on("s")
                    def _(e, tmk=tmk, dk=dk, pmx=pmx):
                        e.raw.wait_ge(EV["t"].h, tmk)
                        P.ev("s", dcast(e, dk, pmx))
                    P.bfree(dk % 4)

            kb_free = [None]
            vb_free = [None]
            rgb_free = [None]
            wrs_marks = [None, None]
            for c in range(NCq):
                emit_ln(c * TCq, TCq, LNB, 1, g_col=16, b_col=24)
                lnmk = (EV["v"].h, EV["v"].n)
                dense_mm(WKT, lambda e, dk, pmx: e.scopy(KB[:, dk, 0:TCq],
                                                         pmx[:, 0:TCq]), wldmk,
                         lnmk, kb_free[0])
                dense_mm(WVT, lambda e, dk, pmx: e.scopy(VB[:, dk, 0:TCq],
                                                         pmx[:, 0:TCq]), wldmk,
                         lnmk, vb_free[0])
                for dk in range(DT):
                    pmo = PS[4 + dk % 2]

                    @P.on("t")
                    def _(e, dk=dk):
                        e.raw.wait_ge(*lnmk)
                        if rgb_free[0] is not None:
                            e.raw.wait_ge(*rgb_free[0])
                        P.bwait(e, 4 + dk % 2)
                    for kk in range(DT):
                        if kk % 2 == 0:
                            for j in range(2):
                                mkj = wrs_marks[j]

                                @P.on("sy")
                                def _(e, mkj=mkj):
                                    if mkj is not None:
                                        e.raw.wait_ge(*mkj)
                                ld(WRS[:, j, :],
                                   Hn["wrn"][(kk + j) * 128:(kk + j + 1) * 128, :])
                            ldmk3 = ldm()

                            @P.on("t")
                            def _(e, ldmk3=ldmk3):
                                e.raw.wait_ge(*ldmk3)

                        @P.on("t")
                        def _(e, pmo=pmo, dk=dk, kk=kk):
                            ins = e.mm(pmo[:, 0:TCq],
                                       WRS[:, kk % 2, dk * 128:(dk + 1) * 128],
                                       LNB[:, kk, 0:TCq], start=(kk == 0),
                                       stop=(kk == DT - 1))
                            if kk % 2 == 1:
                                P.ev("t", ins)
                        if kk % 2 == 1:
                            wrs_marks[0] = (EV["t"].h, EV["t"].n)
                            wrs_marks[1] = (EV["t"].h, EV["t"].n)
                    tmk = EV["t"].n

                    @P.on("s")
                    def _(e, tmk=tmk, dk=dk, pmo=pmo):
                        e.raw.wait_ge(EV["t"].h, tmk)
                        P.ev("s", e.scopy(RGB[:, dk, 0:TCq], pmo[:, 0:TCq]))
                    P.bfree(4 + dk % 2)
                @P.on("v")
                def _(e):
                    P.wt(e, "s")
                    for k in range(DT):
                        e.copy(LNB[:, k, 0:1], LNB[:, k, TCq:TCq + 1])
                    P.ev("v")

                smk_k = (EV["s"].h, EV["s"].n)

                @P.on("v")
                def _(e, c=c):
                    e.raw.wait_ge(*smk_k)
                    if c > 0:
                        e.ts(NLC, WCAR[:, 0:8], -1.0, 0.0, OP.mult, OP.add)
                    e.memset(racc, 1.0)
                    e.memset(BARR[:, :, 0:1], 0.0 if c == 0 else 1.0)
                    e.tt(XT[:, :, 0:1].rearrange("p a b -> p (a b)"),
                         WCAR[:, 0:8], KB[:, :, 0:1].rearrange("p a b -> p (a b)"),
                         OP.subtract)
                    P.ev("v")
                vmk_e = (EV["v"].h, EV["v"].n)

                @P.on("s")
                def _(e, c=c):
                    e.raw.wait_ge(*vmk_e)
                    for k in range(DT):
                        if c == 0:
                            e.act(E2[:, k, :], KB[:, k, :], AF.Exp)
                        else:
                            e.act(E2[:, k, :], KB[:, k, :], AF.Exp,
                                  bias=NLC[:, k:k + 1])
                    e.drain()
                    P.ev("s")
                smk_e = (EV["s"].h, EV["s"].n)

                @P.on("v")
                def _(e, c=c):
                    e.raw.wait_ge(*smk_e)
                    for i in range(TCq):
                        sub0 = (i % RSUB == 0)
                        if sub0 and i > 0:
                            bpre = BARR[:, :, i:i + 1].rearrange("p a b -> p (a b)")
                            e.recip(r_, bpre)
                            e.tt(racc, racc, r_, OP.mult)
                            rb = racc.rearrange("p (a b) -> p a b", b=1) \
                                .broadcast_to([128, DT, RSUB])
                            e.tt(E2[:, :, i:i + RSUB], E2[:, :, i:i + RSUB], rb,
                                 OP.mult)
                        ecol = E2[:, :, i:i + 1].rearrange("p a b -> p (a b)")
                        if sub0:
                            e.ts(a_, ecol, 1.0, 0.0, OP.add, OP.add)
                            if c == 0 and i == 0:
                                e.copy(m_, ecol)
                                e.memset(a_, 0.0)
                                e.drain()
                                e.tt(a_, a_, ecol, OP.add)
                            else:
                                e.ts(m_, ecol, 1.0, 0.0, OP.max, OP.add)
                        else:
                            sin = BARR[:, :, i:i + 1].rearrange("p a b -> p (a b)")
                            e.tt(a_, sin, ecol, OP.add)
                            e.tt(m_, sin, ecol, OP.max)
                        e.tt(p_, a_, a_, OP.mult)
                        e.recip(r_, m_)
                        e.tt(p_, p_, r_, OP.mult)
                        e.tt(BARR[:, :, i + 1:i + 2].rearrange("p a b -> p (a b)"),
                             p_, WEXP, OP.mult)
                    P.ev("v")

                @P.on("v")
                def _(e):
                    e.tss(BARR[:, :, 1:TCq + 1], 1e-37, OP.max) if False else None
                    e.ts(BARR[:, :, 1:TCq + 1], BARR[:, :, 1:TCq + 1], 1e-37, 0.0,
                         OP.max, OP.add)
                    P.ev("v")
                vmk_c = (EV["v"].h, EV["v"].n)

                @P.on("s")
                def _(e):
                    e.raw.wait_ge(*vmk_c)
                    P.ev("s", e.act(BARR[:, :, 1:TCq + 1], BARR[:, :, 1:TCq + 1],
                                    AF.Ln))
                smk_l = (EV["s"].h, EV["s"].n)
                NJ = TCq // RSUB

                @P.on("v")
                def _(e, c=c):
                    e.raw.wait_ge(*smk_l)
                    if c == 0:
                        e.memset(LNSJ[:, 0:8], 0.0)
                    else:
                        e.copy(LNSJ[:, 0:8], WCAR[:, 0:8])
                    for j in range(1, NJ):
                        e.tt(LNSJ[:, j * 8:(j + 1) * 8], LNSJ[:, (j - 1) * 8:j * 8],
                             BARR[:, :, j * RSUB:j * RSUB + 1]
                             .rearrange("p a b -> p (a b)"), OP.add)
                    for j in range(NJ):
                        c0_ = j * RSUB
                        lb = LNSJ[:, j * 8:(j + 1) * 8] \
                            .rearrange("p (a b) -> p a b", b=1) \
                            .broadcast_to([128, DT, RSUB])
                        e.tt(XT[:, :, c0_ + 1:c0_ + 1 + RSUB],
                             BARR[:, :, c0_ + 1:c0_ + 1 + RSUB], lb, OP.add)
                    e.drain()
                    e.copy(WCAR[:, 0:8],
                           XT[:, :, TCq:TCq + 1].rearrange("p a b -> p (a b)"))
                    e.drain()
                    e.tt(XT[:, :, 1:TCq], XT[:, :, 1:TCq], KB[:, :, 1:TCq],
                         OP.subtract)
                    P.ev("v")
                vscan = EV["v"].n

                XTf = XT[:, :, 0:TCq]

                @P.on("s")
                def _(e):
                    e.raw.wait_ge(EV["v"].h, vscan)
                    e.act(SGX[:, :, :], XTf, AF.Sigmoid)
                    e.drain()
                    for k in range(DT):
                        e.act(E2[:, k, 0:TCq], XT[:, k, 0:TCq], AF.Sigmoid,
                              bias=VEC[:, 8 + k:8 + k + 1], scale=-1.0)
                    P.ev("s")
                smk5 = EV["s"].n
                DLT = E2

                @P.on("v")
                def _(e, smk5=smk5, c=c):
                    e.raw.wait_ge(EV["s"].h, smk5)
                    e.ts(YW[:, :, :], SGX[:, :, :], -1.0, 1.0, OP.mult, OP.add)
                    e.drain()
                    e.tt(YW[:, :, :], YW[:, :, :], VB[:, :, :], OP.mult)
                    e.drain()
                    e.copy(RAR[:, :, 0:1].rearrange("p a b -> p (a b)"),
                           WCAR[:, 8:16])
                    e.drain()
                    for k in range(DT):
                        e.scan(RAR[:, k, 1:TCq + 1], SGX[:, k, :], YW[:, k, :],
                               WCAR[:, 8 + k:8 + k + 1], OP.mult, OP.add)
                    e.drain()
                    e.copy(WCAR[:, 8:16],
                           RAR[:, :, TCq:TCq + 1].rearrange("p a b -> p (a b)"))
                    e.drain()
                    e.tt(YW[:, :, :], VB[:, :, :], RAR[:, :, 0:TCq], OP.subtract)
                    e.drain()
                    e.tt(YW[:, :, :], YW[:, :, :], DLT[:, :, 0:TCq], OP.mult)
                    e.drain()
                    e.tt(YW[:, :, :], YW[:, :, :], RAR[:, :, 0:TCq], OP.add)
                    P.ev("v")
                vmk6 = EV["v"].n

                @P.on("s")
                def _(e, vmk6=vmk6):
                    e.raw.wait_ge(EV["v"].h, vmk6)
                    P.ev("s", e.act(SGX[:, :, :], RGB[:, :, :], AF.Sigmoid))
                smk6 = EV["s"].n

                @P.on("v")
                def _(e, smk6=smk6, c=c):
                    e.raw.wait_ge(EV["s"].h, smk6)
                    e.tt(YW[:, :, :], YW[:, :, :], SGX[:, :, :], OP.mult)
                    e.drain()
                    xs = X[:, :, c * TCq:(c + 1) * TCq]
                    e.tt(xs, xs, YW[:, :, :], OP.add)
                    P.ev("v")
                kb_free[0] = (EV["v"].h, EV["v"].n)
                vb_free[0] = (EV["v"].h, EV["v"].n)
                rgb_free[0] = (EV["s"].h, EV["s"].n)
            P.barrier()

            TC2 = 256
            NC2 = T // TC2
            WCKT = r3(bf(PA[:, 0:8192]), 2048)
            HCK = r3(bf(PA[:, 8192:12288]), TC2)
            WCRT = r3(bf(PB[:, 0:4096]), 1024)
            WCVS = r3(bf(PB[:, 4096:5120]), 1024)
            RRB = r3(bf(PB[:, 5120:6144]), TC2)
            for k in range(DT):
                ld(WCRT[:, k, :], Hn["wcrn"][k * 128:(k + 1) * 128, :])
            wldmk2 = ldm()

            @P.on("v")
            def _(e):
                for k in range(DT):
                    e.copy(LNB[:, k, 0:1], VEC[:, 48 + k:48 + k + 1])
                P.ev("v")

            for tseg in range(NC2):
                emit_ln(tseg * TC2, TC2, LNB, 1, g_col=32, b_col=40)
                lnmk = EV["v"].n
                for dk in range(DT):
                    pmo = PS[4 + dk % 2]

                    @P.on("t")
                    def _(e, pmo=pmo, dk=dk):
                        e.raw.wait_ge(*wldmk2)
                        e.raw.wait_ge(EV["v"].h, lnmk)
                        P.bwait(e, 4 + dk % 2)
                        for k in range(DT):
                            ins = e.mm(pmo[:, 0:TC2], WCRT[:, k, dk * 128:(dk + 1) * 128],
                                       LNB[:, k, 0:TC2], start=(k == 0),
                                       stop=(k == DT - 1))
                        P.ev("t", ins)
                    tmk = EV["t"].n

                    @P.on("s")
                    def _(e, tmk=tmk, dk=dk, pmo=pmo):
                        e.raw.wait_ge(EV["t"].h, tmk)
                        P.ev("s", e.act(RRB[:, dk, :], pmo[:, 0:TC2], AF.Sigmoid))
                    P.bfree(4 + dk % 2)
                for fh in range(2):
                    tprev = EV["t"].n

                    @P.on("sy")
                    def _(e):
                        e.raw.wait_ge(EV["t"].h, tprev)
                    for k in range(DT):
                        ld(WCKT[:, k, :],
                           Hn["wckn"][k * 128:(k + 1) * 128, fh * 2048:(fh + 1) * 2048])
                    ldmk = ldm()
                    for pf in range(16):
                        pmx = PS[pf % 4]

                        @P.on("t")
                        def _(e, pmx=pmx, pf=pf):
                            e.raw.wait_ge(*ldmk)
                            e.raw.wait_ge(EV["v"].h, lnmk)
                            P.bwait(e, pf % 4)
                            for k in range(DT):
                                ins = e.mm(pmx[:, 0:TC2],
                                           WCKT[:, k, pf * 128:(pf + 1) * 128],
                                           LNB[:, k, 0:TC2], start=(k == 0),
                                           stop=(k == DT - 1))
                            P.ev("t", ins)
                        tmk = EV["t"].n

                        @P.on("s")
                        def _(e, tmk=tmk, pf=pf, fh=fh, pmx=pmx):
                            e.raw.wait_ge(EV["t"].h, tmk)
                            e.act(MREP[:, 0:TC2], pmx[:, 0:TC2], AF.Relu)
                            P.ev("s", e.act(HCK[:, fh * 16 + pf, :], MREP[:, 0:TC2],
                                            AF.Square))
                        P.bfree(pf % 4)
                smk = EV["s"].n
                accv = [PS[d][:, 0:TC2] for d in range(DT)]
                for kk in range(FT0):
                    if kk % 2 == 0:
                        tprev = EV["t"].n

                        @P.on("sy")
                        def _(e):
                            e.raw.wait_ge(EV["t"].h, tprev)
                        for j in range(2):
                            ld(WCVS[:, j, :],
                               Hn["wcvn"][(kk + j) * 128:(kk + j + 1) * 128, :])
                        ldmk4 = ldm()

                        @P.on("t")
                        def _(e):
                            e.raw.wait_ge(*ldmk4)
                            if kk == 0:
                                e.raw.wait_ge(EV["s"].h, smk)
                                for b in range(8):
                                    P.bwait(e, b)

                    @P.on("t")
                    def _(e, kk=kk):
                        for d in range(DT):
                            ins = e.mm(accv[d],
                                       WCVS[:, kk % 2, d * 128:(d + 1) * 128],
                                       HCK[:, kk, :], start=(kk == 0),
                                       stop=(kk == FT0 - 1))
                        if kk % 2 == 1:
                            P.ev("t", ins)
                tmk = EV["t"].n
                for dk in range(DT):
                    @P.on("v")
                    def _(e, tmk=tmk, dk=dk, tseg=tseg):
                        e.raw.wait_ge(EV["t"].h, tmk)
                        e.tt(MREP[:, 0:TC2], RRB[:, dk, :], accv[dk], OP.mult)
                        e.drain()
                        xs = X[:, dk, tseg * TC2:(tseg + 1) * TC2]
                        e.tt(xs, xs, MREP[:, 0:TC2], OP.add)
                        P.ev("v")
                for b in range(8):
                    P.bfree(b, key="v")

                @P.on("v")
                def _(e):
                    P.wt(e, "t")
                    for k in range(DT):
                        e.copy(LNB[:, k, 0:1], LNB[:, k, TC2:TC2 + 1])
                    P.ev("v")
            P.barrier()


        def allreduce_max(srcs):
            n = len(srcs)
            for i, src in enumerate(srcs):
                vm = (EV["v"].h, EV["v"].n)

                @P.on("g")
                def _(e, src=src, vm=vm):
                    e.raw.wait_ge(*vm)
                    gdma(e, GS1[0:1, 0:128], src,
                         allow_slow_non_contiguous=True)
                dgmk = gdm()

                @P.on("v")
                def _(e, dgmk=dgmk, i=i):
                    e.raw.wait_ge(*dgmk)
                    e.red(GS1[0:1, 152 + i:153 + i], GS1[0:1, 0:128], OP.max)
                    P.ev("v")
            vmk = (EV["v"].h, EV["v"].n)

            @P.on("g")
            def _(e):
                e.raw.wait_ge(*vmk)
                gdma(e, Hn["ccin"][0:1, 0:n], GS1[0:1, 152:152 + n])
                m = gdm()
                e.raw.wait_ge(*m)
                CC.inc(e.collective("AllReduce", OP.max, group,
                                    [Hn["ccin"][0:1, 0:n]],
                                    [Hn["ccout"][0:1, 0:n]]), 1)
                e.raw.wait_ge(CC.h, CC.n)
                gdma(e, GS1[0:1, 152:152 + n], Hn["ccout"][0:1, 0:n])
            armk = gdm()

            @P.on("v")
            def _(e):
                e.raw.wait_ge(*armk)
                P.ev("v")
            return [GS1[0:1, 152 + i:153 + i] for i in range(n)]

        def bcast_cols(pairs):
            @P.on("g")
            def _(e):
                P.wt(e, "v")
                for src, col in pairs:
                    gdma(e, Hn["bounce"][0:1, col:col + 1], src)
                m = gdm()
                e.raw.wait_ge(*m)
                for src, col in pairs:
                    gdma(e, GS[:, col:col + 1],
                         Hn["bounce"][0:1, col:col + 1].broadcast_to([128, 1]))
            return gdm()

        def sub_qffn(callidx):
            QXB = r3(bf(PA[:, 0:DT * T // 2]), T)
            WQH = r3(bf(PA[:, 8192:14336]), 1536)
            STGB = r3(bf(PB[:, 0:2048]), TCF)
            STGI = PB[:, 2048:2304].bitcast(I16)[:, 0:TCF]
            STGI2 = PB[:, 2304:2560].bitcast(I16)[:, 0:TCF]
            S2 = PC[:, 2048:2560][:, 0:TCF]
            S3 = PC[:, 2560:3072][:, 0:TCF]
            S4 = PC[:, 3072:3584][:, 0:TCF]
            AMX = SM[:, 0:16]

            gc_rsx, gc_half, gc_rsup, gc_rsg, gc_sg, gc_rsg2, gc_rso, gc_so = \
                range(2, 10)

            @P.on("v")
            def _(e):
                e.memset(AMX[:, 0:6], 0.0)
                P.ev("v")
            for tseg in range(NCF):
                emit_ln(tseg * TCF, TCF, QXB, tseg * TCF)
                vmk = (EV["v"].h, EV["v"].n)

                @P.on("v")
                def _(e, vmk=vmk, tseg=tseg):
                    e.raw.wait_ge(*vmk)
                    e.red(SM[:, 24:32], QXB[:, :, tseg * TCF:(tseg + 1) * TCF],
                          OP.max, absval=True)
                    e.drain()
                    e.red(SM[:, 20:21], SM[:, 24:32], OP.max)
                    e.drain()
                    e.tt(AMX[:, 0:1], AMX[:, 0:1], SM[:, 20:21], OP.max)
                    P.ev("v")
            if DBGQ[0] is not None:
                vq = (EV["v"].h, EV["v"].n)

                @P.on("g")
                def _(e, vq=vq):
                    e.raw.wait_ge(*vq)
                    for k in range(DT):
                        gdma(e, DBGQ[0][k * 128:(k + 1) * 128, :], QXB[:, k, :])
                    gdm()
            g = allreduce_max([AMX[:, 0:1]])[0]

            @P.on("v")
            def _(e, g=g):
                P.wt(e, "v")
                e.ts(GS1[0:1, 132:133], g, 1.0 / QMAX, 1e-12, OP.mult, OP.max)
                e.recip(GS1[0:1, 133:134], GS1[0:1, 132:133])
                P.ev("v")
            bm = bcast_cols([(GS1[0:1, 133:134], gc_rsx)])

            S2X = r3(PA[:, 8192:12288][:, 0:DT * TCF], TCF)
            STGIX = PA[:, 12288:14336].bitcast(I16)[:, 0:DT * TCF]
            T4X = PB[:, 2048:6144][:, 0:DT * TCF]
            qv_free = None
            for tseg in range(NCF):
                qvf = qv_free

                @P.on("s")
                def _(e, tseg=tseg, bm=bm, qvf=qvf):
                    e.raw.wait_ge(*bm)
                    if qvf is not None:
                        e.raw.wait_ge(*qvf)
                    P.ev("s", e.act(S2X, QXB[:, :, tseg * TCF:(tseg + 1) * TCF],
                                    AF.Identity, scale=GS[:, gc_rsx:gc_rsx + 1]))
                vm2 = (EV["s"].h, EV["s"].n)

                @P.on("v")
                def _(e2, vm2=vm2, tseg=tseg):
                    e2.raw.wait_ge(*vm2)
                    flat = S2X.rearrange("p a b -> p (a b)")
                    t4x3 = T4X.rearrange("p (a b) -> p a b", b=TCF)
                    if rmode == "int":
                        e2.copy(STGIX, flat)
                        e2.drain()
                        e2.copy(QXB[:, :, tseg * TCF:(tseg + 1) * TCF], STGIX
                                .rearrange("p (a b) -> p a b", b=TCF))
                    else:
                        e2.ts(flat, flat, 0.5, 0.0, OP.add, OP.add)
                        e2.drain()
                        e2.tss(T4X, flat, 1.0, OP.mod)
                        e2.drain()
                        e2.tt(QXB[:, :, tseg * TCF:(tseg + 1) * TCF], S2X, t4x3,
                              OP.subtract)
                    P.ev("v")
                qv_free = (EV["v"].h, EV["v"].n)
            qx_mk = (EV["v"].h, EV["v"].n)

            slot_marks = [None] * 8

            def ff_mm(hkey, rawkey, amx_col):
                for fh in range(2):
                    tprev = (EV["t"].h, EV["t"].n)

                    @P.on("sy")
                    def _(e, tprev=tprev):
                        e.raw.wait_ge(*tprev)
                        e.raw.wait_ge(*qx_mk)
                    for k in range(DT):
                        ld(WQH[:, k, :],
                           Hn[hkey][fh * 1536:(fh + 1) * 1536, k * 128:(k + 1) * 128],
                           transpose=True)
                    ldmk = ldm()
                    for tseg in range(NCF):
                        for pf in range(12):
                            f = fh * 12 + pf
                            pmx = PS[pf % 4]

                            @P.on("t")
                            def _(e, pmx=pmx, pf=pf, f=f, tseg=tseg, ldmk=ldmk,
                                  qx_mk=qx_mk):
                                e.raw.wait_ge(*ldmk)
                                e.raw.wait_ge(*qx_mk)
                                P.bwait(e, pf % 4)
                                for k in range(DT):
                                    ins = e.mm(pmx[:, 0:TCF],
                                               WQH[:, k, pf * 128:(pf + 1) * 128],
                                               QXB[:, k, tseg * TCF:(tseg + 1) * TCF],
                                               start=(k == 0), stop=(k == DT - 1))
                                P.ev("t", ins)
                            tmk = (EV["t"].h, EV["t"].n)

                            @P.on("s")
                            def _(e, tmk=tmk, pf=pf, pmx=pmx,
                                  sl=None):
                                e.raw.wait_ge(*tmk)
                                if slot_marks[pf % 8] is not None:
                                    e.raw.wait_ge(*slot_marks[pf % 8])
                                P.ev("s", e.scopy(STGB[:, pf % 8, :], pmx[:, 0:TCF]))
                            P.bfree(pf % 4)
                            if pf % 4 == 3:
                                base = (pf - 3) % 8
                                smk = (EV["s"].h, EV["s"].n)

                                @P.on("v")
                                def _(e, smk=smk, base=base, amx_col=amx_col):
                                    e.raw.wait_ge(*smk)
                                    e.red(SM[:, 21:22],
                                          STGB[:, base:base + 4, :]
                                          .rearrange("p a b -> p (a b)"), OP.max,
                                          absval=True)
                                    e.drain()
                                    e.tt(AMX[:, amx_col:amx_col + 1],
                                         AMX[:, amx_col:amx_col + 1],
                                         SM[:, 21:22], OP.max)
                                    P.ev("v")
                                vmk = (EV["v"].h, EV["v"].n)

                                @P.on("g")
                                def _(e, vmk=vmk, f=f, tseg=tseg, base=base,
                                      pf=pf, rawkey=rawkey):
                                    e.raw.wait_ge(*vmk)
                                    for jj in range(4):
                                        fj = f - 3 + jj
                                        gdma(e, Hn[rawkey][fj * 128:(fj + 1) * 128,
                                                           tseg * TCF:(tseg + 1) * TCF],
                                             STGB[:, base + jj, :])
                                gm4 = gdm()
                                for jj in range(4):
                                    slot_marks[base + jj] = gm4

            ff_mm("wun", "upraw", 1)
            ff_mm("wgn", "gateraw", 2)
            gups = allreduce_max([AMX[:, 1:2], AMX[:, 2:3]])

            @P.on("v")
            def _(e):
                P.wt(e, "v")
                for i, (dst_s, dst_rs) in enumerate([(134, 135), (136, 137)]):
                    e.tt(GS1[0:1, 142:143], GS1[0:1, 152 + i:153 + i],
                         GS1[0:1, 132:133], OP.mult)
                    e.drain()
                    e.ts(GS1[0:1, dst_s:dst_s + 1], GS1[0:1, 142:143],
                         1.0 / QMAX, 1e-12, OP.mult, OP.max)
                    e.drain()
                    e.recip(GS1[0:1, 143:144], GS1[0:1, dst_s:dst_s + 1])
                    e.drain()
                    e.tt(GS1[0:1, dst_rs:dst_rs + 1], GS1[0:1, 143:144],
                         GS1[0:1, 132:133], OP.mult)
                    e.drain()
                P.ev("v")
            bm2 = bcast_cols([(GS1[0:1, 137:138], gc_rsg),
                              (GS1[0:1, 135:136], gc_rsup),
                              (GS1[0:1, 136:137], gc_sg)])

            BQ = 6
            NG = FT1 // BQ
            QW = BQ * TCF
            UP6 = bf(PA[:, 0:QW // 2])
            GT6 = bf(PA[:, QW // 2:QW])
            S2B = PA[:, QW:2 * QW]
            S4B = PA[:, 2 * QW:3 * QW]
            S3B = PA[:, 3 * QW:4 * QW]
            STGI6 = PA[:, 4 * QW:4 * QW + QW // 2].bitcast(I16)
            STGI2B = PB[:, 0:QW // 2].bitcast(I16)
            qup_mark = [None]
            g2_mark = [None]
            sm_snapshot = [m for m in slot_marks if m is not None]

            @P.on("sy")
            def _(e):
                for m in sm_snapshot:
                    e.raw.wait_ge(*m)
            for tseg in range(NCF):
                for gi in range(NG):
                    f0 = gi * BQ
                    gm_prev = g2_mark[0]
                    if gm_prev is not None:
                        @P.on("sy")
                        def _(e, gm_prev=gm_prev):
                            e.raw.wait_ge(*gm_prev)
                    ld_guard("s", "v")
                    for j in range(BQ):
                        ld(UP6[:, j * TCF:(j + 1) * TCF],
                           Hn["upraw"][(f0 + j) * 128:(f0 + j + 1) * 128,
                                       tseg * TCF:(tseg + 1) * TCF])
                        ld(GT6[:, j * TCF:(j + 1) * TCF],
                           Hn["gateraw"][(f0 + j) * 128:(f0 + j + 1) * 128,
                                         tseg * TCF:(tseg + 1) * TCF])
                    mark = ldm()

                    @P.on("s")
                    def _(e, mark=mark, bm2=bm2):
                        e.raw.wait_ge(*mark)
                        e.raw.wait_ge(*bm2)
                        P.wt(e, "v")
                        e.act(S2B, UP6, AF.Identity,
                              scale=GS[:, gc_rsup:gc_rsup + 1])
                        e.drain()
                        P.ev("s", e.act(S3B, GT6, AF.Identity,
                                        scale=GS[:, gc_rsg:gc_rsg + 1]))
                    smk = (EV["s"].h, EV["s"].n)

                    @P.on("v")
                    def _(e, smk=smk):
                        e.raw.wait_ge(*smk)
                        if qup_mark[0] is not None:
                            e.raw.wait_ge(*qup_mark[0])
                        if rmode == "int":
                            e.copy(STGI6, S2B)
                            e.drain()
                            e.copy(STGI2B, S3B)
                        else:
                            e.ts(S2B, S2B, 0.5, 0.0, OP.add, OP.add)
                            e.drain()
                            e.tss(S4B, S2B, 1.0, OP.mod)
                            e.drain()
                            e.tt(STGI6, S2B, S4B, OP.subtract)
                            e.drain()
                            e.ts(S3B, S3B, 0.5, 0.0, OP.add, OP.add)
                            e.drain()
                            e.tss(S4B, S3B, 1.0, OP.mod)
                            e.drain()
                            e.tt(STGI2B, S3B, S4B, OP.subtract)
                        P.ev("v")
                    vmk = (EV["v"].h, EV["v"].n)

                    @P.on("g")
                    def _(e, vmk=vmk, f0=f0, tseg=tseg):
                        e.raw.wait_ge(*vmk)
                        for j in range(BQ):
                            gdma(e, Hn["qup"][(f0 + j) * 128:(f0 + j + 1) * 128,
                                              tseg * TCF:(tseg + 1) * TCF],
                                 STGI6[:, j * TCF:(j + 1) * TCF])
                    qup_mark[0] = gdm()

                    @P.on("s")
                    def _(e):
                        P.wt(e, "v")
                        P.ev("s", e.act(S2B, STGI2B, AF.Sigmoid,
                                        scale=GS[:, gc_sg:gc_sg + 1]))
                    smk2 = (EV["s"].h, EV["s"].n)

                    @P.on("v")
                    def _(e, smk2=smk2):
                        e.raw.wait_ge(*smk2)
                        e.ts(S3B, S2B, 65536.0, 0.0, OP.mult, OP.add)
                        e.drain()
                        if rmode == "int":
                            e.copy(S4B.bitcast(I32), S3B)
                            e.drain()
                            e.tss(S2B, S4B.bitcast(I32), 65535.0, OP.min)
                        else:
                            e.ts(S3B, S3B, 0.5, 0.0, OP.add, OP.add)
                            e.drain()
                            e.tss(S4B, S3B, 1.0, OP.mod)
                            e.drain()
                            e.tt(S2B, S3B, S4B, OP.subtract)
                            e.drain()
                            e.tss(S2B, S2B, 65535.0, OP.min)
                        e.drain()
                        e.tt(S3B, STGI2B, S2B, OP.mult)
                        e.drain()
                        if g2_mark[0] is not None:
                            e.raw.wait_ge(*g2_mark[0])
                        e.copy(GT6, S3B)
                        e.drain()
                        e.red(SM[:, 21:22], S3B, OP.max, absval=True)
                        e.drain()
                        e.tt(AMX[:, 3:4], AMX[:, 3:4], SM[:, 21:22], OP.max)
                        P.ev("v")
                    vmk2 = (EV["v"].h, EV["v"].n)

                    @P.on("g")
                    def _(e, vmk2=vmk2, f0=f0, tseg=tseg):
                        e.raw.wait_ge(*vmk2)
                        for j in range(BQ):
                            gdma(e, Hn["g2raw"][(f0 + j) * 128:(f0 + j + 1) * 128,
                                                tseg * TCF:(tseg + 1) * TCF],
                                 GT6[:, j * TCF:(j + 1) * TCF])
                    g2_mark[0] = gdm()

            gg2 = allreduce_max([AMX[:, 3:4]])[0]

            @P.on("v")
            def _(e, gg2=gg2):
                P.wt(e, "v")
                e.ts(GS1[0:1, 144:145], GS1[0:1, 136:137], 1.0 / 65536.0, 0.0,
                     OP.mult, OP.add)
                e.drain()
                e.tt(GS1[0:1, 145:146], gg2, GS1[0:1, 144:145], OP.mult)
                e.drain()
                e.ts(GS1[0:1, 138:139], GS1[0:1, 145:146], 1.0 / QMAX, 1e-12,
                     OP.mult, OP.max)
                e.drain()
                e.recip(GS1[0:1, 143:144], GS1[0:1, 138:139])
                e.drain()
                e.tt(GS1[0:1, 139:140], GS1[0:1, 143:144], GS1[0:1, 144:145],
                     OP.mult)
                P.ev("v")
            bm3 = bcast_cols([(GS1[0:1, 139:140], gc_rsg2)])

            oraw_mark = [None]
            if g2_mark[0] is not None:
                gm_ = g2_mark[0]
                qm_ = qup_mark[0]

                @P.on("sy")
                def _(e, gm_=gm_, qm_=qm_):
                    e.raw.wait_ge(*gm_)
                    if qm_ is not None:
                        e.raw.wait_ge(*qm_)
            for tseg in range(NCF):
                for gi in range(NG):
                    f0 = gi * BQ
                    ld_guard("s", "v")
                    for j in range(BQ):
                        ld(GT6[:, j * TCF:(j + 1) * TCF],
                           Hn["g2raw"][(f0 + j) * 128:(f0 + j + 1) * 128,
                                       tseg * TCF:(tseg + 1) * TCF])
                        ld(STGI6[:, j * TCF:(j + 1) * TCF],
                           Hn["qup"][(f0 + j) * 128:(f0 + j + 1) * 128,
                                     tseg * TCF:(tseg + 1) * TCF])
                    mark = ldm()

                    @P.on("s")
                    def _(e, mark=mark, bm3=bm3):
                        e.raw.wait_ge(*mark)
                        e.raw.wait_ge(*bm3)
                        P.wt(e, "v")
                        P.ev("s", e.act(S2B, GT6, AF.Identity,
                                        scale=GS[:, gc_rsg2:gc_rsg2 + 1]))
                    smk = (EV["s"].h, EV["s"].n)

                    @P.on("v")
                    def _(e, smk=smk):
                        e.raw.wait_ge(*smk)
                        if rmode == "int":
                            e.copy(STGI2B, S2B)
                            e.drain()
                            e.tt(S2B, STGI2B, STGI6, OP.mult)
                        else:
                            e.ts(S2B, S2B, 0.5, 0.0, OP.add, OP.add)
                            e.drain()
                            e.tss(S4B, S2B, 1.0, OP.mod)
                            e.drain()
                            e.tt(S3B, S2B, S4B, OP.subtract)
                            e.drain()
                            e.tt(S2B, S3B, STGI6, OP.mult)
                        e.drain()
                        if oraw_mark[0] is not None:
                            e.raw.wait_ge(*oraw_mark[0])
                        e.copy(UP6, S2B)
                        e.drain()
                        e.red(SM[:, 21:22], S2B, OP.max, absval=True)
                        e.drain()
                        e.tt(AMX[:, 4:5], AMX[:, 4:5], SM[:, 21:22], OP.max)
                        P.ev("v")
                    vmk = (EV["v"].h, EV["v"].n)

                    @P.on("g")
                    def _(e, vmk=vmk, f0=f0, tseg=tseg):
                        e.raw.wait_ge(*vmk)
                        for j in range(BQ):
                            gdma(e, Hn["upraw"][(f0 + j) * 128:(f0 + j + 1) * 128,
                                                tseg * TCF:(tseg + 1) * TCF],
                                 UP6[:, j * TCF:(j + 1) * TCF])
                    oraw_mark[0] = gdm()
            go = allreduce_max([AMX[:, 4:5]])[0]

            @P.on("v")
            def _(e, go=go):
                P.wt(e, "v")
                e.tt(GS1[0:1, 146:147], GS1[0:1, 138:139], GS1[0:1, 134:135],
                     OP.mult)
                e.drain()
                e.tt(GS1[0:1, 147:148], go, GS1[0:1, 146:147], OP.mult)
                e.drain()
                e.ts(GS1[0:1, 140:141], GS1[0:1, 147:148], 1.0 / QMAX, 1e-12,
                     OP.mult, OP.max)
                e.drain()
                e.recip(GS1[0:1, 143:144], GS1[0:1, 140:141])
                e.drain()
                e.tt(GS1[0:1, 141:142], GS1[0:1, 143:144], GS1[0:1, 146:147],
                     OP.mult)
                e.drain()
                e.ts(GS1[0:1, 148:149], GS1[0:1, 140:141], 0.5, 0.0, OP.mult,
                     OP.add)
                P.ev("v")
            bm4 = bcast_cols([(GS1[0:1, 141:142], gc_rso),
                              (GS1[0:1, 148:149], gc_so)])

            QOT = r3(bf(PB[:, 2048:8192]), TCF)
            WDK = r3(bf(PA[:, 8192:9216]), 1024)
            dn_free = [None, None]
            if oraw_mark[0] is not None:
                om_ = oraw_mark[0]

                @P.on("sy")
                def _(e, om_=om_):
                    e.raw.wait_ge(*om_)
            for tseg in range(NCF):
                for gi in range(NG):
                    f0 = gi * BQ
                    ld_guard("s", "v")
                    for j in range(BQ):
                        ld(UP6[:, j * TCF:(j + 1) * TCF],
                           Hn["upraw"][(f0 + j) * 128:(f0 + j + 1) * 128,
                                       tseg * TCF:(tseg + 1) * TCF])
                    mark = ldm()

                    @P.on("s")
                    def _(e, mark=mark, bm4=bm4):
                        e.raw.wait_ge(*mark)
                        e.raw.wait_ge(*bm4)
                        P.wt(e, "v")
                        P.ev("s", e.act(S2B, UP6, AF.Identity,
                                        scale=GS[:, gc_rso:gc_rso + 1]))
                    smk = (EV["s"].h, EV["s"].n)

                    @P.on("v")
                    def _(e, smk=smk, f0=f0):
                        e.raw.wait_ge(*smk)
                        P.wt(e, "t")
                        qsl = QOT[:, f0:f0 + BQ, :].rearrange("p a b -> p (a b)")
                        if rmode == "int":
                            e.copy(STGI2B, S2B)
                            e.drain()
                            e.copy(qsl, STGI2B)
                        else:
                            e.ts(S2B, S2B, 0.5, 0.0, OP.add, OP.add)
                            e.drain()
                            e.tss(S3B, S2B, 1.0, OP.mod)
                            e.drain()
                            e.tt(S2B, S2B, S3B, OP.subtract)
                            e.drain()
                            e.copy(qsl, S2B)
                        P.ev("v")
                qomk = (EV["v"].h, EV["v"].n)

                @P.on("t")
                def _(e):
                    for b in range(8):
                        P.bwait(e, b)
                for kk in range(FT1):
                    if kk % 2 == 0:
                        tprev = (EV["t"].h, EV["t"].n)

                        @P.on("sy")
                        def _(e, tprev=tprev):
                            e.raw.wait_ge(*tprev)
                        for j in range(2):
                            ld(WDK[:, j, :],
                               Hn["wdn"][0:1024, (kk + j) * 128:(kk + j + 1) * 128],
                               transpose=True)
                        ldmk2 = ldm()

                        @P.on("t")
                        def _(e, ldmk2=ldmk2, qomk=qomk, kk=kk):
                            e.raw.wait_ge(*ldmk2)
                            if kk == 0:
                                e.raw.wait_ge(*qomk)

                    @P.on("t")
                    def _(e, kk=kk):
                        for dk in range(DT):
                            ins = e.mm(PS[dk][:, 0:TCF],
                                       WDK[:, kk % 2, dk * 128:(dk + 1) * 128],
                                       QOT[:, kk, :], start=(kk == 0),
                                       stop=(kk == FT1 - 1))
                        if kk % 2 == 1:
                            P.ev("t", ins)
                tmk = (EV["t"].h, EV["t"].n)
                for dk in range(DT):
                    drep = MREP if dk % 2 == 0 else VREP
                    dfree = dn_free[dk % 2]

                    @P.on("s")
                    def _(e, tmk=tmk, dk=dk, drep=drep, dfree=dfree):
                        e.raw.wait_ge(*tmk)
                        if dfree is not None:
                            e.raw.wait_ge(*dfree)
                        P.ev("s", e.act(drep[:, 0:TCF], PS[dk][:, 0:TCF], AF.Identity,
                                        scale=GS[:, gc_so:gc_so + 1]))
                    P.bfree(dk)
                    smk2 = (EV["s"].h, EV["s"].n)

                    @P.on("v")
                    def _(e, smk2=smk2, dk=dk, tseg=tseg, drep=drep):
                        e.raw.wait_ge(*smk2)
                        xs = X[:, dk, tseg * TCF:(tseg + 1) * TCF]
                        e.tt(xs, xs, drep[:, 0:TCF], OP.add)
                        P.ev("v")
                    dn_free[dk % 2] = (EV["v"].h, EV["v"].n)
            P.barrier()

        def sub_mamba():
            BCB_B = r3(hf(PA[:, 0:4096]), TCF)
            BCB_C = r3(hf(PA[:, 4096:8192]), TCF)
            YBUF = r3(bf(PA[:, 8192:12288]), TCF)
            DTB = PA[:, 12288:12800][:, 0:TCF]
            PBt = hf(PA[:, 12800:13056])[:, 0:TCF]
            A2 = r3(hf(PA[:, 13056:13568])[:, 0:2 * TCF], TCF)
            U2 = r3(hf(PA[:, 13568:14080])[:, 0:2 * TCF], TCF)
            H2 = r3(hf(PC[:, 5120:5632])[:, 0:2 * TCF], TCF)
            HC2 = r3(hf(PC[:, 5632:6144])[:, 0:2 * TCF], TCF)
            DTR_ = hf(PA[:, 14080:14336])[:, 0:TCF]
            XS = r3(hf(PB[:, 0:4096]), TCF)
            ZB = r3(bf(PB[:, 4096:8192]), TCF)
            WQ = r3(bf(PA[:, 0:8192]), 2048)
            XCB = PC[:, 0:520]
            CACC = PC[:, 1024:1544][:, 0:TCF]
            SYV = PC[:, 2048:2560][:, 0:TCF]
            SZ = PC[:, 2560:3072][:, 0:TCF]
            BROW = hf(PC[:, 3072:3584])

            @P.on("v")
            def _(e):
                e.memset(CARRY[:, :, :], 0.0)
                e.memset(XHALO[:, :, :], 0.0)
                P.ev("v")

            prev_ymk = None
            brow_free = []
            dtb_free = None
            a_free = None
            for tseg in range(NCF):
                emit_ln(tseg * TCF, TCF, LNB, 0)
                lnmk = (EV["v"].h, EV["v"].n)
                for eh in range(2):
                    tprev = (EV["t"].h, EV["t"].n)
                    pym = prev_ymk

                    @P.on("sy")
                    def _(e, tprev=tprev, pym=pym):
                        e.raw.wait_ge(*tprev)
                        if pym is not None:
                            e.raw.wait_ge(*pym)
                    for k in range(DT):
                        ld(WQ[:, k, :],
                           Hn["inpn"][eh * 2048:(eh + 1) * 2048, k * 128:(k + 1) * 128],
                           transpose=True)
                    ldmk = ldm()
                    for et in range(IT):
                        pmx = PS[et % 4]

                        @P.on("t")
                        def _(e, pmx=pmx, et=et, ldmk=ldmk, lnmk=lnmk):
                            e.raw.wait_ge(*ldmk)
                            e.raw.wait_ge(*lnmk)
                            P.bwait(e, et % 4)
                            for k in range(DT):
                                ins = e.mm(pmx[:, 0:TCF],
                                           WQ[:, k, et * 128:(et + 1) * 128],
                                           LNB[:, k, 0:TCF], start=(k == 0),
                                           stop=(k == DT - 1))
                            P.ev("t", ins)
                        tmk = (EV["t"].h, EV["t"].n)
                        if eh == 1:
                            @P.on("s")
                            def _(e, tmk=tmk, et=et, pmx=pmx):
                                e.raw.wait_ge(*tmk)
                                P.ev("s", e.scopy(ZB[:, et, :], pmx[:, 0:TCF]))
                            P.bfree(et % 4)
                        else:
                            @P.on("s")
                            def _(e, tmk=tmk, et=et, pmx=pmx):
                                e.raw.wait_ge(*tmk)
                                P.ev("s", e.scopy(XCB[:, 3:3 + TCF], pmx[:, 0:TCF]))
                            P.bfree(et % 4)
                            smk = (EV["s"].h, EV["s"].n)

                            @P.on("v")
                            def _(e, smk=smk, et=et):
                                e.raw.wait_ge(*smk)
                                e.copy(XCB[:, 0:3], XHALO[:, et, :])
                                e.drain()
                                e.copy(XHALO[:, et, :], XCB[:, TCF:TCF + 3])
                                e.drain()
                                e.ts(CACC, XCB[:, 0:TCF], CONVW[:, et:et + 1],
                                     0.0, OP.mult, OP.add)
                                e.drain()
                                for j in range(1, 4):
                                    e.stt(CACC, XCB[:, j:j + TCF],
                                          CONVW[:, j * IT + et:j * IT + et + 1],
                                          CACC, OP.mult, OP.add)
                                    e.drain()
                                P.ev("v")
                            vmk = (EV["v"].h, EV["v"].n)

                            @P.on("s")
                            def _(e, vmk=vmk, et=et):
                                e.raw.wait_ge(*vmk)
                                P.ev("s", e.act(XS[:, et, :], CACC, AF.Silu,
                                                bias=VEC2[:, 40 + et:41 + et]))
                bc_marks = []
                smk0 = (EV["s"].h, EV["s"].n)
                for ci, (c0, cn, pidx) in enumerate([(0, 64, 0), (64, 128, 1),
                                                     (192, 128, 2)]):
                    pmx = PS[pidx]

                    @P.on("t")
                    def _(e, c0=c0, cn=cn, pmx=pmx, smk0=smk0, pidx=pidx):
                        e.raw.wait_ge(*smk0)
                        P.bwait(e, pidx)
                        for k in range(IT):
                            ins = e.mm(pmx[0:cn, 0:TCF], XPT[:, k, c0:c0 + cn],
                                       XS[:, k, :], start=(k == 0),
                                       stop=(k == IT - 1))
                        P.ev("t", ins)
                    tmk = (EV["t"].h, EV["t"].n)
                    if ci == 0:
                        @P.on("s")
                        def _(e, tmk=tmk, pmx=pmx):
                            e.raw.wait_ge(*tmk)
                            P.ev("s", e.scopy(DTR_[0:64, :], pmx[0:64, 0:TCF]))
                        P.bfree(pidx)
                        dtr_mk = (EV["s"].h, EV["s"].n)
                    else:
                        bfw = list(brow_free)

                        @P.on("s")
                        def _(e, tmk=tmk, pmx=pmx, ci=ci, bfw=bfw):
                            e.raw.wait_ge(*tmk)
                            for m_ in bfw:
                                e.raw.wait_ge(*m_)
                            P.ev("s", e.scopy(BROW[:, (ci - 1) * TCF:ci * TCF],
                                              pmx[:, 0:TCF]))
                        P.bfree(pidx)
                        smk = (EV["s"].h, EV["s"].n)
                        pym = prev_ymk

                        @P.on("g")
                        def _(e, smk=smk, ci=ci, tseg=tseg, pym=pym):
                            e.raw.wait_ge(*smk)
                            hn = Hn["rowbB"] if ci == 1 else Hn["rowbC"]
                            for n_ in range(DS):
                                gdma(e, hn[n_:n_ + 1, tseg * TCF:(tseg + 1) * TCF],
                                     BROW[n_:n_ + 1, (ci - 1) * TCF:(ci - 1) * TCF + TCF])
                            m = gdm()
                            e.raw.wait_ge(*m)
                            if pym is not None:
                                e.raw.wait_ge(*pym)
                            dstt = BCB_B if ci == 1 else BCB_C
                            for n_ in range(DS):
                                gdma(e, dstt[:, n_, :],
                                     hn[n_:n_ + 1, tseg * TCF:(tseg + 1) * TCF]
                                     .broadcast_to([128, TCF]))
                        bc_marks.append(gdm())
                for it_ in range(IT):
                    pdt = PS[3]

                    @P.on("t")
                    def _(e, it_=it_, pdt=pdt, dtr_mk=dtr_mk):
                        e.raw.wait_ge(*dtr_mk)
                        P.bwait(e, 3)
                        ins = e.mm(pdt[:, 0:TCF], DTPT[0:64, it_ * 128:(it_ + 1) * 128],
                                   DTR_[0:64, :], start=True, stop=True)
                        P.ev("t", ins)
                    tmk = (EV["t"].h, EV["t"].n)
                    dtbf = dtb_free

                    @P.on("s")
                    def _(e, tmk=tmk, it_=it_, pdt=pdt, dtbf=dtbf):
                        e.raw.wait_ge(*tmk)
                        if dtbf is not None:
                            e.raw.wait_ge(*dtbf)
                        e.act(MREP[:, 0:TCF], pdt[:, 0:TCF],
                              AF.Exp, bias=VEC2[:, 56 + it_:57 + it_])
                        e.drain()
                        P.ev("s", e.act(DTB, MREP[:, 0:TCF], AF.Ln,
                                        bias=CONSTS[:, 3:4]))
                    P.bfree(3)
                    smk = (EV["s"].h, EV["s"].n)

                    @P.on("v")
                    def _(e, smk=smk, it_=it_):
                        e.raw.wait_ge(*smk)
                        e.tt(PBt, DTB, XS[:, it_, :], OP.mult)
                        P.ev("v")
                    pmk = (EV["v"].h, EV["v"].n)
                    dtb_free = pmk
                    pyp = PS[2]
                    for n2 in range(DS // 2):
                        n_ = 2 * n2
                        af = a_free

                        @P.on("s")
                        def _(e, it_=it_, n_=n_, pmk=pmk, af=af):
                            if n_ == 0:
                                e.raw.wait_ge(*pmk)
                            if af is not None:
                                e.raw.wait_ge(*af)
                            e.act(A2[:, 0, :], DTB, AF.Exp,
                                  scale=ASB[:, it_ * DS + n_:it_ * DS + n_ + 1])
                            P.ev("s", e.act(A2[:, 1, :], DTB, AF.Exp,
                                            scale=ASB[:, it_ * DS + n_ + 1:
                                                      it_ * DS + n_ + 2]))
                        smk2 = (EV["s"].h, EV["s"].n)

                        @P.on("v")
                        def _(e, smk2=smk2, it_=it_, n_=n_, bc_marks=bc_marks):
                            e.raw.wait_ge(*smk2)
                            if it_ == 0 and n_ == 0:
                                for m_ in bc_marks:
                                    e.raw.wait_ge(*m_)
                            P.wt(e, "t")
                            pb2 = PBt.rearrange("p (a b) -> p a b", a=1) \
                                .broadcast_to([128, 2, TCF])
                            e.tt(U2[:, :, :], pb2, BCB_B[:, n_:n_ + 2, :], OP.mult)
                            e.drain()
                            e.scan(H2[:, 0, :], A2[:, 0, :], U2[:, 0, :],
                                   CARRY[:, it_, n_:n_ + 1], OP.mult, OP.add)
                            e.scan(H2[:, 1, :], A2[:, 1, :], U2[:, 1, :],
                                   CARRY[:, it_, n_ + 1:n_ + 2], OP.mult, OP.add)
                            e.drain()
                            e.copy(CARRY[:, it_, n_:n_ + 2],
                                   H2[:, :, TCF - 1:TCF]
                                   .rearrange("p a b -> p (a b)"))
                            e.drain()
                            e.tt(HC2[:, :, :], H2[:, :, :], BCB_C[:, n_:n_ + 2, :],
                                 OP.mult)
                            P.ev("v")
                        vmk2 = (EV["v"].h, EV["v"].n)
                        a_free = vmk2

                        @P.on("t")
                        def _(e, vmk2=vmk2, n_=n_, pyp=pyp):
                            e.raw.wait_ge(*vmk2)
                            if n_ == 0:
                                P.bwait(e, 2)
                            e.mm(pyp[:, 0:TCF], IDH[:], HC2[:, 0, :],
                                 start=(n_ == 0), stop=False)
                            ins = e.mm(pyp[:, 0:TCF], IDH[:], HC2[:, 1, :],
                                       start=False, stop=(n_ + 2 == DS))
                            P.ev("t", ins)
                    tmk2 = (EV["t"].h, EV["t"].n)

                    @P.on("v")
                    def _(e, tmk2=tmk2, it_=it_, pyp=pyp):
                        e.raw.wait_ge(*tmk2)
                        e.stt(SYV, XS[:, it_, :], VEC2[:, 72 + it_:73 + it_],
                              pyp[:, 0:TCF], OP.mult, OP.add)
                        P.ev("v")
                    P.bfree(2, key="v")
                    vmk3 = (EV["v"].h, EV["v"].n)

                    @P.on("s")
                    def _(e, vmk3=vmk3, it_=it_):
                        e.raw.wait_ge(*vmk3)
                        P.ev("s", e.act(SZ, ZB[:, it_, :], AF.Silu))
                    smk3 = (EV["s"].h, EV["s"].n)

                    @P.on("v")
                    def _(e, smk3=smk3, it_=it_):
                        e.raw.wait_ge(*smk3)
                        e.tt(YBUF[:, it_, :], SYV, SZ, OP.mult)
                        P.ev("v")
                WOK = r3(bf(PC[:, 4096:5120]), 1024)
                ymk = (EV["v"].h, EV["v"].n)

                @P.on("t")
                def _(e):
                    for b in range(8):
                        P.bwait(e, b)
                for kk in range(IT):
                    if kk % 2 == 0:
                        tprev = (EV["t"].h, EV["t"].n)

                        @P.on("sy")
                        def _(e, tprev=tprev):
                            e.raw.wait_ge(*tprev)
                        for j in range(2):
                            ld(WOK[:, j, :],
                               Hn["outpn"][0:1024, (kk + j) * 128:(kk + j + 1) * 128],
                               transpose=True)
                        ldmk3 = ldm()

                        @P.on("t")
                        def _(e, ldmk3=ldmk3, ymk=ymk):
                            e.raw.wait_ge(*ldmk3)
                            if kk == 0:
                                e.raw.wait_ge(*ymk)

                    @P.on("t")
                    def _(e, kk=kk):
                        for dk in range(DT):
                            ins = e.mm(PS[dk][:, 0:TCF],
                                       WOK[:, kk % 2, dk * 128:(dk + 1) * 128],
                                       YBUF[:, kk, :], start=(kk == 0),
                                       stop=(kk == IT - 1))
                        if kk % 2 == 1:
                            P.ev("t", ins)
                tmk3 = (EV["t"].h, EV["t"].n)
                for dk in range(DT):
                    @P.on("v")
                    def _(e, tmk3=tmk3, dk=dk, tseg=tseg):
                        e.raw.wait_ge(*tmk3)
                        xs2 = X[:, dk, tseg * TCF:(tseg + 1) * TCF]
                        e.tt(xs2, xs2, PS[dk][:, 0:TCF], OP.add)
                        P.ev("v")
                    P.bfree(dk, key="v")
                prev_ymk = ymk
                brow_free = bc_marks
            P.barrier()

        sub_ffn0()
        if stages >= 2:
            sub_rwkv()
        if stages >= 3:
            sub_ffn0()
        if stages >= 4:
            sub_qffn(0)
        if stages >= 5:
            sub_mamba()
        if stages >= 6:
            sub_qffn(1)

        prev_store = [None]
        for tt in range(T // 128):
            for dk in range(DT):
                pe = PS[dk % 2]

                @P.on("t")
                def _(e, pe=pe, dk=dk, tt=tt):
                    P.wt(e, "v")
                    P.bwait(e, dk % 2)
                    P.ev("t", e.transpose(pe[:, 0:128],
                                          X[:, dk, tt * 128:(tt + 1) * 128], IDF[:]))
                tmk = EV["t"].n

                @P.on("s")
                def _(e, tmk=tmk, dk=dk, pe=pe):
                    e.raw.wait_ge(EV["t"].h, tmk)
                    if prev_store[0] is not None:
                        e.raw.wait_ge(*prev_store[0])
                    P.ev("s", e.scopy(PC[:, dk * 128:(dk + 1) * 128], pe[:, 0:128]))
                P.bfree(dk % 2)
            smk = EV["s"].n

            @P.on("g")
            def _(e, smk=smk, tt=tt):
                e.raw.wait_ge(EV["s"].h, smk)
                gdma(e, xout[tt * 128:(tt + 1) * 128, :], PC[:, 0:1024])
            prev_store[0] = gdm()

        @P.on("g")
        def _(e):
            for h_, v_ in DG.all_marks():
                if v_:
                    e.raw.wait_ge(h_, v_)

    st.close()
    names_in = ["x_in"] + list(wshapes.keys())
    return nc, names_in


def make_in_maps(inputs, T=2048, ncores=NCORES):
    x = np.ascontiguousarray(np.asarray(inputs["L0_attn_x"], dtype=np.float32))
    in_maps = []
    for c in range(ncores):
        m = {"x_in": np.ascontiguousarray(x[c][:T])}
        for k, v in inputs.items():
            if k == "L0_attn_x":
                continue
            v = np.asarray(v, dtype=np.float32)
            if k == "L0_attn_time_shift":
                v = v.reshape(1, D)
            elif k == "L1_attn_conv_w":
                v = v.reshape(DI, 4)
            elif v.ndim == 1:
                v = v.reshape(1, -1)
            m[k] = np.ascontiguousarray(v)
        in_maps.append(m)
    return in_maps


def kernel(**inputs):
    T = 2048
    nc, names_in = build_core_program(T=T)
    in_maps = make_in_maps(inputs, T=T)
    res = run_bass_kernel_spmd(nc, in_maps, core_ids=list(range(NCORES)))
    out = np.stack([np.asarray(r["x_out"]).reshape(T, D) for r in res.results])
    return out.astype(np.float32)


# version 12 (final)
# speedup vs baseline: 2.0019x; 1.0426x over previous
import numpy as np
from contextlib import ExitStack

import concourse.bass as bass
import concourse.mybir as mybir
from concourse.alu_op_type import AluOpType as OP
from concourse.bass_utils import run_bass_kernel_spmd

F32 = mybir.dt.float32
BF16 = mybir.dt.bfloat16
FP16 = mybir.dt.float16
I16 = mybir.dt.int16
I32 = mybir.dt.int32
AF = mybir.ActivationFunctionType
AX = mybir.AxisListType

D = 1024
DT = D // 128
FF0 = 4096
FT0 = FF0 // 128
DI = 2048
IT = DI // 128
DS = 16
DTRANK = 64
FF1 = 3072
FT1 = FF1 // 128
QMAX = 32767.0
NCORES = 8
LN_EPS = 1e-5

ENG_NAMES = {"t": "tensor", "v": "vector", "s": "scalar", "g": "gpsimd", "sy": "sync"}


class Sem:
    def __init__(self, st, nc, name):
        self.h = st.enter_context(nc.semaphore(name))
        self.n = 0
        self.thr = 0

    def inc(self, instr, k=1):
        instr.then_inc(self.h, k)
        self.n += k
        return self.n

    def throttle(self, raw, limit=192):
        if self.n - self.thr >= limit:
            raw.wait_ge(self.h, self.n)
            self.thr = self.n


class SemPool:

    def __init__(self, st, nc, name, n):
        self.sems = [Sem(st, nc, "%s%d" % (name, i)) for i in range(n)]
        self.i = 0
        self.open = False
        self.maps = []

    def cur(self, raw):
        sm = self.sems[self.i]
        if not self.open and sm.n > sm.thr:
            raw.wait_ge(sm.h, sm.n)
            sm.thr = sm.n
        self.open = True
        return sm

    def mark(self):
        sm = self.sems[self.i]
        m = (sm.h, sm.n)
        if self.open:
            for ent in self.maps:
                for j, (n2, l2, h2, sh, sv) in enumerate(ent):
                    if sh is sm.h and sv < sm.n:
                        ent[j] = (n2, l2, h2, sh, sm.n)
            self.i = (self.i + 1) % len(self.sems)
            self.open = False
        return m

    def all_marks(self):
        return [(sm.h, sm.n) for sm in self.sems]


class EngQ:
    def __init__(self, raw, name):
        self.raw = raw
        self.name = name
        self.dw = []
        self.dr = []

    @staticmethod
    def _rng(ap):
        t = getattr(ap, "tensor", None)
        name = getattr(t, "name", None) if t is not None else None
        if name is None:
            return None
        try:
            lo = ap.offset * 4
        except Exception:
            return (name, 0, 1 << 60)
        try:
            sz = 1
            for st_, n_ in ap.ap:
                pass
            spans = [(st_ * (n_ - 1)) for st_, n_ in list(ap.ap)[1:]]
            ext = sum(spans) + 1
        except Exception:
            ext = 1 << 40
        try:
            import concourse.mybir as _mb
            esz = {F32: 4, I32: 4, BF16: 2, FP16: 2, I16: 2}.get(ap.dtype, 4)
        except Exception:
            esz = 4
        lo = ap.offset * esz
        return (name, lo, lo + ext * esz)

    @staticmethod
    def _ov(a, b):
        return a[0] == b[0] and a[1] < b[2] and b[1] < a[2]

    def _pre(self, outs, ins):
        og = [r for r in (self._rng(a) for a in outs) if r is not None]
        ig = [r for r in (self._rng(a) for a in ins) if r is not None]
        conflict = False
        for o in og:
            for d in self.dw:
                if self._ov(o, d):
                    conflict = True
                    break
            if conflict:
                break
            for d in self.dr:
                if self._ov(o, d):
                    conflict = True
                    break
            if conflict:
                break
        if not conflict:
            for i_ in ig:
                for d in self.dw:
                    if self._ov(i_, d):
                        conflict = True
                        break
                if conflict:
                    break
        if conflict:
            self.raw.drain()
            self.dw = []
            self.dr = []
        self.dw += og
        self.dr += ig

    def drain(self):
        self.raw.drain()
        self.dw = []
        self.dr = []

    def clear(self):
        self.dw = []
        self.dr = []

    def tt(self, out, a, bb, op):
        self._pre([out], [a, bb])
        return self.raw.tensor_tensor(out, a, bb, op)

    def ts(self, out, a, s1, s2, op0, op1):
        self._pre([out], [a, s1, s2])
        return self.raw.tensor_scalar(out, a, s1, s2, op0, op1)

    def tss(self, out, a, s, op):
        self._pre([out], [a])
        return self.raw.tensor_single_scalar(out, a, s, op)

    def stt(self, out, a, s, bb, op0, op1):
        self._pre([out], [a, bb, s])
        return self.raw.scalar_tensor_tensor(out, a, s, bb, op0, op1)

    def scan(self, out, d0, d1, init, op0, op1):
        self._pre([out], [d0, d1, init])
        return self.raw.tensor_tensor_scan(out, d0, d1, init, op0, op1)

    def red(self, out, a, op, absval=False):
        self._pre([out], [a])
        return self.raw.tensor_reduce(out, a, AX.X, op, apply_absolute_value=absval)

    def copy(self, out, a):
        self._pre([out], [a])
        return self.raw.tensor_copy(out, a)

    def memset(self, out, v):
        self._pre([out], [])
        return self.raw.memset(out, v)

    def recip(self, out, a):
        self._pre([out], [a])
        return self.raw.reciprocal(out, a)

    def act(self, out, a, func, bias=0.0, scale=1.0):
        self._pre([out], [a, bias, scale])
        return self.raw.activation(out, a, func, bias=bias, scale=scale)

    def scopy(self, out, a):
        self._pre([out], [a])
        return self.raw.copy(out, a)

    def mm(self, out, lhsT, rhs, start, stop):
        return self.raw.matmul(out, lhsT, rhs, start=start, stop=stop)

    def transpose(self, out, a, ident):
        return self.raw.transpose(out, a, ident)

    def iota(self, out, pattern, base=0, cm=0):
        self._pre([out], [])
        return self.raw.iota(out, pattern, base=base, channel_multiplier=cm,
                             allow_small_or_imprecise_dtypes=True)

    def dma(self, out, in_, **kw):
        self._pre([], [in_])
        return self.raw.dma_start(out=out, in_=in_, **kw)

    def collective(self, kind, op, groups, ins, outs):
        return self.raw.collective_compute(kind, op, replica_groups=groups,
                                           ins=ins, outs=outs)


class Prog:
    def __init__(self, nc, st, blk, T):
        self.nc = nc
        self.st = st
        self.blk = blk
        self.T = T
        self.q = {k: EngQ(None, k) for k in ENG_NAMES}
        self.DL = SemPool(st, nc, "DL", 12)
        self.DG = SemPool(st, nc, "DG", 8)
        self.CC = Sem(st, nc, "CCs")
        self.EV = {e: Sem(st, nc, "ev_" + e) for e in ("t", "v", "s", "g")}
        self.bank = [None] * 8
        self.lnscr = None

    def bwait(self, e, b):
        m = self.bank[b]
        if m is not None:
            e.raw.wait_ge(*m)

    def bfree(self, b, key="s"):
        self.bank[b] = (self.EV[key].h, self.EV[key].n)

    def on(self, key):
        dec = getattr(self.blk, ENG_NAMES[key])
        q = self.q[key]

        def wrap(fn):
            def body(raw):
                q.raw = raw
                fn(q)
            dec(body)
        return wrap

    def ev(self, key, instr=None, k=1):
        s = self.EV[key]
        if instr is None:
            self.q[key].raw.maybe_drain_then_inc((s.h, k))
            s.n += k
        else:
            instr.then_inc(s.h, k)
            s.n += k
        s.throttle(self.q[key].raw)
        return s.n

    def wt(self, e, key):
        e.raw.wait_ge(self.EV[key].h, self.EV[key].n)

    def barrier(self):
        for k in ("t", "v", "s", "g"):
            @self.on(k)
            def _(e):
                self.ev(k)
        marks = {k: self.EV[k].n for k in self.EV}
        dlm = self.DL.all_marks()
        dgm = self.DG.all_marks()
        for k in ("t", "v", "s", "g", "sy"):
            @self.on(k)
            def _(e):
                for j in ("t", "v", "s", "g"):
                    e.raw.wait_ge(self.EV[j].h, marks[j])
                for h, v in dlm + dgm:
                    if v:
                        e.raw.wait_ge(h, v)
                e.clear()
        for s_ in self.EV.values():
            s_.thr = s_.n




def build_core_program(T=2048, TCF=512, TCW=256, group=None, stages=6, dbgq=False, rmode="int"):
    if group is None:
        group = [list(range(NCORES))]
    nc = bass.Bass("TRN2", target_bir_lowering=False, debug=False)
    st = ExitStack()
    NCF = T // TCF
    NCW = T // TCW

    xin = nc.dram_tensor("x_in", [T, D], F32, kind="ExternalInput").ap()
    wshapes = {
        "L0_attn_time_decay": (1, D), "L0_attn_time_first": (1, D),
        "L0_attn_time_shift": (1, D),
        "L0_attn_Wk": (D, D), "L0_attn_Wv": (D, D), "L0_attn_Wr": (D, D),
        "L0_attn_Wck": (D, FF0), "L0_attn_Wcv": (FF0, D), "L0_attn_Wcr": (D, D),
        "L0_attn_ln1_g": (1, D), "L0_attn_ln1_b": (1, D),
        "L0_attn_ln2_g": (1, D), "L0_attn_ln2_b": (1, D),
        "L0_ffn_W1": (FF0, D), "L0_ffn_b1": (1, FF0), "L0_ffn_W2": (D, FF0),
        "L0_ffn_b2": (1, D),
        "L1_attn_in_proj_w": (2 * DI, D), "L1_attn_conv_w": (DI, 4),
        "L1_attn_conv_b": (1, DI),
        "L1_attn_x_proj_w": (DTRANK + 2 * DS, DI),
        "L1_attn_dt_proj_w": (DI, DTRANK),
        "L1_attn_dt_proj_b": (1, DI), "L1_attn_D_skip": (1, DI),
        "L1_attn_A_log": (DI, DS),
        "L1_attn_out_proj_w": (D, DI),
        "L1_ffn_w_gate": (FF1, D), "L1_ffn_w_up": (FF1, D),
        "L1_ffn_w_down": (D, FF1),
    }
    W = {k: nc.dram_tensor(k, list(sh), F32, kind="ExternalInput").ap()
         for k, sh in wshapes.items()}
    xout = nc.dram_tensor("x_out", [T, D], F32, kind="ExternalOutput").ap()

    Hn = {}
    for k, sh in [("w1n", (FF0, D)), ("w2n", (D, FF0)), ("wkn", (D, D)),
                  ("wvn", (D, D)), ("wrn", (D, D)), ("wckn", (D, FF0)),
                  ("wcvn", (FF0, D)), ("wcrn", (D, D)), ("inpn", (2 * DI, D)),
                  ("outpn", (D, DI)), ("wgn", (FF1, D)), ("wun", (FF1, D)),
                  ("wdn", (D, FF1)), ("xpn", (DTRANK + 2 * DS, DI)),
                  ("dtpn", (DI, 128))]:
        Hn[k] = nc.dram_tensor(k, list(sh), FP16 if k == "dtpn" else BF16).ap()
    Hn["upraw"] = nc.dram_tensor("upraw", [FF1, T], BF16).ap()
    Hn["gateraw"] = nc.dram_tensor("gateraw", [FF1, T], BF16).ap()
    Hn["g2raw"] = nc.dram_tensor("g2raw", [FF1, T], BF16).ap()
    Hn["qup"] = nc.dram_tensor("qup", [FF1, T], I16).ap()
    Hn["bounce"] = nc.dram_tensor("bounceb", [64, 64], F32).ap()
    Hn["rowbB"] = nc.dram_tensor("rowbB", [DS, T], FP16).ap()
    Hn["rowbC"] = nc.dram_tensor("rowbC", [DS, T], FP16).ap()
    Hn["ccin"] = nc.dram_tensor("ccin", [1, 4], F32).ap()
    DBGQ = [nc.dram_tensor("dbgq", [D, T], BF16, kind="ExternalOutput").ap() if dbgq else None]
    Hn["ccout"] = nc.dram_tensor("ccout", [1, 4], F32).ap()

    def sb(name, shape, dt):
        return st.enter_context(nc.sbuf_tensor(name, shape, dt))

    X = sb("X", [128, DT, T], F32)
    PA = sb("PA", [128, 14336], F32)
    PB = sb("PB", [128, 8192], F32)
    PC = sb("PC", [128, 6144], F32)
    LNB = sb("LNB", [128, DT, TCF + 1], BF16)
    MREP = sb("MREP", [128, TCF], F32)
    VREP = sb("VREP", [128, TCF], F32)
    VEC = sb("VEC", [128, 56], F32)
    VEC2 = sb("VEC2", [128, 96], F32)
    CONVW = sb("CONVW", [128, 4 * IT], F32)
    GS = sb("GS", [128, 16], F32)
    GS1 = sb("GS1", [1, 160], F32)
    ONESD = sb("ONESD", [128, 128], BF16)
    IDF = sb("IDF", [128, 128], F32)
    IDH = sb("IDH", [128, 128], FP16)
    IOTA_P = sb("IOTA_P", [128, 1], F32)
    NSCALE = None
    CONSTS = sb("CONSTS", [128, 4], F32)
    ASB = sb("ASB", [128, IT * DS], F32)
    XPT = sb("XPT", [128, IT, 320], FP16)
    DTPT = sb("DTPT", [128, DI], FP16)
    CARRY = sb("CARRY", [128, IT, DS], F32)
    WCAR = sb("WCAR", [128, 16], F32)
    XHALO = sb("XHALO", [128, IT, 3], F32)
    SM = sb("SM", [128, 128], F32)
    WKS = sb("WKS", [128, 40], F32)

    PS = [st.enter_context(nc.psum_tensor("pp%d" % i, [128, 512], F32))
          for i in range(8)]

    def bf(ap):
        return ap.bitcast(BF16)

    def hf(ap):
        return ap.bitcast(FP16)

    def r3(ap, inner):
        return ap.rearrange("p (a b) -> p a b", b=inner)

    with nc.Block() as blk:
        P = Prog(nc, st, blk, T)
        DL, DG, CC, EV = P.DL, P.DG, P.CC, P.EV

        ld_map = []
        gd_map = []
        DL.maps.append(ld_map)
        DG.maps.append(gd_map)

        def _span(ap):
            t = getattr(ap, "tensor", None)
            name = getattr(t, "name", None)
            try:
                lo = ap.offset
                hi = lo + max(ap.free_nbytes(), 1)
            except Exception:
                lo, hi = 0, 1 << 60
            return name, lo, hi

        def _waw(e, mp, out, sem_of_cur):
            name, lo, hi = _span(out)
            keep = []
            for (n2, l2, h2, sh, sv) in mp:
                if n2 == name and l2 < hi and h2 > lo:
                    e.raw.wait_ge(sh, sv)
                else:
                    keep.append((n2, l2, h2, sh, sv))
            mp[:] = keep
            return name, lo, hi

        def ld(out, in_, **kw):
            @P.on("sy")
            def _(e):
                name, lo, hi = _span(out)
                for (n2, l2, h2, sh, sv) in ld_map:
                    if n2 == name and l2 < hi and h2 > lo and DL.open and \
                            sh is DL.sems[DL.i].h and sv > DL.sems[DL.i].thr:
                        DL.mark()
                        break
                _waw(e, ld_map, out, None)
                sm = DL.cur(e.raw)
                sm.inc(e.dma(out, in_, **kw), 16)
                ld_map.append((name, lo, hi, sm.h, sm.n))

        def ldm():
            return DL.mark()

        def ld_guard(*evs):
            marks = [(EV[k].h, EV[k].n) for k in evs]

            @P.on("sy")
            def _(e):
                for h_, v_ in marks:
                    e.raw.wait_ge(h_, v_)

        def gdma(e, out, in_, **kw):
            name, lo, hi = _span(out)
            for (n2, l2, h2, sh, sv) in gd_map:
                if n2 == name and l2 < hi and h2 > lo and DG.open and \
                        sh is DG.sems[DG.i].h and sv > DG.sems[DG.i].thr:
                    DG.mark()
                    break
            _waw(e, gd_map, out, None)
            sm = DG.cur(e.raw)
            sm.inc(e.dma(out, in_, **kw), 16)
            gd_map.append((name, lo, hi, sm.h, sm.n))

        def gdm():
            return DG.mark()

        def bcast_scalar(src_1x1, gs_col, after_ev="v"):
            @P.on("g")
            def _(e):
                e.raw.wait_ge(EV[after_ev].h, EV[after_ev].n)
                gdma(e, Hn["bounce"][0:1, gs_col:gs_col + 1], src_1x1)
                m = gdm()
                e.raw.wait_ge(*m)
                gdma(e, GS[:, gs_col:gs_col + 1],
                     Hn["bounce"][0:1, gs_col:gs_col + 1].broadcast_to([128, 1]))
            return gdm()

        def part_gather_reduce(src_128x1, dst_col, op):
            @P.on("g")
            def _(e):
                e.raw.wait_ge(EV["v"].h, EV["v"].n)
                gdma(e, GS1[0:1, 0:128], src_128x1,
                     allow_slow_non_contiguous=True)
            dgmk = gdm()

            @P.on("v")
            def _(e):
                e.raw.wait_ge(*dgmk)
                e.red(GS1[0:1, dst_col:dst_col + 1], GS1[0:1, 0:128], op)
                P.ev("v")

        @P.on("g")
        def _(e):
            e.iota(IOTA_P[:], [[0, 1]], base=0, cm=1)
            e.iota(PA[:, 0:128], [[1, 128]], base=0, cm=0)
            P.ev("g")

        @P.on("v")
        def _(e):
            e.raw.wait_ge(EV["g"].h, EV["g"].n)
            e.ts(IDF[:], PA[:, 0:128], IOTA_P[:, 0:1], 0.0, OP.is_equal, OP.add)
            e.copy(IDH[:], IDF[:])
            e.memset(ONESD[:], 1.0 / D)
            e.memset(CONSTS[:, 0:1], LN_EPS)
            e.memset(CONSTS[:, 1:2], 65536.0)
            e.memset(CONSTS[:, 2:3], 0.5)
            e.memset(CONSTS[:, 3:4], 1.0)
            P.ev("v")

        vecmap = [("L0_attn_time_decay", VEC, 0, DT), ("L0_attn_time_first", VEC, 8, DT),
                  ("L0_attn_ln1_g", VEC, 16, DT), ("L0_attn_ln1_b", VEC, 24, DT),
                  ("L0_attn_ln2_g", VEC, 32, DT), ("L0_attn_ln2_b", VEC, 40, DT),
                  ("L0_attn_time_shift", VEC, 48, DT),
                  ("L0_ffn_b1", VEC2, 0, FT0), ("L0_ffn_b2", VEC2, 32, DT),
                  ("L1_attn_conv_b", VEC2, 40, IT), ("L1_attn_dt_proj_b", VEC2, 56, IT),
                  ("L1_attn_D_skip", VEC2, 72, IT)]
        for key, dstt, c0, nt in vecmap:
            ld(dstt[:, c0:c0 + nt], W[key].rearrange("o (j p) -> (o p) j", p=128),
               allow_slow_non_contiguous=True)
        for j in range(4):
            ld(CONVW[:, j * IT:(j + 1) * IT],
               W["L1_attn_conv_w"][:, j:j + 1].rearrange("(j p) o -> p (j o)", p=128),
               allow_slow_non_contiguous=True)
        ldv = ldm()

        @P.on("v")
        def _(e):
            e.raw.wait_ge(*ldv)
            e.ts(VEC2[:, 88:96], VEC2[:, 32:40], 0.5, 0.0, OP.mult, OP.add)
            P.ev("v")

        for ik in range(IT):
            ld_guard("v", "s")
            ld(SM[:, 16:32], W["L1_attn_A_log"][ik * 128:(ik + 1) * 128, :])
            mark = ldm()

            @P.on("s")
            def _(e, mark=mark):
                e.raw.wait_ge(*mark)
                P.wt(e, "v")
                P.ev("s", e.act(SM[:, 32:48], SM[:, 16:32], AF.Exp))
            smk = EV["s"].n

            @P.on("v")
            def _(e, smk=smk, ik=ik):
                e.raw.wait_ge(EV["s"].h, smk)
                e.ts(ASB[:, ik * DS:(ik + 1) * DS], SM[:, 32:48], -1.0, 0.0,
                     OP.mult, OP.add)
                P.ev("v")

        tr_tmk = [None, None]
        tr_smk = [None]
        for tt in range(T // 128):
            buf = PA[:, 1024:2048] if tt % 2 == 0 else PA[:, 2048:3072]
            pm = tr_tmk[tt % 2]
            if pm is not None:
                @P.on("sy")
                def _(e, pm=pm):
                    e.raw.wait_ge(*pm)
            ld(buf, xin[tt * 128:(tt + 1) * 128, :])
            mark = ldm()
            sm_prev = tr_smk[0]

            @P.on("t")
            def _(e, buf=buf, mark=mark, sm_prev=sm_prev, tt=tt):
                e.raw.wait_ge(*mark)
                if tt == 0:
                    e.raw.wait_ge(EV["v"].h, EV["v"].n)
                if sm_prev is not None:
                    e.raw.wait_ge(*sm_prev)
                for dk in range(DT):
                    ins = e.transpose(PS[dk][:, 0:128],
                                      buf[:, dk * 128:(dk + 1) * 128], IDF[:])
                P.ev("t", ins)
            tr_tmk[tt % 2] = (EV["t"].h, EV["t"].n)
            tmk = (EV["t"].h, EV["t"].n)

            @P.on("s")
            def _(e, tmk=tmk, tt=tt):
                e.raw.wait_ge(*tmk)
                for dk in range(DT):
                    ins = e.scopy(X[:, dk, tt * 128:(tt + 1) * 128],
                                  PS[dk][:, 0:128])
                P.ev("s", ins)
            tr_smk[0] = (EV["s"].h, EV["s"].n)

        P.barrier()

        PSETS = [
            (PA[:, 0:2048], PA[:, 2048:4096], PA[:, 4096:6144],
             bf(PA[:, 6144:7168])[:, 0:2048]),
            (PA[:, 7168:9216], PA[:, 9216:11264], PA[:, 11264:13312],
             bf(PA[:, 13312:14336])[:, 0:2048]),
        ]
        SCR = PSETS[0][0]
        SCR2 = PSETS[0][1]
        SCR3 = PSETS[0][2]
        WPB = PSETS[0][3]
        WPH = hf(PA[:, 6144:6208])[:, 0:128]
        pp_idx = [0]
        pp_cmk = [None, None]
        pp_gmk = [None, None]

        def pp_next():
            i = pp_idx[0] % 2
            pp_idx[0] += 1
            pm = pp_cmk[i]
            if pm is not None:
                @P.on("sy")
                def _(e, pm=pm):
                    e.raw.wait_ge(*pm)
            return i

        def cast_slice(key, hkey, r0, c0, cols, rr=128):
            i = pp_next()
            scr, _, _, wpb = PSETS[i]
            ld(scr[0:rr, 0:cols], W[key][r0:r0 + rr, c0:c0 + cols])
            mark = ldm()
            gm = pp_gmk[i]

            @P.on("s")
            def _(e, mark=mark, gm=gm, scr=scr, wpb=wpb, rr=rr, cols=cols):
                e.raw.wait_ge(*mark)
                if gm is not None:
                    e.raw.wait_ge(*gm)
                P.ev("s", e.act(wpb[0:rr, 0:cols], scr[0:rr, 0:cols], AF.Identity))
            smk = (EV["s"].h, EV["s"].n)
            pp_cmk[i] = smk

            @P.on("g")
            def _(e, smk=smk, r0=r0, c0=c0, hkey=hkey, wpb=wpb, rr=rr, cols=cols):
                e.raw.wait_ge(*smk)
                gdma(e, Hn[hkey][r0:r0 + rr, c0:c0 + cols], wpb[0:rr, 0:cols])
            pp_gmk[i] = gdm()

        for key, hkey, rows, cols in [
                ("L0_attn_Wk", "wkn", D, D), ("L0_attn_Wv", "wvn", D, D),
                ("L0_attn_Wr", "wrn", D, D), ("L0_attn_Wcr", "wcrn", D, D),
                ("L1_attn_in_proj_w", "inpn", 2 * DI, D),
                ("L1_attn_out_proj_w", "outpn", D, DI),
                ("L1_attn_x_proj_w", "xpn", DTRANK + 2 * DS, DI),
                ("L0_attn_Wck", "wckn", D, FF0), ("L0_attn_Wcv", "wcvn", FF0, D)]:
            for r0 in range(0, rows, 128):
                rr = min(128, rows - r0)
                for c0 in range(0, cols, 2048):
                    cw = min(2048, cols - c0)
                    cast_slice(key, hkey, r0, c0, cw, rr=rr)
        P.barrier()

        WPHS = [hf(PA[:, 6144:6208])[:, 0:128], hf(PA[:, 13312:13376])[:, 0:128]]
        for r0 in range(0, DI, 128):
            i = pp_next()
            scr = PSETS[i][0]
            wph = WPHS[i]
            ld(scr[:, 0:DTRANK], W["L1_attn_dt_proj_w"][r0:r0 + 128, :])
            mark = ldm()
            gm = pp_gmk[i]

            @P.on("s")
            def _(e, mark=mark, gm=gm, scr=scr, wph=wph):
                e.raw.wait_ge(*mark)
                if gm is not None:
                    e.raw.wait_ge(*gm)
                e.act(wph[:, 0:DTRANK], scr[:, 0:DTRANK], AF.Identity)
                e._pre([wph], [])
                e.raw.memzero(wph[:, DTRANK:128])
                P.ev("s")
            smk = EV["s"].n
            pp_cmk[i] = (EV["s"].h, smk)

            @P.on("g")
            def _(e, smk=smk, r0=r0, wph=wph):
                e.raw.wait_ge(EV["s"].h, smk)
                gdma(e, Hn["dtpn"][r0:r0 + 128, 0:128], wph[:, 0:128])
            pp_gmk[i] = gdm()
        def ternary(key, hkey, rows, cols, gs_col):
            colw = min(cols, 2048)

            @P.on("v")
            def _(e):
                e.memset(SM[:, 8:9], 0.0)
                P.ev("v")
            for r0 in range(0, rows, 128):
                for c0 in range(0, cols, colw):
                    i = pp_next()
                    scr = PSETS[i][0]
                    ld(scr[:, 0:colw], W[key][r0:r0 + 128, c0:c0 + colw])
                    mark = ldm()

                    @P.on("v")
                    def _(e, mark=mark, scr=scr):
                        e.raw.wait_ge(*mark)
                        e.red(SM[:, 9:10], scr[:, 0:colw], OP.add, absval=True)
                        e.tt(SM[:, 8:9], SM[:, 8:9], SM[:, 9:10], OP.add)
                        P.ev("v")
                    pp_cmk[i] = (EV["v"].h, EV["v"].n)
            part_gather_reduce(SM[:, 8:9], 128, OP.add)

            @P.on("v")
            def _(e):
                e.raw.wait_ge(EV["v"].h, EV["v"].n)
                e.ts(GS1[0:1, 129:130], GS1[0:1, 128:129], 1.0 / (rows * cols),
                     1e-8, OP.mult, OP.max)
                e.recip(GS1[0:1, 130:131], GS1[0:1, 129:130])
                P.ev("v")
            dgm2 = bcast_scalar(GS1[0:1, 130:131], gs_col)
            for r0 in range(0, rows, 128):
                for c0 in range(0, cols, colw):
                    i = pp_next()
                    scr, scr2, scr3, wpb = PSETS[i]
                    ld(scr[:, 0:colw], W[key][r0:r0 + 128, c0:c0 + colw])
                    mark = ldm()
                    gm = pp_gmk[i]
                    cm = pp_cmk[i]

                    @P.on("s")
                    def _(e, mark=mark, cm=cm, scr=scr, scr2=scr2):
                        e.raw.wait_ge(*mark)
                        if cm is not None:
                            e.raw.wait_ge(*cm)
                        e.raw.wait_ge(*dgm2)
                        P.ev("s", e.act(scr2[:, 0:colw], scr[:, 0:colw], AF.Identity,
                                        scale=GS[:, gs_col:gs_col + 1]))
                    smk = EV["s"].n

                    @P.on("v")
                    def _(e, smk=smk, gm=gm, scr2=scr2, scr3=scr3, wpb=wpb):
                        e.raw.wait_ge(EV["s"].h, smk)
                        if gm is not None:
                            e.raw.wait_ge(*gm)
                        if rmode == "int":
                            e.copy(scr3[:, 0:colw].bitcast(I16)[:, 0:colw],
                                   scr2[:, 0:colw])
                            e.drain()
                            e.ts(wpb[:, 0:colw],
                                 scr3[:, 0:colw].bitcast(I16)[:, 0:colw],
                                 1.0, -1.0, OP.min, OP.max)
                        else:
                            e.ts(scr2[:, 0:colw], scr2[:, 0:colw], 0.5, 0.0,
                                 OP.add, OP.add)
                            e.drain()
                            e.tss(scr3[:, 0:colw], scr2[:, 0:colw], 1.0, OP.mod)
                            e.drain()
                            e.tt(scr2[:, 0:colw], scr2[:, 0:colw], scr3[:, 0:colw],
                                 OP.subtract)
                            e.drain()
                            e.ts(wpb[:, 0:colw], scr2[:, 0:colw], 1.0, -1.0,
                                 OP.min, OP.max)
                        P.ev("v")
                    vmk = EV["v"].n
                    pp_cmk[i] = (EV["v"].h, vmk)

                    @P.on("g")
                    def _(e, vmk=vmk, r0=r0, c0=c0, wpb=wpb):
                        e.raw.wait_ge(EV["v"].h, vmk)
                        gdma(e, Hn[hkey][r0:r0 + 128, c0:c0 + colw],
                                     wpb[:, 0:colw])
                    pp_gmk[i] = gdm()

        ternary("L0_ffn_W1", "w1n", FF0, D, 0)
        ternary("L0_ffn_W2", "w2n", D, FF0, 1)
        P.barrier()

        def lpbq(key, hkey, rows, cols):
            colw = min(cols, 2048)
            for r0 in range(0, rows, 128):
                for c0 in range(0, cols, colw):
                    cw = min(colw, cols - c0)
                    nb = cw // 32
                    i = pp_next()
                    scr, scr2, scr3, wpb = PSETS[i]
                    ld(scr[:, 0:cw], W[key][r0:r0 + 128, c0:c0 + cw])
                    mark = ldm()
                    gm = pp_gmk[i]
                    cm = pp_cmk[i]

                    @P.on("s")
                    def _(e, mark=mark, cm=cm, scr=scr, scr2=scr2):
                        e.raw.wait_ge(*mark)
                        if cm is not None:
                            e.raw.wait_ge(*cm)
                        P.ev("s", e.act(scr2[:, 0:cw], scr[:, 0:cw], AF.Abs))
                    smk = EV["s"].n

                    @P.on("v")
                    def _(e, smk=smk, gm=gm, scr=scr, scr2=scr2, scr3=scr3,
                          wpb=wpb):
                        e.raw.wait_ge(EV["s"].h, smk)
                        if gm is not None:
                            e.raw.wait_ge(*gm)
                        e.red(SM[:, 0:nb], r3(scr2[:, 0:cw], 32), OP.max)
                        e.ts(SM[:, 64:64 + nb], SM[:, 0:nb], 1.0 / 7.0, 1e-12,
                             OP.mult, OP.max)
                        e.recip(SM[:, 0:nb], SM[:, 64:64 + nb])
                        rsb = SM[:, 0:nb].rearrange("p (b o) -> p b o", o=1) \
                            .broadcast_to([128, nb, 32])
                        e.tt(r3(scr2[:, 0:cw], 32), r3(scr[:, 0:cw], 32), rsb,
                             OP.mult)
                        e.drain()
                        if rmode == "int":
                            e.copy(scr3[:, 0:cw].bitcast(I16)[:, 0:cw], scr2[:, 0:cw])
                            e.drain()
                            e.ts(scr2[:, 0:cw], scr3[:, 0:cw].bitcast(I16)[:, 0:cw],
                                 7.0, -8.0, OP.min, OP.max)
                        else:
                            e.ts(scr2[:, 0:cw], scr2[:, 0:cw], 0.5, 0.0, OP.add, OP.add)
                            e.drain()
                            e.tss(scr3[:, 0:cw], scr2[:, 0:cw], 1.0, OP.mod)
                            e.drain()
                            e.tt(scr2[:, 0:cw], scr2[:, 0:cw], scr3[:, 0:cw],
                                 OP.subtract)
                            e.drain()
                            e.ts(scr2[:, 0:cw], scr2[:, 0:cw], 7.0, -8.0, OP.min, OP.max)
                        ssb = SM[:, 64:64 + nb].rearrange("p (b o) -> p b o", o=1) \
                            .broadcast_to([128, nb, 32])
                        e.tt(r3(wpb[:, 0:cw], 32), r3(scr2[:, 0:cw], 32), ssb,
                             OP.mult)
                        P.ev("v")
                    vmk = EV["v"].n
                    pp_cmk[i] = (EV["v"].h, vmk)

                    @P.on("g")
                    def _(e, vmk=vmk, r0=r0, c0=c0, wpb=wpb):
                        e.raw.wait_ge(EV["v"].h, vmk)
                        gdma(e, Hn[hkey][r0:r0 + 128, c0:c0 + cw],
                                     wpb[:, 0:cw])
                    pp_gmk[i] = gdm()

        lpbq("L1_ffn_w_gate", "wgn", FF1, D)
        lpbq("L1_ffn_w_up", "wun", FF1, D)
        lpbq("L1_ffn_w_down", "wdn", D, FF1)
        P.barrier()

        xp_mk = [None, None]
        for ik in range(IT):
            i = ik % 2
            wpb = PSETS[i][3]
            pm = xp_mk[i] if xp_mk[i] is not None else pp_gmk[i]
            if pm is not None:
                @P.on("sy")
                def _(e, pm=pm):
                    e.raw.wait_ge(*pm)
            ld(wpb[:, 0:96], Hn["xpn"][0:96, ik * 128:(ik + 1) * 128],
               transpose=True)
            mark = ldm()

            @P.on("v")
            def _(e, mark=mark, ik=ik, wpb=wpb):
                e.raw.wait_ge(*mark)
                tile = wpb[:, 0:96]
                e.copy(XPT[:, ik, 0:64], tile[:, 0:64])
                for r in range(8):
                    e.copy(XPT[:, ik, 64 + 16 * r:80 + 16 * r], tile[:, 64:80])
                for r in range(8):
                    e.copy(XPT[:, ik, 192 + 16 * r:208 + 16 * r], tile[:, 80:96])
                P.ev("v")
            xp_mk[i] = (EV["v"].h, EV["v"].n)
        ld(DTPT[:, :], Hn["dtpn"][:, 0:128], transpose=True)
        P.barrier()

        def emit_ln(ts0, TCc, dst, dst_off, g_col=None, b_col=None, store_mv=None):
            XC_ = r3(PC[:, 0:DT * TCc], TCc)
            XBQ = r3(bf(PC[:, 4096:4096 + DT * TCc // 2])[:, 0:DT * TCc], TCc)
            Xsl = X[:, :, ts0:ts0 + TCc]
            pm, pv = PS[6], PS[7]

            @P.on("s")
            def _(e):
                P.bwait(e, 6)
                P.bwait(e, 7)
                if P.lnscr is not None:
                    e.raw.wait_ge(*P.lnscr)
                P.wt(e, "v")
                P.ev("s", e.act(XBQ, Xsl, AF.Identity))
            smk = EV["s"].n

            @P.on("t")
            def _(e):
                e.raw.wait_ge(EV["s"].h, smk)
                for k in range(DT):
                    ins = e.mm(pm[:, 0:TCc], ONESD[:], XBQ[:, k, :],
                               start=(k == 0), stop=(k == DT - 1))
                P.ev("t", ins)
            tmk = EV["t"].n

            @P.on("s")
            def _(e):
                e.raw.wait_ge(EV["t"].h, tmk)
                P.ev("s", e.scopy(MREP[:, 0:TCc], pm[:, 0:TCc]))
            P.bfree(6)
            smk2 = EV["s"].n

            @P.on("v")
            def _(e):
                e.raw.wait_ge(EV["s"].h, smk2)
                mb = MREP[:, 0:TCc].rearrange("p (a b) -> p a b", a=1) \
                    .broadcast_to([128, DT, TCc])
                e.tt(XC_, Xsl, mb, OP.subtract)
                P.ev("v")
            vmk = EV["v"].n

            @P.on("s")
            def _(e):
                e.raw.wait_ge(EV["v"].h, vmk)
                P.ev("s", e.act(XBQ, XC_, AF.Square))
            smk3 = EV["s"].n

            @P.on("t")
            def _(e):
                e.raw.wait_ge(EV["s"].h, smk3)
                for k in range(DT):
                    ins = e.mm(pv[:, 0:TCc], ONESD[:], XBQ[:, k, :],
                               start=(k == 0), stop=(k == DT - 1))
                P.ev("t", ins)
            tmk2 = EV["t"].n
            P.lnscr = (EV["t"].h, EV["t"].n)

            @P.on("s")
            def _(e):
                e.raw.wait_ge(EV["t"].h, tmk2)
                P.ev("s", e.act(MREP[:, 0:TCc], pv[:, 0:TCc], AF.Sqrt, bias=CONSTS[:, 0:1]))
            P.bfree(7)
            smk4 = EV["s"].n

            @P.on("v")
            def _(e):
                e.raw.wait_ge(EV["s"].h, smk4)
                e.recip(VREP[:, 0:TCc], MREP[:, 0:TCc])
                e.drain()
                vb = VREP[:, 0:TCc].rearrange("p (a b) -> p a b", a=1) \
                    .broadcast_to([128, DT, TCc])
                e.tt(dst[:, :, dst_off:dst_off + TCc], XC_, vb, OP.mult)
                if g_col is not None:
                    e.drain()
                    for k in range(DT):
                        e.ts(dst[:, k, dst_off:dst_off + TCc],
                             dst[:, k, dst_off:dst_off + TCc],
                             VEC[:, g_col + k:g_col + k + 1],
                             VEC[:, b_col + k:b_col + k + 1], OP.mult, OP.add)
                if store_mv is not None:
                    mdst, vdst = store_mv
                    e.copy(mdst, MREP[:, 0:TCc])
                    e.copy(vdst, VREP[:, 0:TCc])
                P.ev("v")

        def sub_ffn0():
            W1TH = r3(bf(PA[:, 0:8192]), 2048)
            W2KT = r3(bf(PA[:, 8192:10240]), 1024)
            HB_ = r3(bf(PB[:, 0:8192]), TCF)
            for tseg in range(NCF):
                emit_ln(tseg * TCF, TCF, LNB, 0)
                lnmk = EV["v"].n
                for fh in range(2):
                    tprev = EV["t"].n

                    @P.on("sy")
                    def _(e):
                        e.raw.wait_ge(EV["t"].h, tprev)
                    for k in range(DT):
                        ld(W1TH[:, k, :], Hn["w1n"][fh * 2048:(fh + 1) * 2048,
                                                    k * 128:(k + 1) * 128],
                           transpose=True)
                    ldmk = ldm()
                    for pf in range(16):
                        f = fh * 16 + pf
                        pmx = PS[pf % 4]

                        @P.on("t")
                        def _(e, pmx=pmx, pf=pf, f=f):
                            e.raw.wait_ge(*ldmk)
                            e.raw.wait_ge(EV["v"].h, lnmk)
                            P.wt(e, "s")
                            for k in range(DT):
                                ins = e.mm(pmx[:, 0:TCF],
                                           W1TH[:, k, pf * 128:(pf + 1) * 128],
                                           LNB[:, k, 0:TCF],
                                           start=(k == 0), stop=(k == DT - 1))
                            P.ev("t", ins)
                        tmk = EV["t"].n

                        @P.on("s")
                        def _(e, tmk=tmk, f=f, pmx=pmx):
                            e.raw.wait_ge(EV["t"].h, tmk)
                            P.ev("s", e.act(HB_[:, f, :], pmx[:, 0:TCF], AF.Gelu,
                                            bias=VEC2[:, f:f + 1]))
                smk = EV["s"].n
                w2marks = [None] * 4
                qglob = [0]
                for dk in range(DT):
                    pmo = PS[4 + dk % 2]

                    @P.on("t")
                    def _(e, dk=dk):
                        P.bwait(e, 4 + dk % 2)
                    for kk in range(FT0):
                        if kk % 2 == 0:
                            q = qglob[0]
                            for j in range(2):
                                slj = (q + j) % 4
                                mk = w2marks[slj]

                                @P.on("sy")
                                def _(e, mk=mk):
                                    if mk is not None:
                                        e.raw.wait_ge(*mk)
                                ld(W2KT[:, slj, :],
                                   Hn["w2n"][0:1024, (kk + j) * 128:(kk + j + 1) * 128],
                                   transpose=True)
                            ldmk2 = ldm()

                            @P.on("t")
                            def _(e):
                                e.raw.wait_ge(*ldmk2)
                                e.raw.wait_ge(EV["s"].h, smk)

                        @P.on("t")
                        def _(e, pmo=pmo, dk=dk, kk=kk):
                            sl_ = (qglob[0] + (kk % 2)) % 4
                            ins = e.mm(pmo[:, 0:TCF],
                                       W2KT[:, sl_, dk * 128:(dk + 1) * 128],
                                       HB_[:, kk, :], start=(kk == 0),
                                       stop=(kk == FT0 - 1))
                            if kk % 2 == 1 or kk == FT0 - 1:
                                P.ev("t", ins)
                        if kk % 2 == 1:
                            w2marks[qglob[0] % 4] = (EV["t"].h, EV["t"].n)
                            w2marks[(qglob[0] + 1) % 4] = (EV["t"].h, EV["t"].n)
                            qglob[0] = (qglob[0] + 2) % 4
                    tmk = EV["t"].n

                    @P.on("s")
                    def _(e, tmk=tmk, dk=dk, pmo=pmo):
                        e.raw.wait_ge(EV["t"].h, tmk)
                        P.wt(e, "v")
                        P.ev("s", e.act(MREP[:, 0:TCF], pmo[:, 0:TCF], AF.Identity,
                                        bias=VEC2[:, 88 + dk:88 + dk + 1], scale=0.5))
                    P.bfree(4 + dk % 2)
                    smk2 = EV["s"].n

                    @P.on("v")
                    def _(e, smk2=smk2, dk=dk, tseg=tseg):
                        e.raw.wait_ge(EV["s"].h, smk2)
                        xs = X[:, dk, tseg * TCF:(tseg + 1) * TCF]
                        e.tt(xs, xs, MREP[:, 0:TCF], OP.add)
                        P.ev("v")
            P.barrier()

        def sub_rwkv():
            TCq = TCW
            NCq = NCW
            RSUB = 8
            KB = r3(PA[:, 0:DT * TCq], TCq)
            E2 = r3(PA[:, DT * TCq:2 * DT * TCq], TCq)
            BARR = r3(PA[:, 2 * DT * TCq:2 * DT * TCq + DT * (TCq + 1)], TCq + 1)
            XT = r3(PA[:, 6160:6160 + DT * (TCq + 1)], TCq + 1)
            SGX = r3(hf(PA[:, 8216:8216 + DT * TCq // 2]), TCq)
            RAR = r3(hf(PA[:, 9240:9240 + DT * (TCq + 1) // 2]), TCq + 1)
            VB = r3(hf(PA[:, 10268:10268 + DT * TCq // 2]), TCq)
            YW = r3(PA[:, 11292:11292 + DT * TCq], TCq)
            LNSJ = PA[:, 11292:11292 + (TCq // RSUB) * DT]
            RGB = r3(bf(PC[:, 2048:2048 + DT * TCq // 2]), TCq)
            WKT = r3(bf(PB[:, 0:4096]), 1024)
            WVT = r3(bf(PB[:, 4096:8192]), 1024)
            WRS = r3(bf(PC[:, 5120:6144]), 1024)
            WEXP = SM[:, 120:128]
            NLC = SM[:, 112:120]
            a_, m_, p_, r_, racc = (WKS[:, 0:8], WKS[:, 8:16], WKS[:, 16:24],
                                    WKS[:, 24:32], WKS[:, 32:40])
            for k in range(DT):
                ld(WKT[:, k, :], Hn["wkn"][k * 128:(k + 1) * 128, :])
                ld(WVT[:, k, :], Hn["wvn"][k * 128:(k + 1) * 128, :])
            wldmk = ldm()

            @P.on("s")
            def _(e):
                P.wt(e, "v")
                P.ev("s", e.act(WEXP, VEC[:, 0:8], AF.Exp))

            @P.on("v")
            def _(e):
                P.wt(e, "s")
                for k in range(DT):
                    e.copy(LNB[:, k, 0:1], VEC[:, 48 + k:48 + k + 1])
                e.memset(WCAR[:, 0:8], -1.0e38)
                e.memset(WCAR[:, 8:16], 0.0)
                P.ev("v")

            def dense_mm(wt, dcast, ldmark, lnmk, freemk):
                for dk in range(DT):
                    pmx = PS[dk % 4]

                    @P.on("t")
                    def _(e, pmx=pmx, dk=dk):
                        e.raw.wait_ge(*ldmark)
                        e.raw.wait_ge(*lnmk)
                        if freemk is not None:
                            e.raw.wait_ge(*freemk)
                        P.bwait(e, dk % 4)
                        for k in range(DT):
                            ins = e.mm(pmx[:, 0:TCq], wt[:, k, dk * 128:(dk + 1) * 128],
                                       LNB[:, k, 0:TCq], start=(k == 0),
                                       stop=(k == DT - 1))
                        P.ev("t", ins)
                    tmk = EV["t"].n

                    @P.on("s")
                    def _(e, tmk=tmk, dk=dk, pmx=pmx):
                        e.raw.wait_ge(EV["t"].h, tmk)
                        P.ev("s", dcast(e, dk, pmx))
                    P.bfree(dk % 4)

            kb_free = [None]
            vb_free = [None]
            rgb_free = [None]
            wrs_marks = [None, None]
            for c in range(NCq):
                emit_ln(c * TCq, TCq, LNB, 1, g_col=16, b_col=24)
                lnmk = (EV["v"].h, EV["v"].n)
                dense_mm(WKT, lambda e, dk, pmx: e.scopy(KB[:, dk, 0:TCq],
                                                         pmx[:, 0:TCq]), wldmk,
                         lnmk, kb_free[0])
                dense_mm(WVT, lambda e, dk, pmx: e.scopy(VB[:, dk, 0:TCq],
                                                         pmx[:, 0:TCq]), wldmk,
                         lnmk, vb_free[0])
                accr = [PS[d][:, 0:TCq] for d in range(DT)]

                @P.on("t")
                def _(e):
                    e.raw.wait_ge(*lnmk)
                    if rgb_free[0] is not None:
                        e.raw.wait_ge(*rgb_free[0])
                    for b in range(8):
                        P.bwait(e, b)
                for kk in range(DT):
                    if kk % 2 == 0:
                        for j in range(2):
                            mkj = wrs_marks[j]

                            @P.on("sy")
                            def _(e, mkj=mkj):
                                if mkj is not None:
                                    e.raw.wait_ge(*mkj)
                            ld(WRS[:, j, :],
                               Hn["wrn"][(kk + j) * 128:(kk + j + 1) * 128, :])
                        ldmk3 = ldm()

                        @P.on("t")
                        def _(e, ldmk3=ldmk3):
                            e.raw.wait_ge(*ldmk3)

                    @P.on("t")
                    def _(e, kk=kk):
                        for d in range(DT):
                            ins = e.mm(accr[d],
                                       WRS[:, kk % 2, d * 128:(d + 1) * 128],
                                       LNB[:, kk, 0:TCq], start=(kk == 0),
                                       stop=(kk == DT - 1))
                        if kk % 2 == 1:
                            P.ev("t", ins)
                    if kk % 2 == 1:
                        wrs_marks[0] = (EV["t"].h, EV["t"].n)
                        wrs_marks[1] = (EV["t"].h, EV["t"].n)
                tmk = EV["t"].n
                for dk in range(DT):
                    @P.on("s")
                    def _(e, tmk=tmk, dk=dk):
                        e.raw.wait_ge(EV["t"].h, tmk)
                        P.ev("s", e.scopy(RGB[:, dk, 0:TCq], accr[dk]))
                for b in range(8):
                    P.bfree(b)
                @P.on("v")
                def _(e):
                    P.wt(e, "s")
                    for k in range(DT):
                        e.copy(LNB[:, k, 0:1], LNB[:, k, TCq:TCq + 1])
                    P.ev("v")

                smk_k = (EV["s"].h, EV["s"].n)

                @P.on("v")
                def _(e, c=c):
                    e.raw.wait_ge(*smk_k)
                    if c > 0:
                        e.ts(NLC, WCAR[:, 0:8], -1.0, 0.0, OP.mult, OP.add)
                    e.memset(racc, 1.0)
                    e.memset(BARR[:, :, 0:1], 0.0 if c == 0 else 1.0)
                    e.tt(XT[:, :, 0:1].rearrange("p a b -> p (a b)"),
                         WCAR[:, 0:8], KB[:, :, 0:1].rearrange("p a b -> p (a b)"),
                         OP.subtract)
                    P.ev("v")
                vmk_e = (EV["v"].h, EV["v"].n)

                @P.on("s")
                def _(e, c=c):
                    e.raw.wait_ge(*vmk_e)
                    for k in range(DT):
                        if c == 0:
                            e.act(E2[:, k, :], KB[:, k, :], AF.Exp)
                        else:
                            e.act(E2[:, k, :], KB[:, k, :], AF.Exp,
                                  bias=NLC[:, k:k + 1])
                    e.drain()
                    P.ev("s")
                smk_e = (EV["s"].h, EV["s"].n)

                @P.on("v")
                def _(e, c=c):
                    e.raw.wait_ge(*smk_e)
                    for i in range(TCq):
                        sub0 = (i % RSUB == 0)
                        if sub0 and i > 0:
                            bpre = BARR[:, :, i:i + 1].rearrange("p a b -> p (a b)")
                            e.recip(r_, bpre)
                            e.tt(racc, racc, r_, OP.mult)
                            rb = racc.rearrange("p (a b) -> p a b", b=1) \
                                .broadcast_to([128, DT, RSUB])
                            e.tt(E2[:, :, i:i + RSUB], E2[:, :, i:i + RSUB], rb,
                                 OP.mult)
                        ecol = E2[:, :, i:i + 1].rearrange("p a b -> p (a b)")
                        if sub0:
                            e.ts(a_, ecol, 1.0, 0.0, OP.add, OP.add)
                            if c == 0 and i == 0:
                                e.copy(m_, ecol)
                                e.memset(a_, 0.0)
                                e.drain()
                                e.tt(a_, a_, ecol, OP.add)
                            else:
                                e.ts(m_, ecol, 1.0, 0.0, OP.max, OP.add)
                        else:
                            sin = BARR[:, :, i:i + 1].rearrange("p a b -> p (a b)")
                            e.tt(a_, sin, ecol, OP.add)
                            e.tt(m_, sin, ecol, OP.max)
                        e.tt(p_, a_, a_, OP.mult)
                        e.recip(r_, m_)
                        e.tt(p_, p_, r_, OP.mult)
                        e.tt(BARR[:, :, i + 1:i + 2].rearrange("p a b -> p (a b)"),
                             p_, WEXP, OP.mult)
                    P.ev("v")

                @P.on("v")
                def _(e):
                    e.tss(BARR[:, :, 1:TCq + 1], 1e-37, OP.max) if False else None
                    e.ts(BARR[:, :, 1:TCq + 1], BARR[:, :, 1:TCq + 1], 1e-37, 0.0,
                         OP.max, OP.add)
                    P.ev("v")
                vmk_c = (EV["v"].h, EV["v"].n)

                @P.on("s")
                def _(e):
                    e.raw.wait_ge(*vmk_c)
                    P.ev("s", e.act(BARR[:, :, 1:TCq + 1], BARR[:, :, 1:TCq + 1],
                                    AF.Ln))
                smk_l = (EV["s"].h, EV["s"].n)
                NJ = TCq // RSUB

                @P.on("v")
                def _(e, c=c):
                    e.raw.wait_ge(*smk_l)
                    if c == 0:
                        e.memset(LNSJ[:, 0:8], 0.0)
                    else:
                        e.copy(LNSJ[:, 0:8], WCAR[:, 0:8])
                    for j in range(1, NJ):
                        e.tt(LNSJ[:, j * 8:(j + 1) * 8], LNSJ[:, (j - 1) * 8:j * 8],
                             BARR[:, :, j * RSUB:j * RSUB + 1]
                             .rearrange("p a b -> p (a b)"), OP.add)
                    for j in range(NJ):
                        c0_ = j * RSUB
                        lb = LNSJ[:, j * 8:(j + 1) * 8] \
                            .rearrange("p (a b) -> p a b", b=1) \
                            .broadcast_to([128, DT, RSUB])
                        e.tt(XT[:, :, c0_ + 1:c0_ + 1 + RSUB],
                             BARR[:, :, c0_ + 1:c0_ + 1 + RSUB], lb, OP.add)
                    e.drain()
                    e.copy(WCAR[:, 0:8],
                           XT[:, :, TCq:TCq + 1].rearrange("p a b -> p (a b)"))
                    e.drain()
                    e.tt(XT[:, :, 1:TCq], XT[:, :, 1:TCq], KB[:, :, 1:TCq],
                         OP.subtract)
                    P.ev("v")
                vscan = EV["v"].n

                XTf = XT[:, :, 0:TCq]

                @P.on("s")
                def _(e):
                    e.raw.wait_ge(EV["v"].h, vscan)
                    e.act(SGX[:, :, :], XTf, AF.Sigmoid)
                    e.drain()
                    for k in range(DT):
                        e.act(E2[:, k, 0:TCq], XT[:, k, 0:TCq], AF.Sigmoid,
                              bias=VEC[:, 8 + k:8 + k + 1], scale=-1.0)
                    P.ev("s")
                smk5 = EV["s"].n
                DLT = E2

                @P.on("v")
                def _(e, smk5=smk5, c=c):
                    e.raw.wait_ge(EV["s"].h, smk5)
                    e.ts(YW[:, :, :], SGX[:, :, :], -1.0, 1.0, OP.mult, OP.add)
                    e.drain()
                    e.tt(YW[:, :, :], YW[:, :, :], VB[:, :, :], OP.mult)
                    e.drain()
                    e.copy(RAR[:, :, 0:1].rearrange("p a b -> p (a b)"),
                           WCAR[:, 8:16])
                    e.drain()
                    for k in range(DT):
                        e.scan(RAR[:, k, 1:TCq + 1], SGX[:, k, :], YW[:, k, :],
                               WCAR[:, 8 + k:8 + k + 1], OP.mult, OP.add)
                    e.drain()
                    e.copy(WCAR[:, 8:16],
                           RAR[:, :, TCq:TCq + 1].rearrange("p a b -> p (a b)"))
                    e.drain()
                    e.tt(YW[:, :, :], VB[:, :, :], RAR[:, :, 0:TCq], OP.subtract)
                    e.drain()
                    e.tt(YW[:, :, :], YW[:, :, :], DLT[:, :, 0:TCq], OP.mult)
                    e.drain()
                    e.tt(YW[:, :, :], YW[:, :, :], RAR[:, :, 0:TCq], OP.add)
                    P.ev("v")
                vmk6 = EV["v"].n

                @P.on("s")
                def _(e, vmk6=vmk6):
                    e.raw.wait_ge(EV["v"].h, vmk6)
                    P.ev("s", e.act(SGX[:, :, :], RGB[:, :, :], AF.Sigmoid))
                smk6 = EV["s"].n

                @P.on("v")
                def _(e, smk6=smk6, c=c):
                    e.raw.wait_ge(EV["s"].h, smk6)
                    e.tt(YW[:, :, :], YW[:, :, :], SGX[:, :, :], OP.mult)
                    e.drain()
                    xs = X[:, :, c * TCq:(c + 1) * TCq]
                    e.tt(xs, xs, YW[:, :, :], OP.add)
                    P.ev("v")
                kb_free[0] = (EV["v"].h, EV["v"].n)
                vb_free[0] = (EV["v"].h, EV["v"].n)
                rgb_free[0] = (EV["s"].h, EV["s"].n)
            P.barrier()

            TC2 = 256
            NC2 = T // TC2
            WCKT = r3(bf(PA[:, 0:8192]), 2048)
            HCK = r3(bf(PA[:, 8192:12288]), TC2)
            WCRT = r3(bf(PB[:, 0:4096]), 1024)
            WCVS = r3(bf(PB[:, 4096:5120]), 1024)
            RRB = r3(bf(PB[:, 5120:6144]), TC2)
            for k in range(DT):
                ld(WCRT[:, k, :], Hn["wcrn"][k * 128:(k + 1) * 128, :])
            wldmk2 = ldm()

            @P.on("v")
            def _(e):
                for k in range(DT):
                    e.copy(LNB[:, k, 0:1], VEC[:, 48 + k:48 + k + 1])
                P.ev("v")

            for tseg in range(NC2):
                emit_ln(tseg * TC2, TC2, LNB, 1, g_col=32, b_col=40)
                lnmk = EV["v"].n
                for dk in range(DT):
                    pmo = PS[4 + dk % 2]

                    @P.on("t")
                    def _(e, pmo=pmo, dk=dk):
                        e.raw.wait_ge(*wldmk2)
                        e.raw.wait_ge(EV["v"].h, lnmk)
                        P.bwait(e, 4 + dk % 2)
                        for k in range(DT):
                            ins = e.mm(pmo[:, 0:TC2], WCRT[:, k, dk * 128:(dk + 1) * 128],
                                       LNB[:, k, 0:TC2], start=(k == 0),
                                       stop=(k == DT - 1))
                        P.ev("t", ins)
                    tmk = EV["t"].n

                    @P.on("s")
                    def _(e, tmk=tmk, dk=dk, pmo=pmo):
                        e.raw.wait_ge(EV["t"].h, tmk)
                        P.ev("s", e.act(RRB[:, dk, :], pmo[:, 0:TC2], AF.Sigmoid))
                    P.bfree(4 + dk % 2)
                for fh in range(2):
                    tprev = EV["t"].n

                    @P.on("sy")
                    def _(e):
                        e.raw.wait_ge(EV["t"].h, tprev)
                    for k in range(DT):
                        ld(WCKT[:, k, :],
                           Hn["wckn"][k * 128:(k + 1) * 128, fh * 2048:(fh + 1) * 2048])
                    ldmk = ldm()
                    for pf in range(16):
                        pmx = PS[pf % 4]

                        @P.on("t")
                        def _(e, pmx=pmx, pf=pf):
                            e.raw.wait_ge(*ldmk)
                            e.raw.wait_ge(EV["v"].h, lnmk)
                            P.bwait(e, pf % 4)
                            for k in range(DT):
                                ins = e.mm(pmx[:, 0:TC2],
                                           WCKT[:, k, pf * 128:(pf + 1) * 128],
                                           LNB[:, k, 0:TC2], start=(k == 0),
                                           stop=(k == DT - 1))
                            P.ev("t", ins)
                        tmk = EV["t"].n

                        @P.on("s")
                        def _(e, tmk=tmk, pf=pf, fh=fh, pmx=pmx):
                            e.raw.wait_ge(EV["t"].h, tmk)
                            e.act(MREP[:, 0:TC2], pmx[:, 0:TC2], AF.Relu)
                            P.ev("s", e.act(HCK[:, fh * 16 + pf, :], MREP[:, 0:TC2],
                                            AF.Square))
                        P.bfree(pf % 4)
                smk = EV["s"].n
                accv = [PS[d][:, 0:TC2] for d in range(DT)]
                for kk in range(FT0):
                    if kk % 2 == 0:
                        tprev = EV["t"].n

                        @P.on("sy")
                        def _(e):
                            e.raw.wait_ge(EV["t"].h, tprev)
                        for j in range(2):
                            ld(WCVS[:, j, :],
                               Hn["wcvn"][(kk + j) * 128:(kk + j + 1) * 128, :])
                        ldmk4 = ldm()

                        @P.on("t")
                        def _(e):
                            e.raw.wait_ge(*ldmk4)
                            if kk == 0:
                                e.raw.wait_ge(EV["s"].h, smk)
                                for b in range(8):
                                    P.bwait(e, b)

                    @P.on("t")
                    def _(e, kk=kk):
                        for d in range(DT):
                            ins = e.mm(accv[d],
                                       WCVS[:, kk % 2, d * 128:(d + 1) * 128],
                                       HCK[:, kk, :], start=(kk == 0),
                                       stop=(kk == FT0 - 1))
                        if kk % 2 == 1:
                            P.ev("t", ins)
                tmk = EV["t"].n
                for dk in range(DT):
                    @P.on("v")
                    def _(e, tmk=tmk, dk=dk, tseg=tseg):
                        e.raw.wait_ge(EV["t"].h, tmk)
                        e.tt(MREP[:, 0:TC2], RRB[:, dk, :], accv[dk], OP.mult)
                        e.drain()
                        xs = X[:, dk, tseg * TC2:(tseg + 1) * TC2]
                        e.tt(xs, xs, MREP[:, 0:TC2], OP.add)
                        P.ev("v")
                for b in range(8):
                    P.bfree(b, key="v")

                @P.on("v")
                def _(e):
                    P.wt(e, "t")
                    for k in range(DT):
                        e.copy(LNB[:, k, 0:1], LNB[:, k, TC2:TC2 + 1])
                    P.ev("v")
            P.barrier()


        def allreduce_max(srcs):
            n = len(srcs)
            for i, src in enumerate(srcs):
                vm = (EV["v"].h, EV["v"].n)

                @P.on("g")
                def _(e, src=src, vm=vm):
                    e.raw.wait_ge(*vm)
                    gdma(e, GS1[0:1, 0:128], src,
                         allow_slow_non_contiguous=True)
                dgmk = gdm()

                @P.on("v")
                def _(e, dgmk=dgmk, i=i):
                    e.raw.wait_ge(*dgmk)
                    e.red(GS1[0:1, 152 + i:153 + i], GS1[0:1, 0:128], OP.max)
                    P.ev("v")
            vmk = (EV["v"].h, EV["v"].n)

            @P.on("g")
            def _(e):
                e.raw.wait_ge(*vmk)
                gdma(e, Hn["ccin"][0:1, 0:n], GS1[0:1, 152:152 + n])
                m = gdm()
                e.raw.wait_ge(*m)
                CC.inc(e.collective("AllReduce", OP.max, group,
                                    [Hn["ccin"][0:1, 0:n]],
                                    [Hn["ccout"][0:1, 0:n]]), 1)
                e.raw.wait_ge(CC.h, CC.n)
                gdma(e, GS1[0:1, 152:152 + n], Hn["ccout"][0:1, 0:n])
            armk = gdm()

            @P.on("v")
            def _(e):
                e.raw.wait_ge(*armk)
                P.ev("v")
            return [GS1[0:1, 152 + i:153 + i] for i in range(n)]

        def bcast_cols(pairs):
            @P.on("g")
            def _(e):
                P.wt(e, "v")
                for src, col in pairs:
                    gdma(e, Hn["bounce"][0:1, col:col + 1], src)
                m = gdm()
                e.raw.wait_ge(*m)
                for src, col in pairs:
                    gdma(e, GS[:, col:col + 1],
                         Hn["bounce"][0:1, col:col + 1].broadcast_to([128, 1]))
            return gdm()

        def sub_qffn(callidx):
            QXB = r3(bf(PA[:, 0:DT * T // 2]), T)
            WQH = r3(bf(PA[:, 8192:14336]), 1536)
            STGB = r3(bf(PB[:, 0:2048]), TCF)
            STGI = PB[:, 2048:2304].bitcast(I16)[:, 0:TCF]
            STGI2 = PB[:, 2304:2560].bitcast(I16)[:, 0:TCF]
            S2 = PC[:, 2048:2560][:, 0:TCF]
            S3 = PC[:, 2560:3072][:, 0:TCF]
            S4 = PC[:, 3072:3584][:, 0:TCF]
            AMX = SM[:, 0:16]

            gc_rsx, gc_half, gc_rsup, gc_rsg, gc_sg, gc_rsg2, gc_rso, gc_so = \
                range(2, 10)

            @P.on("v")
            def _(e):
                e.memset(AMX[:, 0:6], 0.0)
                P.ev("v")
            for tseg in range(NCF):
                emit_ln(tseg * TCF, TCF, QXB, tseg * TCF)
                vmk = (EV["v"].h, EV["v"].n)

                @P.on("v")
                def _(e, vmk=vmk, tseg=tseg):
                    e.raw.wait_ge(*vmk)
                    e.red(SM[:, 24:32], QXB[:, :, tseg * TCF:(tseg + 1) * TCF],
                          OP.max, absval=True)
                    e.drain()
                    e.red(SM[:, 20:21], SM[:, 24:32], OP.max)
                    e.drain()
                    e.tt(AMX[:, 0:1], AMX[:, 0:1], SM[:, 20:21], OP.max)
                    P.ev("v")
            if DBGQ[0] is not None:
                vq = (EV["v"].h, EV["v"].n)

                @P.on("g")
                def _(e, vq=vq):
                    e.raw.wait_ge(*vq)
                    for k in range(DT):
                        gdma(e, DBGQ[0][k * 128:(k + 1) * 128, :], QXB[:, k, :])
                    gdm()
            g = allreduce_max([AMX[:, 0:1]])[0]

            @P.on("v")
            def _(e, g=g):
                P.wt(e, "v")
                e.ts(GS1[0:1, 132:133], g, 1.0 / QMAX, 1e-12, OP.mult, OP.max)
                e.recip(GS1[0:1, 133:134], GS1[0:1, 132:133])
                P.ev("v")
            bm = bcast_cols([(GS1[0:1, 133:134], gc_rsx)])

            S2X = r3(PA[:, 8192:12288][:, 0:DT * TCF], TCF)
            STGIX = PA[:, 12288:14336].bitcast(I16)[:, 0:DT * TCF]
            T4X = PB[:, 2048:6144][:, 0:DT * TCF]
            qv_free = None
            for tseg in range(NCF):
                qvf = qv_free

                @P.on("s")
                def _(e, tseg=tseg, bm=bm, qvf=qvf):
                    e.raw.wait_ge(*bm)
                    if qvf is not None:
                        e.raw.wait_ge(*qvf)
                    P.ev("s", e.act(S2X, QXB[:, :, tseg * TCF:(tseg + 1) * TCF],
                                    AF.Identity, scale=GS[:, gc_rsx:gc_rsx + 1]))
                vm2 = (EV["s"].h, EV["s"].n)

                @P.on("v")
                def _(e2, vm2=vm2, tseg=tseg):
                    e2.raw.wait_ge(*vm2)
                    flat = S2X.rearrange("p a b -> p (a b)")
                    t4x3 = T4X.rearrange("p (a b) -> p a b", b=TCF)
                    if rmode == "int":
                        e2.copy(STGIX, flat)
                        e2.drain()
                        e2.copy(QXB[:, :, tseg * TCF:(tseg + 1) * TCF], STGIX
                                .rearrange("p (a b) -> p a b", b=TCF))
                    else:
                        e2.ts(flat, flat, 0.5, 0.0, OP.add, OP.add)
                        e2.drain()
                        e2.tss(T4X, flat, 1.0, OP.mod)
                        e2.drain()
                        e2.tt(QXB[:, :, tseg * TCF:(tseg + 1) * TCF], S2X, t4x3,
                              OP.subtract)
                    P.ev("v")
                qv_free = (EV["v"].h, EV["v"].n)
            qx_mk = (EV["v"].h, EV["v"].n)

            slot_marks = [None] * 8

            def ff_mm(hkey, rawkey, amx_col):
                for fh in range(2):
                    tprev = (EV["t"].h, EV["t"].n)

                    @P.on("sy")
                    def _(e, tprev=tprev):
                        e.raw.wait_ge(*tprev)
                        e.raw.wait_ge(*qx_mk)
                    for k in range(DT):
                        ld(WQH[:, k, :],
                           Hn[hkey][fh * 1536:(fh + 1) * 1536, k * 128:(k + 1) * 128],
                           transpose=True)
                    ldmk = ldm()
                    for tseg in range(NCF):
                        for pf in range(12):
                            f = fh * 12 + pf
                            pmx = PS[pf % 4]

                            @P.on("t")
                            def _(e, pmx=pmx, pf=pf, f=f, tseg=tseg, ldmk=ldmk,
                                  qx_mk=qx_mk):
                                e.raw.wait_ge(*ldmk)
                                e.raw.wait_ge(*qx_mk)
                                P.bwait(e, pf % 4)
                                for k in range(DT):
                                    ins = e.mm(pmx[:, 0:TCF],
                                               WQH[:, k, pf * 128:(pf + 1) * 128],
                                               QXB[:, k, tseg * TCF:(tseg + 1) * TCF],
                                               start=(k == 0), stop=(k == DT - 1))
                                P.ev("t", ins)
                            tmk = (EV["t"].h, EV["t"].n)

                            @P.on("s")
                            def _(e, tmk=tmk, pf=pf, pmx=pmx,
                                  sl=None):
                                e.raw.wait_ge(*tmk)
                                if slot_marks[pf % 8] is not None:
                                    e.raw.wait_ge(*slot_marks[pf % 8])
                                P.ev("s", e.scopy(STGB[:, pf % 8, :], pmx[:, 0:TCF]))
                            P.bfree(pf % 4)
                            if pf % 4 == 3:
                                base = (pf - 3) % 8
                                smk = (EV["s"].h, EV["s"].n)

                                @P.on("v")
                                def _(e, smk=smk, base=base, amx_col=amx_col):
                                    e.raw.wait_ge(*smk)
                                    e.red(SM[:, 21:22],
                                          STGB[:, base:base + 4, :]
                                          .rearrange("p a b -> p (a b)"), OP.max,
                                          absval=True)
                                    e.drain()
                                    e.tt(AMX[:, amx_col:amx_col + 1],
                                         AMX[:, amx_col:amx_col + 1],
                                         SM[:, 21:22], OP.max)
                                    P.ev("v")
                                vmk = (EV["v"].h, EV["v"].n)

                                @P.on("g")
                                def _(e, vmk=vmk, f=f, tseg=tseg, base=base,
                                      pf=pf, rawkey=rawkey):
                                    e.raw.wait_ge(*vmk)
                                    for jj in range(4):
                                        fj = f - 3 + jj
                                        gdma(e, Hn[rawkey][fj * 128:(fj + 1) * 128,
                                                           tseg * TCF:(tseg + 1) * TCF],
                                             STGB[:, base + jj, :])
                                gm4 = gdm()
                                for jj in range(4):
                                    slot_marks[base + jj] = gm4

            ff_mm("wun", "upraw", 1)
            ff_mm("wgn", "gateraw", 2)
            gups = allreduce_max([AMX[:, 1:2], AMX[:, 2:3]])

            @P.on("v")
            def _(e):
                P.wt(e, "v")
                for i, (dst_s, dst_rs) in enumerate([(134, 135), (136, 137)]):
                    e.tt(GS1[0:1, 142:143], GS1[0:1, 152 + i:153 + i],
                         GS1[0:1, 132:133], OP.mult)
                    e.drain()
                    e.ts(GS1[0:1, dst_s:dst_s + 1], GS1[0:1, 142:143],
                         1.0 / QMAX, 1e-12, OP.mult, OP.max)
                    e.drain()
                    e.recip(GS1[0:1, 143:144], GS1[0:1, dst_s:dst_s + 1])
                    e.drain()
                    e.tt(GS1[0:1, dst_rs:dst_rs + 1], GS1[0:1, 143:144],
                         GS1[0:1, 132:133], OP.mult)
                    e.drain()
                P.ev("v")
            bm2 = bcast_cols([(GS1[0:1, 137:138], gc_rsg),
                              (GS1[0:1, 135:136], gc_rsup),
                              (GS1[0:1, 136:137], gc_sg)])

            BQ = 6
            NG = FT1 // BQ
            QW = BQ * TCF
            UP6 = bf(PA[:, 0:QW // 2])
            GT6 = bf(PA[:, QW // 2:QW])
            S2B = PA[:, QW:2 * QW]
            S4B = PA[:, 2 * QW:3 * QW]
            S3B = PA[:, 3 * QW:4 * QW]
            STGI6 = PA[:, 4 * QW:4 * QW + QW // 2].bitcast(I16)
            STGI2B = PB[:, 0:QW // 2].bitcast(I16)
            qup_mark = [None]
            g2_mark = [None]
            sm_snapshot = [m for m in slot_marks if m is not None]

            @P.on("sy")
            def _(e):
                for m in sm_snapshot:
                    e.raw.wait_ge(*m)
            for tseg in range(NCF):
                for gi in range(NG):
                    f0 = gi * BQ
                    gm_prev = g2_mark[0]
                    if gm_prev is not None:
                        @P.on("sy")
                        def _(e, gm_prev=gm_prev):
                            e.raw.wait_ge(*gm_prev)
                    ld_guard("s", "v")
                    for j in range(BQ):
                        ld(UP6[:, j * TCF:(j + 1) * TCF],
                           Hn["upraw"][(f0 + j) * 128:(f0 + j + 1) * 128,
                                       tseg * TCF:(tseg + 1) * TCF])
                        ld(GT6[:, j * TCF:(j + 1) * TCF],
                           Hn["gateraw"][(f0 + j) * 128:(f0 + j + 1) * 128,
                                         tseg * TCF:(tseg + 1) * TCF])
                    mark = ldm()

                    @P.on("s")
                    def _(e, mark=mark, bm2=bm2):
                        e.raw.wait_ge(*mark)
                        e.raw.wait_ge(*bm2)
                        P.wt(e, "v")
                        e.act(S2B, UP6, AF.Identity,
                              scale=GS[:, gc_rsup:gc_rsup + 1])
                        e.drain()
                        P.ev("s", e.act(S3B, GT6, AF.Identity,
                                        scale=GS[:, gc_rsg:gc_rsg + 1]))
                    smk = (EV["s"].h, EV["s"].n)

                    @P.on("v")
                    def _(e, smk=smk):
                        e.raw.wait_ge(*smk)
                        if qup_mark[0] is not None:
                            e.raw.wait_ge(*qup_mark[0])
                        if rmode == "int":
                            e.copy(STGI6, S2B)
                            e.drain()
                            e.copy(STGI2B, S3B)
                        else:
                            e.ts(S2B, S2B, 0.5, 0.0, OP.add, OP.add)
                            e.drain()
                            e.tss(S4B, S2B, 1.0, OP.mod)
                            e.drain()
                            e.tt(STGI6, S2B, S4B, OP.subtract)
                            e.drain()
                            e.ts(S3B, S3B, 0.5, 0.0, OP.add, OP.add)
                            e.drain()
                            e.tss(S4B, S3B, 1.0, OP.mod)
                            e.drain()
                            e.tt(STGI2B, S3B, S4B, OP.subtract)
                        P.ev("v")
                    vmk = (EV["v"].h, EV["v"].n)

                    @P.on("g")
                    def _(e, vmk=vmk, f0=f0, tseg=tseg):
                        e.raw.wait_ge(*vmk)
                        for j in range(BQ):
                            gdma(e, Hn["qup"][(f0 + j) * 128:(f0 + j + 1) * 128,
                                              tseg * TCF:(tseg + 1) * TCF],
                                 STGI6[:, j * TCF:(j + 1) * TCF])
                    qup_mark[0] = gdm()

                    @P.on("s")
                    def _(e):
                        P.wt(e, "v")
                        P.ev("s", e.act(S2B, STGI2B, AF.Sigmoid,
                                        scale=GS[:, gc_sg:gc_sg + 1]))
                    smk2 = (EV["s"].h, EV["s"].n)

                    @P.on("v")
                    def _(e, smk2=smk2):
                        e.raw.wait_ge(*smk2)
                        e.ts(S3B, S2B, 65536.0, 0.0, OP.mult, OP.add)
                        e.drain()
                        if rmode == "int":
                            e.copy(S4B.bitcast(I32), S3B)
                            e.drain()
                            e.tss(S2B, S4B.bitcast(I32), 65535.0, OP.min)
                        else:
                            e.ts(S3B, S3B, 0.5, 0.0, OP.add, OP.add)
                            e.drain()
                            e.tss(S4B, S3B, 1.0, OP.mod)
                            e.drain()
                            e.tt(S2B, S3B, S4B, OP.subtract)
                            e.drain()
                            e.tss(S2B, S2B, 65535.0, OP.min)
                        e.drain()
                        e.tt(S3B, STGI2B, S2B, OP.mult)
                        e.drain()
                        if g2_mark[0] is not None:
                            e.raw.wait_ge(*g2_mark[0])
                        e.copy(GT6, S3B)
                        e.drain()
                        e.red(SM[:, 21:22], S3B, OP.max, absval=True)
                        e.drain()
                        e.tt(AMX[:, 3:4], AMX[:, 3:4], SM[:, 21:22], OP.max)
                        P.ev("v")
                    vmk2 = (EV["v"].h, EV["v"].n)

                    @P.on("g")
                    def _(e, vmk2=vmk2, f0=f0, tseg=tseg):
                        e.raw.wait_ge(*vmk2)
                        for j in range(BQ):
                            gdma(e, Hn["g2raw"][(f0 + j) * 128:(f0 + j + 1) * 128,
                                                tseg * TCF:(tseg + 1) * TCF],
                                 GT6[:, j * TCF:(j + 1) * TCF])
                    g2_mark[0] = gdm()

            gg2 = allreduce_max([AMX[:, 3:4]])[0]

            @P.on("v")
            def _(e, gg2=gg2):
                P.wt(e, "v")
                e.ts(GS1[0:1, 144:145], GS1[0:1, 136:137], 1.0 / 65536.0, 0.0,
                     OP.mult, OP.add)
                e.drain()
                e.tt(GS1[0:1, 145:146], gg2, GS1[0:1, 144:145], OP.mult)
                e.drain()
                e.ts(GS1[0:1, 138:139], GS1[0:1, 145:146], 1.0 / QMAX, 1e-12,
                     OP.mult, OP.max)
                e.drain()
                e.recip(GS1[0:1, 143:144], GS1[0:1, 138:139])
                e.drain()
                e.tt(GS1[0:1, 139:140], GS1[0:1, 143:144], GS1[0:1, 144:145],
                     OP.mult)
                P.ev("v")
            bm3 = bcast_cols([(GS1[0:1, 139:140], gc_rsg2)])

            oraw_mark = [None]
            if g2_mark[0] is not None:
                gm_ = g2_mark[0]
                qm_ = qup_mark[0]

                @P.on("sy")
                def _(e, gm_=gm_, qm_=qm_):
                    e.raw.wait_ge(*gm_)
                    if qm_ is not None:
                        e.raw.wait_ge(*qm_)
            for tseg in range(NCF):
                for gi in range(NG):
                    f0 = gi * BQ
                    ld_guard("s", "v")
                    for j in range(BQ):
                        ld(GT6[:, j * TCF:(j + 1) * TCF],
                           Hn["g2raw"][(f0 + j) * 128:(f0 + j + 1) * 128,
                                       tseg * TCF:(tseg + 1) * TCF])
                        ld(STGI6[:, j * TCF:(j + 1) * TCF],
                           Hn["qup"][(f0 + j) * 128:(f0 + j + 1) * 128,
                                     tseg * TCF:(tseg + 1) * TCF])
                    mark = ldm()

                    @P.on("s")
                    def _(e, mark=mark, bm3=bm3):
                        e.raw.wait_ge(*mark)
                        e.raw.wait_ge(*bm3)
                        P.wt(e, "v")
                        P.ev("s", e.act(S2B, GT6, AF.Identity,
                                        scale=GS[:, gc_rsg2:gc_rsg2 + 1]))
                    smk = (EV["s"].h, EV["s"].n)

                    @P.on("v")
                    def _(e, smk=smk):
                        e.raw.wait_ge(*smk)
                        if rmode == "int":
                            e.copy(STGI2B, S2B)
                            e.drain()
                            e.tt(S2B, STGI2B, STGI6, OP.mult)
                        else:
                            e.ts(S2B, S2B, 0.5, 0.0, OP.add, OP.add)
                            e.drain()
                            e.tss(S4B, S2B, 1.0, OP.mod)
                            e.drain()
                            e.tt(S3B, S2B, S4B, OP.subtract)
                            e.drain()
                            e.tt(S2B, S3B, STGI6, OP.mult)
                        e.drain()
                        if oraw_mark[0] is not None:
                            e.raw.wait_ge(*oraw_mark[0])
                        e.copy(UP6, S2B)
                        e.drain()
                        e.red(SM[:, 21:22], S2B, OP.max, absval=True)
                        e.drain()
                        e.tt(AMX[:, 4:5], AMX[:, 4:5], SM[:, 21:22], OP.max)
                        P.ev("v")
                    vmk = (EV["v"].h, EV["v"].n)

                    @P.on("g")
                    def _(e, vmk=vmk, f0=f0, tseg=tseg):
                        e.raw.wait_ge(*vmk)
                        for j in range(BQ):
                            gdma(e, Hn["upraw"][(f0 + j) * 128:(f0 + j + 1) * 128,
                                                tseg * TCF:(tseg + 1) * TCF],
                                 UP6[:, j * TCF:(j + 1) * TCF])
                    oraw_mark[0] = gdm()
            go = allreduce_max([AMX[:, 4:5]])[0]

            @P.on("v")
            def _(e, go=go):
                P.wt(e, "v")
                e.tt(GS1[0:1, 146:147], GS1[0:1, 138:139], GS1[0:1, 134:135],
                     OP.mult)
                e.drain()
                e.tt(GS1[0:1, 147:148], go, GS1[0:1, 146:147], OP.mult)
                e.drain()
                e.ts(GS1[0:1, 140:141], GS1[0:1, 147:148], 1.0 / QMAX, 1e-12,
                     OP.mult, OP.max)
                e.drain()
                e.recip(GS1[0:1, 143:144], GS1[0:1, 140:141])
                e.drain()
                e.tt(GS1[0:1, 141:142], GS1[0:1, 143:144], GS1[0:1, 146:147],
                     OP.mult)
                e.drain()
                e.ts(GS1[0:1, 148:149], GS1[0:1, 140:141], 0.5, 0.0, OP.mult,
                     OP.add)
                P.ev("v")
            bm4 = bcast_cols([(GS1[0:1, 141:142], gc_rso),
                              (GS1[0:1, 148:149], gc_so)])

            QOT = r3(bf(PB[:, 2048:8192]), TCF)
            WDK = r3(bf(PA[:, 8192:9216]), 1024)
            dn_free = [None, None]
            if oraw_mark[0] is not None:
                om_ = oraw_mark[0]

                @P.on("sy")
                def _(e, om_=om_):
                    e.raw.wait_ge(*om_)
            for tseg in range(NCF):
                for gi in range(NG):
                    f0 = gi * BQ
                    ld_guard("s", "v")
                    for j in range(BQ):
                        ld(UP6[:, j * TCF:(j + 1) * TCF],
                           Hn["upraw"][(f0 + j) * 128:(f0 + j + 1) * 128,
                                       tseg * TCF:(tseg + 1) * TCF])
                    mark = ldm()

                    @P.on("s")
                    def _(e, mark=mark, bm4=bm4):
                        e.raw.wait_ge(*mark)
                        e.raw.wait_ge(*bm4)
                        P.wt(e, "v")
                        P.ev("s", e.act(S2B, UP6, AF.Identity,
                                        scale=GS[:, gc_rso:gc_rso + 1]))
                    smk = (EV["s"].h, EV["s"].n)

                    @P.on("v")
                    def _(e, smk=smk, f0=f0):
                        e.raw.wait_ge(*smk)
                        P.wt(e, "t")
                        qsl = QOT[:, f0:f0 + BQ, :].rearrange("p a b -> p (a b)")
                        if rmode == "int":
                            e.copy(STGI2B, S2B)
                            e.drain()
                            e.copy(qsl, STGI2B)
                        else:
                            e.ts(S2B, S2B, 0.5, 0.0, OP.add, OP.add)
                            e.drain()
                            e.tss(S3B, S2B, 1.0, OP.mod)
                            e.drain()
                            e.tt(S2B, S2B, S3B, OP.subtract)
                            e.drain()
                            e.copy(qsl, S2B)
                        P.ev("v")
                qomk = (EV["v"].h, EV["v"].n)

                @P.on("t")
                def _(e):
                    for b in range(8):
                        P.bwait(e, b)
                for kk in range(FT1):
                    if kk % 2 == 0:
                        tprev = (EV["t"].h, EV["t"].n)

                        @P.on("sy")
                        def _(e, tprev=tprev):
                            e.raw.wait_ge(*tprev)
                        for j in range(2):
                            ld(WDK[:, j, :],
                               Hn["wdn"][0:1024, (kk + j) * 128:(kk + j + 1) * 128],
                               transpose=True)
                        ldmk2 = ldm()

                        @P.on("t")
                        def _(e, ldmk2=ldmk2, qomk=qomk, kk=kk):
                            e.raw.wait_ge(*ldmk2)
                            if kk == 0:
                                e.raw.wait_ge(*qomk)

                    @P.on("t")
                    def _(e, kk=kk):
                        for dk in range(DT):
                            ins = e.mm(PS[dk][:, 0:TCF],
                                       WDK[:, kk % 2, dk * 128:(dk + 1) * 128],
                                       QOT[:, kk, :], start=(kk == 0),
                                       stop=(kk == FT1 - 1))
                        if kk % 2 == 1:
                            P.ev("t", ins)
                tmk = (EV["t"].h, EV["t"].n)
                for dk in range(DT):
                    drep = MREP if dk % 2 == 0 else VREP
                    dfree = dn_free[dk % 2]

                    @P.on("s")
                    def _(e, tmk=tmk, dk=dk, drep=drep, dfree=dfree):
                        e.raw.wait_ge(*tmk)
                        if dfree is not None:
                            e.raw.wait_ge(*dfree)
                        P.ev("s", e.act(drep[:, 0:TCF], PS[dk][:, 0:TCF], AF.Identity,
                                        scale=GS[:, gc_so:gc_so + 1]))
                    P.bfree(dk)
                    smk2 = (EV["s"].h, EV["s"].n)

                    @P.on("v")
                    def _(e, smk2=smk2, dk=dk, tseg=tseg, drep=drep):
                        e.raw.wait_ge(*smk2)
                        xs = X[:, dk, tseg * TCF:(tseg + 1) * TCF]
                        e.tt(xs, xs, drep[:, 0:TCF], OP.add)
                        P.ev("v")
                    dn_free[dk % 2] = (EV["v"].h, EV["v"].n)
            P.barrier()

        def sub_mamba():
            BCB_B = r3(hf(PA[:, 0:4096]), TCF)
            BCB_C = r3(hf(PA[:, 4096:8192]), TCF)
            YBUF = r3(bf(PA[:, 8192:12288]), TCF)
            DTB = PA[:, 12288:12800][:, 0:TCF]
            PBt = hf(PA[:, 12800:13056])[:, 0:TCF]
            A2 = r3(hf(PA[:, 13056:13568])[:, 0:2 * TCF], TCF)
            U2 = r3(hf(PA[:, 13568:14080])[:, 0:2 * TCF], TCF)
            H2 = r3(hf(PC[:, 5120:5632])[:, 0:2 * TCF], TCF)
            HC2 = r3(hf(PC[:, 5632:6144])[:, 0:2 * TCF], TCF)
            DTR_ = hf(PA[:, 14080:14336])[:, 0:TCF]
            XS = r3(hf(PB[:, 0:4096]), TCF)
            ZB = r3(bf(PB[:, 4096:8192]), TCF)
            WQ = r3(bf(PA[:, 0:8192]), 2048)
            XCB = PC[:, 0:520]
            CACC = PC[:, 1024:1544][:, 0:TCF]
            SYV = PC[:, 2048:2560][:, 0:TCF]
            SZ = PC[:, 2560:3072][:, 0:TCF]
            BROW = hf(PC[:, 3072:3584])

            @P.on("v")
            def _(e):
                e.memset(CARRY[:, :, :], 0.0)
                e.memset(XHALO[:, :, :], 0.0)
                P.ev("v")

            prev_ymk = None
            brow_free = []
            dtb_free = None
            a_free = None
            for tseg in range(NCF):
                emit_ln(tseg * TCF, TCF, LNB, 0)
                lnmk = (EV["v"].h, EV["v"].n)
                for eh in range(2):
                    tprev = (EV["t"].h, EV["t"].n)
                    pym = prev_ymk

                    @P.on("sy")
                    def _(e, tprev=tprev, pym=pym):
                        e.raw.wait_ge(*tprev)
                        if pym is not None:
                            e.raw.wait_ge(*pym)
                    for k in range(DT):
                        ld(WQ[:, k, :],
                           Hn["inpn"][eh * 2048:(eh + 1) * 2048, k * 128:(k + 1) * 128],
                           transpose=True)
                    ldmk = ldm()
                    for et in range(IT):
                        pmx = PS[et % 4]

                        @P.on("t")
                        def _(e, pmx=pmx, et=et, ldmk=ldmk, lnmk=lnmk):
                            e.raw.wait_ge(*ldmk)
                            e.raw.wait_ge(*lnmk)
                            P.bwait(e, et % 4)
                            for k in range(DT):
                                ins = e.mm(pmx[:, 0:TCF],
                                           WQ[:, k, et * 128:(et + 1) * 128],
                                           LNB[:, k, 0:TCF], start=(k == 0),
                                           stop=(k == DT - 1))
                            P.ev("t", ins)
                        tmk = (EV["t"].h, EV["t"].n)
                        if eh == 1:
                            @P.on("s")
                            def _(e, tmk=tmk, et=et, pmx=pmx):
                                e.raw.wait_ge(*tmk)
                                P.ev("s", e.scopy(ZB[:, et, :], pmx[:, 0:TCF]))
                            P.bfree(et % 4)
                        else:
                            @P.on("s")
                            def _(e, tmk=tmk, et=et, pmx=pmx):
                                e.raw.wait_ge(*tmk)
                                P.ev("s", e.scopy(XCB[:, 3:3 + TCF], pmx[:, 0:TCF]))
                            P.bfree(et % 4)
                            smk = (EV["s"].h, EV["s"].n)

                            @P.on("v")
                            def _(e, smk=smk, et=et):
                                e.raw.wait_ge(*smk)
                                e.copy(XCB[:, 0:3], XHALO[:, et, :])
                                e.drain()
                                e.copy(XHALO[:, et, :], XCB[:, TCF:TCF + 3])
                                e.drain()
                                e.ts(CACC, XCB[:, 0:TCF], CONVW[:, et:et + 1],
                                     0.0, OP.mult, OP.add)
                                e.drain()
                                for j in range(1, 4):
                                    e.stt(CACC, XCB[:, j:j + TCF],
                                          CONVW[:, j * IT + et:j * IT + et + 1],
                                          CACC, OP.mult, OP.add)
                                    e.drain()
                                P.ev("v")
                            vmk = (EV["v"].h, EV["v"].n)

                            @P.on("s")
                            def _(e, vmk=vmk, et=et):
                                e.raw.wait_ge(*vmk)
                                P.ev("s", e.act(XS[:, et, :], CACC, AF.Silu,
                                                bias=VEC2[:, 40 + et:41 + et]))
                bc_marks = []
                smk0 = (EV["s"].h, EV["s"].n)
                for ci, (c0, cn, pidx) in enumerate([(0, 64, 0), (64, 128, 1),
                                                     (192, 128, 2)]):
                    pmx = PS[pidx]

                    @P.on("t")
                    def _(e, c0=c0, cn=cn, pmx=pmx, smk0=smk0, pidx=pidx):
                        e.raw.wait_ge(*smk0)
                        P.bwait(e, pidx)
                        for k in range(IT):
                            ins = e.mm(pmx[0:cn, 0:TCF], XPT[:, k, c0:c0 + cn],
                                       XS[:, k, :], start=(k == 0),
                                       stop=(k == IT - 1))
                        P.ev("t", ins)
                    tmk = (EV["t"].h, EV["t"].n)
                    if ci == 0:
                        @P.on("s")
                        def _(e, tmk=tmk, pmx=pmx):
                            e.raw.wait_ge(*tmk)
                            P.ev("s", e.scopy(DTR_[0:64, :], pmx[0:64, 0:TCF]))
                        P.bfree(pidx)
                        dtr_mk = (EV["s"].h, EV["s"].n)
                    else:
                        bfw = list(brow_free)

                        @P.on("s")
                        def _(e, tmk=tmk, pmx=pmx, ci=ci, bfw=bfw):
                            e.raw.wait_ge(*tmk)
                            for m_ in bfw:
                                e.raw.wait_ge(*m_)
                            P.ev("s", e.scopy(BROW[:, (ci - 1) * TCF:ci * TCF],
                                              pmx[:, 0:TCF]))
                        P.bfree(pidx)
                        smk = (EV["s"].h, EV["s"].n)
                        pym = prev_ymk

                        @P.on("g")
                        def _(e, smk=smk, ci=ci, tseg=tseg, pym=pym):
                            e.raw.wait_ge(*smk)
                            hn = Hn["rowbB"] if ci == 1 else Hn["rowbC"]
                            for n_ in range(DS):
                                gdma(e, hn[n_:n_ + 1, tseg * TCF:(tseg + 1) * TCF],
                                     BROW[n_:n_ + 1, (ci - 1) * TCF:(ci - 1) * TCF + TCF])
                            m = gdm()
                            e.raw.wait_ge(*m)
                            if pym is not None:
                                e.raw.wait_ge(*pym)
                            dstt = BCB_B if ci == 1 else BCB_C
                            for n_ in range(DS):
                                gdma(e, dstt[:, n_, :],
                                     hn[n_:n_ + 1, tseg * TCF:(tseg + 1) * TCF]
                                     .broadcast_to([128, TCF]))
                        bc_marks.append(gdm())
                for it_ in range(IT):
                    pdt = PS[3]

                    @P.on("t")
                    def _(e, it_=it_, pdt=pdt, dtr_mk=dtr_mk):
                        e.raw.wait_ge(*dtr_mk)
                        P.bwait(e, 3)
                        ins = e.mm(pdt[:, 0:TCF], DTPT[0:64, it_ * 128:(it_ + 1) * 128],
                                   DTR_[0:64, :], start=True, stop=True)
                        P.ev("t", ins)
                    tmk = (EV["t"].h, EV["t"].n)
                    dtbf = dtb_free

                    @P.on("s")
                    def _(e, tmk=tmk, it_=it_, pdt=pdt, dtbf=dtbf):
                        e.raw.wait_ge(*tmk)
                        if dtbf is not None:
                            e.raw.wait_ge(*dtbf)
                        e.act(MREP[:, 0:TCF], pdt[:, 0:TCF],
                              AF.Exp, bias=VEC2[:, 56 + it_:57 + it_])
                        e.drain()
                        P.ev("s", e.act(DTB, MREP[:, 0:TCF], AF.Ln,
                                        bias=CONSTS[:, 3:4]))
                    P.bfree(3)
                    smk = (EV["s"].h, EV["s"].n)

                    @P.on("v")
                    def _(e, smk=smk, it_=it_):
                        e.raw.wait_ge(*smk)
                        e.tt(PBt, DTB, XS[:, it_, :], OP.mult)
                        P.ev("v")
                    pmk = (EV["v"].h, EV["v"].n)
                    dtb_free = pmk
                    pyp = PS[2]
                    for n2 in range(DS // 2):
                        n_ = 2 * n2
                        af = a_free

                        @P.on("s")
                        def _(e, it_=it_, n_=n_, pmk=pmk, af=af):
                            if n_ == 0:
                                e.raw.wait_ge(*pmk)
                            if af is not None:
                                e.raw.wait_ge(*af)
                            e.act(A2[:, 0, :], DTB, AF.Exp,
                                  scale=ASB[:, it_ * DS + n_:it_ * DS + n_ + 1])
                            P.ev("s", e.act(A2[:, 1, :], DTB, AF.Exp,
                                            scale=ASB[:, it_ * DS + n_ + 1:
                                                      it_ * DS + n_ + 2]))
                        smk2 = (EV["s"].h, EV["s"].n)

                        @P.on("v")
                        def _(e, smk2=smk2, it_=it_, n_=n_, bc_marks=bc_marks):
                            e.raw.wait_ge(*smk2)
                            if it_ == 0 and n_ == 0:
                                for m_ in bc_marks:
                                    e.raw.wait_ge(*m_)
                            P.wt(e, "t")
                            pb2 = PBt.rearrange("p (a b) -> p a b", a=1) \
                                .broadcast_to([128, 2, TCF])
                            e.tt(U2[:, :, :], pb2, BCB_B[:, n_:n_ + 2, :], OP.mult)
                            e.drain()
                            e.scan(H2[:, 0, :], A2[:, 0, :], U2[:, 0, :],
                                   CARRY[:, it_, n_:n_ + 1], OP.mult, OP.add)
                            e.scan(H2[:, 1, :], A2[:, 1, :], U2[:, 1, :],
                                   CARRY[:, it_, n_ + 1:n_ + 2], OP.mult, OP.add)
                            e.drain()
                            e.copy(CARRY[:, it_, n_:n_ + 2],
                                   H2[:, :, TCF - 1:TCF]
                                   .rearrange("p a b -> p (a b)"))
                            e.drain()
                            e.tt(HC2[:, :, :], H2[:, :, :], BCB_C[:, n_:n_ + 2, :],
                                 OP.mult)
                            P.ev("v")
                        vmk2 = (EV["v"].h, EV["v"].n)
                        a_free = vmk2

                        @P.on("t")
                        def _(e, vmk2=vmk2, n_=n_, pyp=pyp):
                            e.raw.wait_ge(*vmk2)
                            if n_ == 0:
                                P.bwait(e, 2)
                            e.mm(pyp[:, 0:TCF], IDH[:], HC2[:, 0, :],
                                 start=(n_ == 0), stop=False)
                            ins = e.mm(pyp[:, 0:TCF], IDH[:], HC2[:, 1, :],
                                       start=False, stop=(n_ + 2 == DS))
                            P.ev("t", ins)
                    tmk2 = (EV["t"].h, EV["t"].n)

                    @P.on("v")
                    def _(e, tmk2=tmk2, it_=it_, pyp=pyp):
                        e.raw.wait_ge(*tmk2)
                        e.stt(SYV, XS[:, it_, :], VEC2[:, 72 + it_:73 + it_],
                              pyp[:, 0:TCF], OP.mult, OP.add)
                        P.ev("v")
                    P.bfree(2, key="v")
                    vmk3 = (EV["v"].h, EV["v"].n)

                    @P.on("s")
                    def _(e, vmk3=vmk3, it_=it_):
                        e.raw.wait_ge(*vmk3)
                        P.ev("s", e.act(SZ, ZB[:, it_, :], AF.Silu))
                    smk3 = (EV["s"].h, EV["s"].n)

                    @P.on("v")
                    def _(e, smk3=smk3, it_=it_):
                        e.raw.wait_ge(*smk3)
                        e.tt(YBUF[:, it_, :], SYV, SZ, OP.mult)
                        P.ev("v")
                WOK = r3(bf(PC[:, 4096:5120]), 1024)
                ymk = (EV["v"].h, EV["v"].n)

                @P.on("t")
                def _(e):
                    for b in range(8):
                        P.bwait(e, b)
                for kk in range(IT):
                    if kk % 2 == 0:
                        tprev = (EV["t"].h, EV["t"].n)

                        @P.on("sy")
                        def _(e, tprev=tprev):
                            e.raw.wait_ge(*tprev)
                        for j in range(2):
                            ld(WOK[:, j, :],
                               Hn["outpn"][0:1024, (kk + j) * 128:(kk + j + 1) * 128],
                               transpose=True)
                        ldmk3 = ldm()

                        @P.on("t")
                        def _(e, ldmk3=ldmk3, ymk=ymk):
                            e.raw.wait_ge(*ldmk3)
                            if kk == 0:
                                e.raw.wait_ge(*ymk)

                    @P.on("t")
                    def _(e, kk=kk):
                        for dk in range(DT):
                            ins = e.mm(PS[dk][:, 0:TCF],
                                       WOK[:, kk % 2, dk * 128:(dk + 1) * 128],
                                       YBUF[:, kk, :], start=(kk == 0),
                                       stop=(kk == IT - 1))
                        if kk % 2 == 1:
                            P.ev("t", ins)
                tmk3 = (EV["t"].h, EV["t"].n)
                for dk in range(DT):
                    @P.on("v")
                    def _(e, tmk3=tmk3, dk=dk, tseg=tseg):
                        e.raw.wait_ge(*tmk3)
                        xs2 = X[:, dk, tseg * TCF:(tseg + 1) * TCF]
                        e.tt(xs2, xs2, PS[dk][:, 0:TCF], OP.add)
                        P.ev("v")
                    P.bfree(dk, key="v")
                prev_ymk = ymk
                brow_free = bc_marks
            P.barrier()

        sub_ffn0()
        if stages >= 2:
            sub_rwkv()
        if stages >= 3:
            sub_ffn0()
        if stages >= 4:
            sub_qffn(0)
        if stages >= 5:
            sub_mamba()
        if stages >= 6:
            sub_qffn(1)

        prev_store = [None]
        for tt in range(T // 128):
            for dk in range(DT):
                pe = PS[dk % 2]

                @P.on("t")
                def _(e, pe=pe, dk=dk, tt=tt):
                    P.wt(e, "v")
                    P.bwait(e, dk % 2)
                    P.ev("t", e.transpose(pe[:, 0:128],
                                          X[:, dk, tt * 128:(tt + 1) * 128], IDF[:]))
                tmk = EV["t"].n

                @P.on("s")
                def _(e, tmk=tmk, dk=dk, pe=pe):
                    e.raw.wait_ge(EV["t"].h, tmk)
                    if prev_store[0] is not None:
                        e.raw.wait_ge(*prev_store[0])
                    P.ev("s", e.scopy(PC[:, dk * 128:(dk + 1) * 128], pe[:, 0:128]))
                P.bfree(dk % 2)
            smk = EV["s"].n

            @P.on("g")
            def _(e, smk=smk, tt=tt):
                e.raw.wait_ge(EV["s"].h, smk)
                gdma(e, xout[tt * 128:(tt + 1) * 128, :], PC[:, 0:1024])
            prev_store[0] = gdm()

        @P.on("g")
        def _(e):
            for h_, v_ in DG.all_marks():
                if v_:
                    e.raw.wait_ge(h_, v_)

    st.close()
    names_in = ["x_in"] + list(wshapes.keys())
    return nc, names_in


def make_in_maps(inputs, T=2048, ncores=NCORES):
    x = np.ascontiguousarray(np.asarray(inputs["L0_attn_x"], dtype=np.float32))
    in_maps = []
    for c in range(ncores):
        m = {"x_in": np.ascontiguousarray(x[c][:T])}
        for k, v in inputs.items():
            if k == "L0_attn_x":
                continue
            v = np.asarray(v, dtype=np.float32)
            if k == "L0_attn_time_shift":
                v = v.reshape(1, D)
            elif k == "L1_attn_conv_w":
                v = v.reshape(DI, 4)
            elif v.ndim == 1:
                v = v.reshape(1, -1)
            m[k] = np.ascontiguousarray(v)
        in_maps.append(m)
    return in_maps


def kernel(**inputs):
    T = 2048
    nc, names_in = build_core_program(T=T)
    in_maps = make_in_maps(inputs, T=T)
    res = run_bass_kernel_spmd(nc, in_maps, core_ids=list(range(NCORES)))
    out = np.stack([np.asarray(r["x_out"]).reshape(T, D) for r in res.results])
    return out.astype(np.float32)
